# version 06
# baseline (speedup 1.0000x reference)
import contextlib
import numpy as np

import concourse.bass as bass
import concourse.mybir as mybir
from concourse.bass_utils import run_bass_kernel_spmd

F32 = mybir.dt.float32
BF16 = mybir.dt.bfloat16
I8 = mybir.dt.int8
I32 = mybir.dt.int32
AF = mybir.ActivationFunctionType
ALU = mybir.AluOpType
AX = mybir.AxisListType.X

NEGM = -1.0e9


def make_cfg(mini=False):
    if mini:
        c = dict(S=512, D=256, NH_CORE=1, INTER=512, DE=256, DS=384, E=8, TT=128)
    else:
        c = dict(S=4096, D=2048, NH_CORE=8, INTER=8192, DE=1408, DS=2816, E=8, TT=512)
    c["HARM"] = 64
    c["TOK"] = c["S"] // 2
    c["NF"] = c["D"] // 128
    c["NTT"] = c["S"] // c["TT"]
    c["NTTO"] = c["TOK"] // c["TT"]
    c["HD"] = c["NH_CORE"] * 128
    c["NM"] = c["TT"] // 128
    c["NIC"] = c["INTER"] // 128
    c["NDS"] = c["DS"] // 128
    c["NDE"] = c["DE"] // 128
    c["NHL1"] = c["D"] // 128
    c["NKL"] = c["S"] // 128
    c["TOKP"] = max(c["TT"], c["TOK"] // 4)
    c["PASSES"] = c["TOK"] // c["TOKP"]
    c["NTP"] = max(1, c["TOKP"] // c["TT"])
    return c


class Sem:
    __slots__ = ("h", "n", "name")

    def __init__(self, h, name):
        self.h, self.n, self.name = h, 0, name


class KB:
    QUEUES = ("t", "v", "a", "g", "y")

    def __init__(self, nc, stack):
        self.nc = nc
        self.q = {k: [] for k in self.QUEUES}
        self.sems = {}
        self._bar = {}
        self._stack = stack

    def sem(self, name):
        s = Sem(self._stack.enter_context(self.nc.semaphore(name)), name)
        self.sems[name] = s
        self._bar[name] = 0
        return s

    def E(self, qn, fn, *a, inc=None, iv=1, **k):
        if inc is not None:
            inc.n += iv
        import traceback
        site = traceback.extract_stack(limit=3)[0]

        def run(eng, fn=fn, a=a, k=k, inc=inc, iv=iv, site=site):
            try:
                ins = getattr(eng, fn)(*a, **k)
            except Exception:
                print(f"EMIT FAIL {fn} at {site.filename}:{site.lineno}")
                for x in a:
                    if hasattr(x, 'shape'):
                        print("   arg", x.shape, getattr(x, 'dtype', None))
                raise
            if inc is not None:
                ins.then_inc(inc.h, iv)

        self.q[qn].append(run)
        return inc.n if inc is not None else None

    def W(self, qn, sem, val):
        if val and val > 0:
            self.q[qn].append(lambda eng, s=sem.h, v=val: eng.wait_ge(s, v))

    def barrier(self, exclude=()):
        snap = [(s, s.n) for nm, s in self.sems.items()
                if s.n > self._bar[nm] and nm not in exclude]
        for nm, s in self.sems.items():
            if nm not in exclude:
                self._bar[nm] = s.n
        for qn in self.QUEUES:
            for s, v in snap:
                self.W(qn, s, v)


class Arena:
    def __init__(self, nc, stack, total):
        self.t = stack.enter_context(nc.sbuf_tensor("arena", [128, total], BF16))
        self.total = total
        self.lo = 0
        self.marks = []
        self.peak = 0

    def alloc(self, n_free, dtype=BF16, parts=128):
        need = n_free * (2 if dtype == F32 else 1)
        off = self.lo
        self.lo += (need + 63) & ~63
        self.peak = max(self.peak, self.lo)
        assert self.lo <= self.total, f"arena overflow {self.lo}>{self.total}"
        ap = self.t[0:parts, off:off + need]
        return ap.bitcast(F32) if dtype == F32 else ap

    def push(self):
        self.marks.append(self.lo)

    def pop(self):
        self.lo = self.marks.pop()


class Stream:

    def __init__(self, kb, sems, qn):
        self.kb, self.sems, self.qn = kb, sems, qn
        self.i = 0
        self.vals = []
        self.free = [None, None]
        self.lastv = [0, 0]

    def load(self, emit):
        sl = self.i % 2
        if self.free[sl] is not None:
            s, v = self.free[sl]
            self.kb.W(self.qn, s, v)
            self.free[sl] = None
        if self.lastv[sl]:
            self.kb.W(self.qn, self.sems[sl], self.lastv[sl])
        emit(self.sems[sl])
        self.vals.append(self.sems[sl].n)
        self.lastv[sl] = self.sems[sl].n
        self.i += 1
        return self.i - 1

    def need(self, qn, i):
        self.kb.W(qn, self.sems[i % 2], self.vals[i])

    def done(self, i, sem, val):
        self.free[i % 2] = (sem, val)


def build_nc(cfg):
    S, D, TT = cfg["S"], cfg["D"], cfg["TT"]
    NF, NTT, NTTO = cfg["NF"], cfg["NTT"], cfg["NTTO"]
    NH, HD, NM = cfg["NH_CORE"], cfg["HD"], cfg["NM"]
    INTER, NIC = cfg["INTER"], cfg["NIC"]
    DS, NDS, DE, NDE, E = cfg["DS"], cfg["NDS"], cfg["DE"], cfg["NDE"], cfg["E"]
    NHL1, NKL = cfg["NHL1"], cfg["NKL"]
    TOK, TOKP, NTP = cfg["TOK"], cfg["TOKP"], cfg["NTP"]
    PASSES = cfg["PASSES"]
    HARM = cfg["HARM"]
    ISCALE = float(1.0 / np.float32(np.sqrt(np.float32(128.0))))
    EPS = 1e-5

    nc = bass.Bass("TRN2", target_bir_lowering=False, debug=False)

    def din(name, shape, dt=F32):
        return nc.dram_tensor(name, shape, dt, kind="ExternalInput").ap()

    GW0 = min(256, DE, DS)
    ds_groups = []
    _o = 0
    while _o < DS:
        _w = min(GW0, DS - _o)
        ds_groups.append((_o, _w))
        _o += _w
    de_groups = []
    _o = 0
    while _o < DE:
        _w = min(GW0, DE - _o)
        de_groups.append((_o, _w))
        _o += _w
    xT = din("xT", [D, S])
    xoT = din("xoT", [D, TOK])
    wq_t = din("wq_t", [NH * 128, NF * 128])
    wk_t = din("wk_t", [128, NF * HD])
    wv_t = din("wv_t", [128, NF * HD])
    wo_t = din("wo_t", [NF * 128, NF * 128])
    fg8t = din("fg8t", [(INTER // 128) * 128, NF * 128], I8)
    fu8t = din("fu8t", [(INTER // 128) * 128, NF * 128], I8)
    fd8t = din("fd8t", [NF * 128, NIC * 128], I8)
    sgT = din("sgT", [128, NIC])
    suT = din("suT", [128, NIC])
    sdT = din("sdT", [128, NF])
    bqT = din("bqT", [D, HARM])
    bkT = din("bkT", [D, HARM])
    bvT = din("bvT", [D, HARM])
    boT = din("boT", [D, HARM])
    phT = din("phT", [4 * HARM, D])
    amT = din("amT", [4 * HARM, D])
    wr = din("wr", [D, E])
    weg_t = din("weg_t", [E * 128, NF * DE])
    weu_t = din("weu_t", [E * 128, NF * DE])
    wed_t = din("wed_t", [E * 128, NF * DE])
    wsg_t = din("wsg_t", [128, NF * DS])
    wsu_t = din("wsu_t", [128, NF * DS])
    wsd_t = din("wsd_t", [128, NF * DS])
    idn = din("idn", [128, 128])
    cmask = din("cmask", [NM * 128, TT])
    rolep = din("rolep", [1, 2], I32)

    yout = nc.dram_tensor("yout", [D, TOK], F32, kind="ExternalOutput").ap()

    def scr(name, shape, dt=BF16):
        return nc.dram_tensor(name, shape, dt, kind="Internal").ap()

    q_scr = scr("q_scr", [HD, S])
    k_scr = scr("k_scr", [HD, S])
    v_scr = scr("v_scr", [NH * S, 128])
    xn0_scr = scr("xn0_scr", [D, S])
    xn0_own = scr("xn0_own", [D, TOK])
    ctx_own = scr("ctx_own", [2 * HD, TOK])
    cc0_in = scr("cc0_in", [HD, S])
    cc0_out = scr("cc0_out", [2 * HD, S])
    res_scr = scr("res_scr", [D, TOK], F32)
    x1_scr = scr("x1_scr", [D, TOK], F32)
    xn1_scr = scr("xn1_scr", [D, TOK])
    cc1_in = scr("cc1_in", [TOK, 2 * HARM])
    cc1_out = scr("cc1_out", [S, 2 * HARM])
    comb_scr = scr("comb_scr", [E, TOK], F32)
    weg_b = scr("weg_b", [E * 128, NF * DE])
    weu_b = scr("weu_b", [E * 128, NF * DE])
    wed_b = scr("wed_b", [E * 128, NF * DE])
    wsg_b = scr("wsg_b", [128, NF * DS])
    wsu_b = scr("wsu_b", [128, NF * DS])
    wsd_b = scr("wsd_b", [128, NF * DS])

    stack = contextlib.ExitStack()
    with stack:
        kb = KB(nc, stack)

        mm = kb.sem("mm")
        dv = kb.sem("dv")
        ac = kb.sem("ac")
        ccs = kb.sem("ccs")
        gS = kb.sem("gS")
        yS = kb.sem("yS")
        rS = kb.sem("rS")
        S_wA = Stream(kb, [kb.sem("wA0"), kb.sem("wA1")], "g")
        S_wB = Stream(kb, [kb.sem("wB0"), kb.sem("wB1")], "g")
        S_wC = Stream(kb, [kb.sem("wC0"), kb.sem("wC1")], "g")
        S_xg = Stream(kb, [kb.sem("xg0"), kb.sem("xg1")], "g")
        S_xr = Stream(kb, [kb.sem("xr0"), kb.sem("xr1")], "g")
        S_kb = Stream(kb, [kb.sem("kb0"), kb.sem("kb1")], "y")
        S_vb = Stream(kb, [kb.sem("vb0"), kb.sem("vb1")], "y")
        S_qb = Stream(kb, [kb.sem("qb0"), kb.sem("qb1")], "y")
        S_sL = Stream(kb, [kb.sem("sL0"), kb.sem("sL1")], "y")
        S_yA = Stream(kb, [kb.sem("yA0"), kb.sem("yA1")], "y")
        S_yB = Stream(kb, [kb.sem("yB0"), kb.sem("yB1")], "y")
        S_yC = Stream(kb, [kb.sem("yC0"), kb.sem("yC1")], "y")
        S_wC2 = Stream(kb, [kb.sem("wc20"), kb.sem("wc21")], "y")
        stg = [kb.sem(f"st{i}") for i in range(4)]
        stg_free = [0, 0, 0, 0]
        stg_state = {"i": 0}
        ys = kb.sem("ys")

        AR = 76000
        arena = Arena(nc, stack, AR)

        sb = lambda n, shp, dt: stack.enter_context(nc.sbuf_tensor(n, shp, dt))
        id_bf = sb("id_bf", [128, 128], BF16)
        id_f32 = sb("id_f32", [128, 128], F32)
        ones_c_bf = sb("ones_c_bf", [128, 1], BF16)
        ones_c_f = sb("ones_c_f", [128, 1], F32)
        ones_r_f = sb("ones_r_f", [1, 128], F32)
        sg_sb = sb("sg_sb", [128, NIC], F32)
        su_sb = sb("su_sb", [128, NIC], F32)
        sd_sb = sb("sd_sb", [128, NF], F32)
        wr_sb = sb("wr_sb", [128, NF * E], F32)
        ncs_wr = sb("ncs_wr", [1, E], F32)
        role_sb = sb("role_sb", [1, 2], I32)
        wTvo_sb = sb("wTvo_sb", [HARM, 2 * D], BF16)
        b4_sb = sb("b4_sb", [128, NF * 2 * HARM], BF16)
        bkv_sb = sb("bkv_sb", [128, NF * 2 * HARM], BF16)
        resq_sb = sb("resq_sb", [HARM, TOK], F32)
        rka_sb = sb("rka_sb", [HARM + 1, S], BF16)
        nh_sb = sb("nh_sb", [HARM, NHL1 * HARM], F32)
        RVW = HARM + 8
        rva_sb = sb("rva_sb", [128, NKL * RVW], BF16)
        rows_t = sb("rows_t", [1, 8 * TT + 16], F32)
        cns = sb("cns", [128, 2], F32)
        b4v = b4_sb.rearrange("p (f n h) -> p f n h", f=NF, n=2)

        def row(i):
            return rows_t[:, i * TT:(i + 1) * TT]

        bstat = rows_t[:, 8 * TT:8 * TT + 16]

        psum = stack.enter_context(nc.psum_tensor("psum", [128, 4096], F32))

        def PS(bank, parts=128, width=512):
            return psum[0:parts, bank * 512:bank * 512 + width]

        def PSW(bank, p0, p1, width, off=0):
            return psum[p0:p1, bank * 512 + off:bank * 512 + off + width]

        pg = {}

        def bankw(qn, bank):
            if bank in pg:
                s, v = pg[bank]
                kb.W(qn, s, v)

        def bankg(bank, sem):
            pg[bank] = (sem, sem.n)

        block = stack.enter_context(nc.Block())

        def mml(ps_ap, pairs, inc=None):
            n = len(pairs)
            for i, (l, r) in enumerate(pairs):
                kb.E("t", "matmul", ps_ap, l, r, start=(i == 0), stop=(i == n - 1),
                     inc=(inc if i == n - 1 else None))

        def dma(qn, sem, out, in_):
            return kb.E(qn, "dma_start", inc=sem, iv=16, out=out, in_=in_)

        groles = {}

        def gdma_role(sem, out, h, static_off, pattern):
            sem.n += 16

            def run(eng, out=out, h=h, so=static_off, pat=pattern, s=sem):
                eng.reg_add(groles["tmp"], groles["roff"], so)
                try:
                    eng.dma_start(out=out, in_=bass.AP(h, groles["tmp"], pat)
                                  ).then_inc(s.h, 16)
                except Exception:
                    print("GDMA FAIL", h.name, so, pat, out.shape)
                    raise

            kb.q["g"].append(run)

        def stg_store(wq, out_ap, src_ap, producer_sem, producer_val):
            ss = stg_state["i"]
            kb.W("y", producer_sem, producer_val)
            kb.W("y", stg[ss], stg_free[ss])
            dma("y", stg[ss], out_ap, src_ap)
            stg_free[ss] = stg[ss].n
            stg_state["i"] = (ss + 1) % 4
            return (stg[ss], stg[ss].n)

        dma("y", rS, role_sb[:, :], rolep)

        def g_pre(eng):
            eng.wait_ge(rS.h, 16)
            groles["roff"] = eng.alloc_register("roff")
            groles["tmp"] = eng.alloc_register("rtmp")
            eng.reg_load(groles["roff"], role_sb[0:1, 0:1])

        kb.q["g"].append(g_pre)

        dma("g", gS, id_bf[:, :], idn)
        dma("y", yS, id_f32[:, :], idn)
        dma("y", yS, sg_sb[:, :], sgT)
        dma("y", yS, su_sb[:, :], suT)
        dma("y", yS, sd_sb[:, :], sdT)
        dma("y", yS, wr_sb.rearrange("p (f e) -> p f e", f=NF),
            wr.rearrange("(f p) e -> p f e", p=128))
        for i, b in enumerate((bqT, boT)):
            dma("g", gS, b4v[:, :, i, :], b.rearrange("(f p) h -> p f h", p=128))
        bkvv = bkv_sb.rearrange("p (f n h) -> p f n h", f=NF, n=2)
        dma("g", gS, bkvv[:, :, 0, :], bkT.rearrange("(f p) h -> p f h", p=128))
        dma("g", gS, bkvv[:, :, 1, :], bvT.rearrange("(f p) h -> p f h", p=128))

        kb.E("v", "memset", ones_c_bf[:, :], 1.0, inc=dv)
        kb.E("v", "memset", ones_c_f[:, :], 1.0, inc=dv)
        kb.E("v", "memset", ones_r_f[:, :], 1.0, inc=dv)
        kb.E("v", "memset", cns[:, 0:1], float(np.pi / 2), inc=dv)
        kb.E("v", "memset", cns[:, 1:2], EPS, inc=dv)
        ones_dv = dv.n

        arena.push()
        ph_sb = arena.alloc(4 * D, F32, parts=HARM)
        am_sb = arena.alloc(4 * D, F32, parts=HARM)
        cosb = arena.alloc(4 * D, F32, parts=HARM)
        wTt = arena.alloc(4 * D, BF16, parts=HARM)
        wqtr = arena.alloc(NHL1 * HARM)
        wktr = arena.alloc(NHL1 * HARM)
        for i in range(4):
            dma("y", yS, ph_sb[:, i * D:(i + 1) * D],
                phT[i * HARM:(i + 1) * HARM, :])
            dma("y", yS, am_sb[:, i * D:(i + 1) * D],
                amT[i * HARM:(i + 1) * HARM, :])
        kb.W("a", yS, yS.n)
        kb.W("a", dv, ones_dv)
        kb.E("a", "activation", cosb, ph_sb, AF.Sin, bias=cns[0:HARM, 0:1],
             scale=1.0, inc=ac)
        kb.W("v", ac, ac.n)
        kb.E("v", "tensor_mul", wTt, am_sb, cosb, inc=dv)
        kb.W("v", dv, dv.n)
        kb.E("v", "tensor_copy", wTvo_sb[:, :], wTt[:, 2 * D:4 * D], inc=dv)
        wt_dv = dv.n

        kb.W("t", dv, wt_dv)
        kb.W("t", gS, gS.n)
        for i, dstb in ((0, wqtr), (1, wktr)):
            for h in range(NHL1):
                tb = (i * NHL1 + h) % 2
                bankw("t", tb)
                kb.E("t", "transpose",
                     PS(tb).bitcast(BF16)[0:128, 0:HARM],
                     wTt[:, i * D + h * 128:i * D + (h + 1) * 128],
                     id_bf[0:HARM, 0:HARM], inc=mm)
                kb.W("v", mm, mm.n)
                kb.W("v", dv, dv.n)
                kb.E("v", "tensor_copy", dstb[:, h * HARM:(h + 1) * HARM],
                     PS(tb).bitcast(BF16)[0:128, 0:HARM], inc=dv)
                bankg(tb, dv)
        wtr_dv = dv.n
        kb.W("t", dv, wtr_dv)
        for h in range(NHL1):
            tb = h % 2
            bankw("t", tb)
            kb.E("t", "matmul", PSW(tb, 0, HARM, HARM),
                 wqtr[:, h * HARM:(h + 1) * HARM],
                 wktr[:, h * HARM:(h + 1) * HARM],
                 start=True, stop=True, inc=mm)
            kb.W("v", mm, mm.n)
            kb.W("v", dv, dv.n)
            kb.E("v", "tensor_copy", nh_sb[:, h * HARM:(h + 1) * HARM],
                 PSW(tb, 0, HARM, HARM), inc=dv)
            bankg(tb, dv)
        nh0_dv = dv.n

        kb.W("t", yS, yS.n)
        kb.W("t", dv, dv.n)
        mml(PS(7, 1, E), [(ones_c_f[:, :], wr_sb[:, f * E:(f + 1) * E])
                          for f in range(NF)], inc=mm)
        kb.W("a", mm, mm.n)
        kb.E("a", "activation", ncs_wr[:, :], PS(7, 1, E), AF.Copy, bias=0.0,
             scale=-1.0, inc=ac)
        arena.pop()
        kb.barrier()

        arena.push()
        x_bf = [arena.alloc(NF * TT) for _ in range(2)]
        xn_bf = [arena.alloc(NF * TT) for _ in range(2)]
        wkv_sb = arena.alloc(2 * NF * HD)
        wq_sl = [arena.alloc(NF * 128) for _ in range(2)]
        sq_bf = [arena.alloc(TT) for _ in range(2)]
        mu_b = arena.alloc(TT, F32)
        r_b = arena.alloc(TT, F32)
        tmp_f = arena.alloc(TT, F32)
        stg_sb = [arena.alloc(TT) for _ in range(4)]

        wk_of, wv_of = 0, NF * HD
        dma("g", gS, wkv_sb[:, wk_of:wk_of + NF * HD], wk_t)
        dma("g", gS, wkv_sb[:, wv_of:wv_of + NF * HD], wv_t)
        g_wkv = gS.n

        def ld_x(tt):
            def em(sem):
                dma("g", sem, x_bf[S_xg.i % 2].rearrange("p (f t) -> p f t", f=NF),
                    xT[:, tt * TT:(tt + 1) * TT].rearrange("(f p) t -> p f t", p=128))
            return S_xg.load(em)

        def ld_wq(oc):
            def em(sem):
                dma("g", sem, wq_sl[S_wA.i % 2],
                    wq_t[oc * 128:(oc + 1) * 128, :])
            return S_wA.load(em)

        x_i0 = ld_x(0)
        wq_i0 = ld_wq(0)
        sq_guard = [0, 0]
        xn_guard = [[], []]

        for tt in range(NTT):
            sl = tt % 2
            xi = x_i0 + tt
            if tt + 1 < NTT:
                ld_x(tt + 1)

            S_xg.need("t", xi)
            kb.W("t", dv, ones_dv)
            bankw("t", 0)
            bankw("t", 1)
            mml(PS(0, 1, TT), [(ones_c_bf[:, :], x_bf[sl][:, f * TT:(f + 1) * TT])
                               for f in range(NF)], inc=mm)
            S_xg.need("a", xi)
            for f in range(NF):
                ssl = f % 2
                kb.W("a", mm, sq_guard[ssl])
                kb.E("a", "activation", sq_bf[ssl],
                     x_bf[sl][:, f * TT:(f + 1) * TT], AF.Square,
                     bias=0.0, scale=1.0, inc=ac)
                kb.W("t", ac, ac.n)
                kb.E("t", "matmul", PS(1, 1, TT), ones_c_bf[:, :], sq_bf[ssl],
                     start=(f == 0), stop=(f == NF - 1), inc=mm)
                sq_guard[ssl] = mm.n
            sums_mm = mm.n

            kb.W("a", mm, sums_mm)
            kb.E("a", "activation", row(0), PS(0, 1, TT), AF.Copy, bias=0.0,
                 scale=1.0 / D, inc=ac)
            kb.E("a", "activation", row(1), PS(1, 1, TT), AF.Copy, bias=0.0,
                 scale=1.0 / D, inc=ac)
            bankg(0, ac)
            bankg(1, ac)
            kb.W("a", ac, ac.n)
            kb.E("a", "activation", row(2), row(0), AF.Square, bias=0.0, scale=1.0,
                 inc=ac)
            kb.W("v", ac, ac.n)
            kb.E("v", "tensor_sub", row(3), row(1), row(2), inc=dv)
            kb.W("a", dv, dv.n)
            kb.E("a", "activation", row(4), row(3), AF.Ln, bias=cns[0:1, 1:2],
                 scale=1.0, inc=ac)
            kb.W("a", ac, ac.n)
            kb.E("a", "activation", row(5), row(4), AF.Exp, bias=0.0, scale=-0.5,
                 inc=ac)
            r_ac = ac.n

            kb.W("t", ac, r_ac)
            bankw("t", 2)
            kb.E("t", "matmul", PS(2, 128, TT), ones_r_f[:, :], row(0),
                 start=True, stop=True, inc=mm)
            bankw("t", 3)
            kb.E("t", "matmul", PS(3, 128, TT), ones_r_f[:, :], row(5),
                 start=True, stop=True, inc=mm)
            kb.W("v", mm, mm.n)
            kb.W("v", dv, dv.n)
            kb.E("v", "tensor_copy", mu_b, PS(2, 128, TT), inc=dv)
            kb.W("v", dv, dv.n)
            kb.E("v", "tensor_copy", r_b, PS(3, 128, TT), inc=dv)
            bankg(2, dv)
            bankg(3, dv)
            bc_dv = dv.n

            for s_, v_ in xn_guard[sl]:
                kb.W("v", s_, v_)
            xn_guard[sl] = []
            kb.W("v", dv, bc_dv)
            for f in range(NF):
                kb.W("v", dv, dv.n)
                kb.E("v", "tensor_sub", tmp_f, x_bf[sl][:, f * TT:(f + 1) * TT],
                     mu_b, inc=dv)
                kb.W("v", dv, dv.n)
                kb.E("v", "tensor_mul", xn_bf[sl][:, f * TT:(f + 1) * TT],
                     tmp_f, r_b, inc=dv)
            xn_dv = dv.n

            st_s, st_v = stg_store(
                "y",
                xn0_scr[:, tt * TT:(tt + 1) * TT].rearrange("(f p) t -> p f t",
                                                            p=128),
                xn_bf[sl].rearrange("p (f t) -> p f t", f=NF), dv, xn_dv)
            xn_guard[sl].append((st_s, st_v))

            xn_ap = lambda f, s=sl: xn_bf[s][:, f * TT:(f + 1) * TT]

            kb.W("t", dv, xn_dv)
            kb.W("t", gS, g_wkv)
            for mat in range(2):
                for oc in range(NH):
                    bank = 4 + (oc % 2)
                    if mat == 1:
                        qi = wq_i0 + tt * NH + oc
                        S_wA.need("t", qi)
                        if qi + 1 < wq_i0 + NTT * NH:
                            ld_wq((tt * NH + oc + 1) % NH)
                    bankw("t", bank)
                    if mat == 0:
                        pairs = [(wkv_sb[:, wk_of + f * HD + oc * 128:
                                         wk_of + f * HD + oc * 128 + 128],
                                  xn_ap(f)) for f in range(NF)]
                    else:
                        wsl = (tt * NH + oc) % 2
                        pairs = [(wq_sl[wsl][:, f * 128:(f + 1) * 128],
                                  xn_ap(f)) for f in range(NF)]
                    mml(PS(bank, 128, TT), pairs, inc=mm)
                    if mat == 1:
                        S_wA.done(qi, mm, mm.n)
                    kb.W("v", mm, mm.n)
                    ss = stg_state["i"]
                    kb.W("v", stg[ss], stg_free[ss])
                    kb.W("v", dv, dv.n)
                    kb.E("v", "tensor_copy", stg_sb[ss], PS(bank, 128, TT), inc=dv)
                    bankg(bank, dv)
                    dst = k_scr if mat == 0 else q_scr
                    stg_store("y",
                              dst[oc * 128:(oc + 1) * 128, tt * TT:(tt + 1) * TT],
                              stg_sb[ss], dv, dv.n)

            VG = min(512, HD)
            NVG = HD // VG
            for sub in range(TT // 128):
                for vg in range(NVG):
                    bank = 4 + ((sub * NVG + vg) % 2)
                    bankw("t", bank)
                    mml(PS(bank, 128, VG),
                        [(xn_ap(f)[:, sub * 128:(sub + 1) * 128],
                          wkv_sb[:, wv_of + f * HD + vg * VG:
                                 wv_of + f * HD + (vg + 1) * VG])
                         for f in range(NF)], inc=mm)
                    kb.W("v", mm, mm.n)
                    ss = stg_state["i"]
                    kb.W("v", stg[ss], stg_free[ss])
                    kb.W("v", dv, dv.n)
                    kb.E("v", "tensor_copy", stg_sb[ss][:, 0:VG],
                         PS(bank, 128, VG), inc=dv)
                    bankg(bank, dv)
                    t0 = tt * TT + sub * 128
                    stg_store("y",
                              v_scr.rearrange("(h t) d -> t h d", h=NH)[
                                  t0:t0 + 128,
                                  vg * (VG // 128):(vg + 1) * (VG // 128), :],
                              stg_sb[ss][:, 0:VG].rearrange(
                                  "p (h d) -> p h d", h=VG // 128), dv, dv.n)

            S_xg.done(xi, mm, mm.n)
            xn_guard[sl].append((mm, mm.n))

        arena.pop()
        kb.barrier()

        gdma_role(gS, xn0_own, xn0_scr.tensor, 0, [[S, D], [1, TOK]])
        precast_jobs = [(wsg_b, wsg_t), (wsu_b, wsu_t), (wsd_b, wsd_t)]
        for e_ in range(E):
            precast_jobs.append((weg_b[e_ * 128:(e_ + 1) * 128, :],
                                 weg_t[e_ * 128:(e_ + 1) * 128, :]))
            precast_jobs.append((weu_b[e_ * 128:(e_ + 1) * 128, :],
                                 weu_t[e_ * 128:(e_ + 1) * 128, :]))
            precast_jobs.append((wed_b[e_ * 128:(e_ + 1) * 128, :],
                                 wed_t[e_ * 128:(e_ + 1) * 128, :]))
        precast_jobs.reverse()

        def precast(n):
            for _ in range(n):
                if precast_jobs:
                    d_, s_ = precast_jobs.pop()
                    dma("g", gS, d_, s_)
        arena.push()
        cm_sb = arena.alloc(NM * TT, F32)
        kbuf = [arena.alloc(S) for _ in range(2)]
        vbuf = [arena.alloc(NKL * 128) for _ in range(2)]
        qbuf = [arena.alloc(TT) for _ in range(2)]
        pbuf = arena.alloc(NKL * TT)
        rb_sb = arena.alloc(TT, F32)
        ctx_st = [arena.alloc(TT) for _ in range(2)]

        dma("y", yS, cm_sb.rearrange("p (m t) -> p m t", m=NM),
            cmask.rearrange("(m p) t -> p m t", p=128))
        cm_v = yS.n

        def ld_kv(h):
            def emk(sem):
                dma("y", sem, kbuf[S_kb.i % 2], k_scr[h * 128:(h + 1) * 128, :])

            def emv(sem):
                dma("y", sem, vbuf[S_vb.i % 2].rearrange("p (kc d) -> p kc d",
                                                         kc=NKL),
                    v_scr.rearrange("(h t) d -> h t d", h=NH)[h, :, :].rearrange(
                        "(kc p) d -> p kc d", p=128))

            return S_kb.load(emk), S_vb.load(emv)

        def ld_q(h, qt):
            def em(sem):
                dma("y", sem, qbuf[S_qb.i % 2],
                    q_scr[h * 128:(h + 1) * 128, qt * TT:(qt + 1) * TT])

            return S_qb.load(em)

        ld_kv(0)
        ld_q(0, 0)
        exp_guard = {0: 0, 2: 0}
        p_pe_guard = 0
        ctx_guard = [None, None]

        def emit_scores(hsl, qsl, qt, g, nkc):
            k0 = 2 * g
            kcs = [k0] if k0 == nkc - 1 else [k0, k0 + 1]
            pb = (g % 2) * 2
            kb.W("t", ac, exp_guard[pb])
            s_mm = []
            for j, kc in enumerate(kcs):
                kb.E("t", "matmul", PS(pb + j, 128, TT),
                     kbuf[hsl][:, kc * 128:(kc + 1) * 128],
                     qbuf[qsl], start=True, stop=True, inc=mm)
                s_mm.append((j, kc, mm.n))
            need_dv = False
            for j, kc, v in s_mm:
                if kc >= qt * NM:
                    tgt = PS(pb + j, 128, TT)
                    kb.W("v", mm, v)
                    kb.W("v", dv, dv.n)
                    kb.E("v", "tensor_add", tgt, tgt,
                         cm_sb[:, (kc - qt * NM) * TT:
                               (kc - qt * NM + 1) * TT], inc=dv)
                    need_dv = True
            kb.W("a", mm, mm.n)
            if need_dv:
                kb.W("a", dv, dv.n)
            kb.W("a", mm, p_pe_guard)
            nk_ = len(kcs)
            src_ = psum[0:128, pb * 512:(pb + 2) * 512].rearrange(
                "p (g r) -> p g r", g=2)[:, 0:nk_, 0:TT]
            dst_ = pbuf[:, k0 * TT:(k0 + nk_) * TT].rearrange(
                "p (g t) -> p g t", g=nk_)
            kb.E("a", "activation", dst_, src_, AF.Exp, bias=0.0, scale=ISCALE,
                 inc=ac)
            exp_guard[pb] = ac.n
            return kcs

        pending2 = [None]

        def tail2(h, qt, cb, csl, av_mm):
            kb.W("v", mm, av_mm)
            kb.W("v", mm, prev2["rb_mm"])
            kb.W("v", dv, dv.n)
            kb.E("v", "reciprocal", row(6), PS(4, 1, TT), inc=dv)
            prev2["sums_rd"] = dv.n
            kb.W("t", dv, dv.n)
            kb.W("t", dv, prev2["rb_dv"])
            bankw("t", 6)
            kb.E("t", "matmul", PS(6, 128, TT), ones_r_f[:, :], row(6),
                 start=True, stop=True, inc=mm)
            prev2["rb_mm"] = mm.n
            kb.W("v", mm, mm.n)
            kb.W("v", dv, dv.n)
            kb.E("v", "tensor_copy", rb_sb, PS(6, 128, TT), inc=dv)
            bankg(6, dv)
            prev2["rb_dv"] = dv.n
            if ctx_guard[csl] is not None:
                kb.W("v", ctx_guard[csl][0], ctx_guard[csl][1])
            kb.W("v", dv, dv.n)
            kb.E("v", "tensor_mul", ctx_st[csl], PS(cb, 128, TT), rb_sb, inc=dv)
            bankg(cb, dv)
            ctx_guard[csl] = stg_store(
                "y", cc0_in[h * 128:(h + 1) * 128, qt * TT:(qt + 1) * TT],
                ctx_st[csl], dv, dv.n)

        prev2 = {"rb_mm": 0, "rb_dv": 0, "sums_rd": 0}
        for h in range(NH):
            hsl = h % 2
            if h + 1 < NH:
                ld_kv(h + 1)
            precast(-(-len(precast_jobs) // max(1, NH - h)))
            for qt in range(NTT):
                qi = h * NTT + qt
                cb = 5 if qi % 2 == 0 else 7
                if qi + 1 < NH * NTT:
                    nh_, nqt = divmod(qi + 1, NTT)
                    ld_q(nh_, nqt)
                nkc = (qt + 1) * NM
                S_kb.need("t", h)
                S_vb.need("t", h)
                S_qb.need("t", qi)
                kb.W("t", yS, cm_v)
                ng = (nkc + 1) // 2
                emit_scores(hsl, qi % 2, qt, 0, nkc)
                if pending2[0] is not None:
                    pending2[0]()
                    pending2[0] = None
                first_av = True
                for g in range(ng):
                    if g + 1 < ng:
                        emit_scores(hsl, qi % 2, qt, g + 1, nkc)
                    kcs = [2 * g] if 2 * g == nkc - 1 else [2 * g, 2 * g + 1]
                    kb.W("t", ac, exp_guard[(g % 2) * 2])
                    if first_av:
                        kb.W("t", dv, prev2["sums_rd"])
                        bankw("t", cb)
                        first_av = False
                    for kc in kcs:
                        kb.E("t", "matmul", PS(4, 1, TT), ones_c_bf[:, :],
                             pbuf[:, kc * TT:(kc + 1) * TT],
                             start=(kc == 0), stop=(kc == nkc - 1),
                             inc=(mm if kc == nkc - 1 else None))
                        kb.E("t", "matmul", PS(cb, 128, TT),
                             vbuf[hsl][:, kc * 128:(kc + 1) * 128],
                             pbuf[:, kc * TT:(kc + 1) * TT],
                             start=(kc == 0), stop=(kc == nkc - 1),
                             inc=(mm if kc == nkc - 1 else None))
                av_mm = mm.n
                p_pe_guard = av_mm
                pending2[0] = (lambda h_=h, qt_=qt, cb_=cb, csl_=qi % 2,
                               av_=av_mm: tail2(h_, qt_, cb_, csl_, av_))
                S_qb.done(qi, mm, av_mm)
            S_kb.done(h, mm, mm.n)
            S_vb.done(h, mm, mm.n)
        if pending2[0] is not None:
            pending2[0]()
            pending2[0] = None

        for s in stg:
            kb.W("g", s, s.n)

        def fire_ag1(eng):
            eng.collective_compute(
                "AllGather", ALU.bypass,
                replica_groups=[[0, 1], [2, 3], [4, 5], [6, 7]],
                ins=[cc0_in.opt()], outs=[cc0_out.opt()],
            ).then_inc(ccs.h, 1)

        ccs.n += 1
        kb.q["g"].append(fire_ag1)
        arena.pop()
        kb.barrier(exclude=("ccs",))

        arena.push()
        xn_t = [arena.alloc(NF * TT) for _ in range(2)]
        OG = min(128, INTER)
        NOG = INTER // OG
        wg_sl = [arena.alloc(NF * OG) for _ in range(2)]
        wu_sl = [arena.alloc(NF * OG) for _ in range(2)]
        NICH = NIC // 2
        wd_sl = [arena.alloc(NICH * 128) for _ in range(2)]
        fuse = arena.alloc(NIC * TT)
        silu_st = [arena.alloc(TT, F32) for _ in range(2)]
        xo_t = [arena.alloc(TT, F32) for _ in range(2)]
        rst_sb = [arena.alloc(TT, F32) for _ in range(2)]
        rst_guard = [None, None]

        def ld_xn(tt):
            def em(sem):
                gdma_role(sem, xn_t[S_xg.i % 2].rearrange("p (f t) -> p f t", f=NF),
                          xn0_scr.tensor, tt * TT,
                          [[S, 128], [S * 128, NF], [1, TT]])
            return S_xg.load(em)

        def ld_wgu(og):
            def emg(sem):
                dma("g", sem, wg_sl[S_wA.i % 2],
                    fg8t[og * 128:(og + 1) * 128, :])

            def emu(sem):
                dma("g", sem, wu_sl[S_wB.i % 2],
                    fu8t[og * 128:(og + 1) * 128, :])

            return S_wA.load(emg), S_wB.load(emu)

        def ld_wd(fo, half):
            def em(sem):
                dma("g", sem, wd_sl[S_wC.i % 2],
                    fd8t[fo * 128:(fo + 1) * 128,
                         half * NICH * 128:(half + 1) * NICH * 128])
            return S_wC.load(em)

        def ld_xo(tt, fo):
            def em(sem):
                dma("g", sem, xo_t[S_xr.i % 2],
                    xoT[fo * 128:(fo + 1) * 128, tt * TT:(tt + 1) * TT])
            return S_xr.load(em)

        xn_i0 = ld_xn(0)
        wgu_idx = [ld_wgu(0)]
        wd_i0 = ld_wd(0, 0)
        ld_wd(0, 1)
        xo_i0 = ld_xo(0, 0)
        ctx_ex = {}
        silu_g = {4 + 0: 0, 4 + 1: 0, 0: 0, 1: 0}
        fuse_g = {2: 0, 3: 0, "s": 0}
        fuse_pe = 0

        for tt in range(NTTO):
            if tt == min(1, NTTO - 1) and "v" not in ctx_ex:
                kb.W("g", ccs, 1)
                gdma_role(gS, ctx_own, cc0_out.tensor, 0,
                          [[S, 2 * HD], [1, TOK]])
                ctx_ex["v"] = gS.n
            precast(4)
            xi = xn_i0 + tt
            if tt + 1 < NTTO:
                ld_xn(tt + 1)
            S_xg.need("t", xi)
            xnf = lambda f, s=tt % 2: xn_t[s][:, f * TT:(f + 1) * TT]

            for og in range(NOG):
                gidx0 = tt * NOG + og
                gi, gbi = wgu_idx[gidx0]
                S_wA.need("t", gi)
                S_wB.need("t", gbi)
                if gidx0 + 1 < NTTO * NOG:
                    wgu_idx.append(ld_wgu((gidx0 + 1) % NOG))
                for oc in range(OG // 128):
                    gc = og * (OG // 128) + oc
                    gb = oc % 2
                    ub = 2 + oc % 2
                    kb.W("t", ac, silu_g[gb])
                    kb.W("t", dv, fuse_g[ub])
                    wsl = gi % 2
                    usl = gbi % 2
                    mml(PS(gb, 128, TT),
                        [(wg_sl[wsl][:, f * OG + oc * 128:f * OG + (oc + 1) * 128],
                          xnf(f)) for f in range(NF)], inc=mm)
                    g_mm = mm.n
                    mml(PS(ub, 128, TT),
                        [(wu_sl[usl][:, f * OG + oc * 128:f * OG + (oc + 1) * 128],
                          xnf(f)) for f in range(NF)], inc=mm)
                    u_mm = mm.n
                    ssl = gc % 2
                    kb.W("a", mm, g_mm)
                    kb.W("a", dv, fuse_g.get(("s", ssl), 0))
                    kb.E("a", "activation", silu_st[ssl], PS(gb, 128, TT),
                         AF.Silu, bias=0.0, scale=sg_sb[:, gc:gc + 1], inc=ac)
                    silu_g[gb] = ac.n
                    kb.W("v", mm, u_mm)
                    kb.W("v", ac, ac.n)
                    kb.W("v", mm, fuse_pe)
                    kb.W("v", dv, dv.n)
                    kb.E("v", "scalar_tensor_tensor",
                         fuse[:, gc * TT:(gc + 1) * TT], PS(ub, 128, TT),
                         su_sb[:, gc:gc + 1], silu_st[ssl], ALU.mult, ALU.mult,
                         inc=dv)
                    fuse_g[ub] = dv.n
                    fuse_g[("s", ssl)] = dv.n
                S_wA.done(gi, mm, mm.n)
                S_wB.done(gbi, mm, mm.n)
            fuse_dv = dv.n

            for fo in range(NF):
                di0 = tt * NF + fo
                dia = wd_i0 + 2 * di0
                dib = dia + 1
                S_xr.need("v", xo_i0 + di0)
                if di0 + 1 < NTTO * NF:
                    ld_xo(*divmod(di0 + 1, NF))
                db = 4 + (fo % 2)
                kb.W("t", dv, fuse_dv)
                bankw("t", db)
                S_wC.need("t", dia)
                for ic in range(NICH):
                    kb.E("t", "matmul", PS(db, 128, TT),
                         wd_sl[dia % 2][:, ic * 128:(ic + 1) * 128],
                         fuse[:, ic * TT:(ic + 1) * TT],
                         start=(ic == 0), stop=False,
                         inc=(mm if ic == NICH - 1 else None))
                S_wC.done(dia, mm, mm.n)
                if di0 + 1 < NTTO * NF:
                    ld_wd((di0 + 1) % NF, 0)
                S_wC.need("t", dib)
                for ic in range(NICH):
                    gic = NICH + ic
                    kb.E("t", "matmul", PS(db, 128, TT),
                         wd_sl[dib % 2][:, ic * 128:(ic + 1) * 128],
                         fuse[:, gic * TT:(gic + 1) * TT],
                         start=False, stop=(ic == NICH - 1),
                         inc=(mm if ic == NICH - 1 else None))
                S_wC.done(dib, mm, mm.n)
                if di0 + 1 < NTTO * NF:
                    ld_wd((di0 + 1) % NF, 1)
                kb.W("v", mm, mm.n)
                rsl = di0 % 2
                if rst_guard[rsl] is not None:
                    kb.W("v", rst_guard[rsl][0], rst_guard[rsl][1])
                kb.W("v", dv, dv.n)
                kb.E("v", "scalar_tensor_tensor", rst_sb[rsl], PS(db, 128, TT),
                     sd_sb[:, fo:fo + 1], xo_t[(xo_i0 + di0) % 2],
                     ALU.mult, ALU.add, inc=dv)
                bankg(db, dv)
                S_xr.done(xo_i0 + di0, dv, dv.n)
                rst_guard[rsl] = stg_store(
                    "y", res_scr[fo * 128:(fo + 1) * 128, tt * TT:(tt + 1) * TT],
                    rst_sb[rsl], dv, dv.n)
            fuse_pe = mm.n
            S_xg.done(xi, mm, mm.n)

        arena.pop()
        kb.barrier(exclude=("ccs",))

        for qn in KB.QUEUES:
            kb.W(qn, ccs, 1)
        arena.push()
        ctx_t = [arena.alloc(NF * TT) for _ in range(2)]
        wo_sl = [arena.alloc(NF * 128) for _ in range(2)]
        x1_sb = arena.alloc(NF * TT, F32)
        xn1_sb = arena.alloc(NF * TT)
        res_t = [arena.alloc(TT, F32) for _ in range(2)]
        mu4 = arena.alloc(TT, F32)
        r4 = arena.alloc(TT, F32)
        tmp4 = arena.alloc(TT, F32)
        rcol = arena.alloc(4, F32)
        lg_sb = arena.alloc(E, F32)
        pr_sb = arena.alloc(E, F32)
        m1c = arena.alloc(8, F32)
        tmp8 = arena.alloc(E, F32)
        comb8 = arena.alloc(TOK, F32, parts=E)
        stg4 = [arena.alloc(2 * HARM) for _ in range(4)]

        def ld_ctx(tt):
            def em(sem):
                gdma_role(sem, ctx_t[S_xg.i % 2].rearrange("p (f t) -> p f t", f=NF),
                          cc0_out.tensor, tt * TT,
                          [[S, 128], [S * 128, NF], [1, TT]])
            return S_xg.load(em)

        def ld_wo(oc):
            def em(sem):
                dma("g", sem, wo_sl[S_wA.i % 2],
                    wo_t[oc * 128:(oc + 1) * 128, :])
            return S_wA.load(em)

        def ld_res(tt, fo):
            def em(sem):
                dma("y", sem, res_t[S_sL.i % 2],
                    res_scr[fo * 128:(fo + 1) * 128, tt * TT:(tt + 1) * TT])
            return S_sL.load(em)

        ctx_i0 = ld_ctx(0)
        wo_i0 = ld_wo(0)
        res_i0 = ld_res(0, 0)
        x1_consumed = 0
        x1_pe = 0
        x1_store_g = []
        xn1_pe = 0

        for tt in range(NTTO):
            ci = ctx_i0 + tt
            if tt + 1 < NTTO:
                ld_ctx(tt + 1)
            S_xg.need("t", ci)
            ctxf = lambda f, s=tt % 2: ctx_t[s][:, f * TT:(f + 1) * TT]

            for oc in range(NF):
                oi = tt * NF + oc
                S_wA.need("t", wo_i0 + oi)
                S_sL.need("v", res_i0 + oi)
                if oi + 1 < NTTO * NF:
                    ld_wo((oi + 1) % NF)
                    ld_res(*divmod(oi + 1, NF))
                ob = oc % 2
                bankw("t", ob)
                wsl = (wo_i0 + oi) % 2
                mml(PS(ob, 128, TT),
                    [(wo_sl[wsl][:, f * 128:(f + 1) * 128], ctxf(f))
                     for f in range(NF)], inc=mm)
                S_wA.done(wo_i0 + oi, mm, mm.n)
                kb.W("v", mm, mm.n)
                kb.W("v", dv, x1_consumed)
                kb.W("v", mm, x1_pe)
                kb.W("v", mm, xn1_pe)
                for s_, v_ in x1_store_g:
                    kb.W("v", s_, v_)
                kb.W("v", dv, dv.n)
                kb.E("v", "tensor_add", x1_sb[:, oc * TT:(oc + 1) * TT],
                     PS(ob, 128, TT), res_t[(res_i0 + oi) % 2], inc=dv)
                bankg(ob, dv)
                S_sL.done(res_i0 + oi, dv, dv.n)
            x1_dv = dv.n
            S_xg.done(ci, mm, mm.n)

            kb.W("t", dv, x1_dv)
            bankw("t", 2)
            bankw("t", 3)
            mml(PS(2, 1, TT), [(ones_c_f[:, :], x1_sb[:, f * TT:(f + 1) * TT])
                               for f in range(NF)], inc=mm)
            for f in range(NF):
                kb.W("a", dv, x1_dv)
                kb.W("a", mm, mm.n)
                kb.E("a", "activation", tmp4, x1_sb[:, f * TT:(f + 1) * TT],
                     AF.Square, bias=0.0, scale=1.0, inc=ac)
                kb.W("t", ac, ac.n)
                kb.E("t", "matmul", PS(3, 1, TT), ones_c_f[:, :], tmp4,
                     start=(f == 0), stop=(f == NF - 1), inc=mm)
            sums4 = mm.n
            kb.W("a", mm, sums4)
            kb.E("a", "activation", row(0), PS(2, 1, TT), AF.Copy, bias=0.0,
                 scale=1.0 / D, inc=ac)
            kb.E("a", "activation", row(1), PS(3, 1, TT), AF.Copy, bias=0.0,
                 scale=1.0 / D, inc=ac)
            bankg(2, ac)
            bankg(3, ac)
            kb.W("a", ac, ac.n)
            kb.E("a", "activation", row(2), row(0), AF.Square, bias=0.0,
                 scale=1.0, inc=ac)
            kb.W("v", ac, ac.n)
            kb.W("v", dv, dv.n)
            kb.E("v", "tensor_sub", row(3), row(1), row(2), inc=dv)
            kb.W("a", dv, dv.n)
            kb.E("a", "activation", row(4), row(3), AF.Ln, bias=cns[0:1, 1:2],
                 scale=1.0, inc=ac)
            kb.W("a", ac, ac.n)
            kb.E("a", "activation", row(5), row(4), AF.Exp, bias=0.0, scale=-0.5,
                 inc=ac)
            r4_ac = ac.n
            kb.W("v", ac, r4_ac)
            kb.W("v", dv, dv.n)
            kb.E("v", "tensor_mul", row(6), row(0), row(5), inc=dv)
            mr_dv = dv.n

            kb.W("t", ac, r4_ac)
            bankw("t", 4)
            kb.E("t", "matmul", PS(4, 128, TT), ones_r_f[:, :], row(0),
                 start=True, stop=True, inc=mm)
            bankw("t", 5)
            kb.E("t", "matmul", PS(5, 128, TT), ones_r_f[:, :], row(5),
                 start=True, stop=True, inc=mm)
            kb.W("v", mm, mm.n)
            kb.W("v", dv, dv.n)
            kb.E("v", "tensor_copy", mu4, PS(4, 128, TT), inc=dv)
            kb.W("v", dv, dv.n)
            kb.E("v", "tensor_copy", r4, PS(5, 128, TT), inc=dv)
            bankg(4, dv)
            bankg(5, dv)
            bc4 = dv.n

            kb.W("v", dv, bc4)
            for f in range(NF):
                kb.W("v", dv, dv.n)
                kb.E("v", "tensor_sub", tmp4, x1_sb[:, f * TT:(f + 1) * TT], mu4,
                     inc=dv)
                kb.W("v", dv, dv.n)
                kb.E("v", "tensor_mul", xn1_sb[:, f * TT:(f + 1) * TT], tmp4, r4,
                     inc=dv)
            xn1_dv = dv.n

            x1_store_g = [
                stg_store("y",
                          x1_scr[:, tt * TT:(tt + 1) * TT].rearrange(
                              "(f p) t -> p f t", p=128),
                          x1_sb.rearrange("p (f t) -> p f t", f=NF), dv, xn1_dv),
                stg_store("y",
                          xn1_scr[:, tt * TT:(tt + 1) * TT].rearrange(
                              "(f p) t -> p f t", p=128),
                          xn1_sb.rearrange("p (f t) -> p f t", f=NF), dv, xn1_dv),
            ]

            kb.W("t", dv, xn1_dv)
            bankw("t", 6)
            mml(PSW(6, 0, HARM, TT),
                [(b4v[:, f, 0, :], xn1_sb[:, f * TT:(f + 1) * TT])
                 for f in range(NF)], inc=mm)
            kb.W("v", mm, mm.n)
            kb.W("v", dv, dv.n)
            kb.E("v", "tensor_copy", resq_sb[:, tt * TT:(tt + 1) * TT],
                 PSW(6, 0, HARM, TT), inc=dv)
            bankg(6, dv)

            for sub in range(TT // 128):
                rb_ = sub % 2
                bankw("t", rb_)
                mml(PS(rb_, 128, 2 * HARM),
                    [(xn1_sb[:, f * TT + sub * 128:f * TT + (sub + 1) * 128],
                      bkv_sb[:, f * 2 * HARM:(f + 1) * 2 * HARM])
                     for f in range(NF)], inc=mm)
                kb.W("v", mm, mm.n)
                ss = stg_state["i"]
                kb.W("v", stg[ss], stg_free[ss])
                kb.W("v", dv, dv.n)
                kb.E("v", "tensor_copy", stg4[ss], PS(rb_, 128, 2 * HARM), inc=dv)
                bankg(rb_, dv)
                stg_store("y",
                          cc1_in[tt * TT + sub * 128:tt * TT + (sub + 1) * 128, :],
                          stg4[ss], dv, dv.n)

            for sub in range(TT // 128):
                lb = 6 if sub % 2 == 0 else 7
                tb = 7 if lb == 6 else 6
                bankw("t", lb)
                kb.W("t", dv, mr_dv)
                for f in range(NF):
                    kb.E("t", "matmul", PS(lb, 128, E),
                         x1_sb[:, f * TT + sub * 128:f * TT + (sub + 1) * 128],
                         wr_sb[:, f * E:(f + 1) * E],
                         start=(f == 0), stop=False)
                kb.E("t", "matmul", PS(lb, 128, E),
                     row(6)[:, sub * 128:(sub + 1) * 128], ncs_wr[:, :],
                     start=False, stop=True, inc=mm)
                lg_mm = mm.n
                bankw("t", tb)
                kb.E("t", "matmul", PSW(tb, 0, 128, 1),
                     row(5)[:, sub * 128:(sub + 1) * 128], ones_c_f[0:1, 0:1],
                     start=True, stop=True, inc=mm)
                kb.W("v", mm, mm.n)
                kb.W("v", dv, dv.n)
                kb.E("v", "tensor_copy", rcol[:, 0:1], PSW(tb, 0, 128, 1), inc=dv)
                bankg(tb, dv)
                kb.W("v", mm, lg_mm)
                kb.W("v", dv, dv.n)
                kb.E("v", "tensor_scalar", lg_sb, PS(lb, 128, E), rcol[:, 0:1],
                     None, ALU.mult, inc=dv)
                bankg(lb, dv)
                kb.W("v", dv, dv.n)
                kb.E("v", "reduce_max", m1c[:, 0:1], lg_sb, axis=AX, negate=True,
                     inc=dv)
                kb.W("a", dv, dv.n)
                kb.E("a", "activation", pr_sb, lg_sb, AF.Exp,
                     bias=m1c[:, 0:1], scale=1.0, accum_out=m1c[:, 1:2], inc=ac)
                kb.W("v", ac, ac.n)
                kb.E("v", "reciprocal", m1c[:, 2:3], m1c[:, 1:2], inc=dv)
                kb.W("v", dv, dv.n)
                kb.E("v", "tensor_scalar", pr_sb, pr_sb, m1c[:, 2:3], None,
                     ALU.mult, inc=dv)
                kb.W("v", dv, dv.n)
                kb.E("v", "reduce_max", m1c[:, 3:4], pr_sb, axis=AX, inc=dv)
                kb.W("v", dv, dv.n)
                kb.E("v", "tensor_scalar", tmp8, pr_sb, m1c[:, 3:4], None,
                     ALU.is_ge, inc=dv)
                kb.W("v", dv, dv.n)
                kb.E("v", "scalar_tensor_tensor", tmp8, tmp8, -1e9, pr_sb,
                     ALU.mult, ALU.add, inc=dv)
                kb.W("v", dv, dv.n)
                kb.E("v", "reduce_max", m1c[:, 4:5], tmp8, axis=AX, inc=dv)
                kb.W("v", dv, dv.n)
                kb.E("v", "tensor_scalar", tmp8, pr_sb, m1c[:, 4:5], None,
                     ALU.is_ge, inc=dv)
                kb.W("v", dv, dv.n)
                kb.E("v", "tensor_mul", lg_sb, pr_sb, tmp8, inc=dv)
                comb_dv = dv.n
                kb.W("t", dv, comb_dv)
                bankw("t", tb)
                kb.E("t", "transpose", PSW(tb, 0, E, 128), lg_sb, id_f32[:, :],
                     inc=mm)
                kb.W("v", mm, mm.n)
                kb.W("v", dv, dv.n)
                kb.E("v", "tensor_copy",
                     comb8[:, tt * TT + sub * 128:tt * TT + (sub + 1) * 128],
                     PSW(tb, 0, E, 128), inc=dv)
                bankg(tb, dv)
            x1_consumed = dv.n
            x1_pe = mm.n
            xn1_pe = mm.n

        stg_store("y", comb_scr, comb8[:, :], dv, dv.n)
        for s in stg:
            kb.W("g", s, s.n)

        def fire_ag2(eng):
            eng.collective_compute(
                "AllGather", ALU.bypass,
                replica_groups=[[0, 1], [2, 3], [4, 5], [6, 7]],
                ins=[cc1_in.opt()], outs=[cc1_out.opt()],
            ).then_inc(ccs.h, 1)

        ccs.n += 1
        kb.q["g"].append(fire_ag2)
        arena.pop()
        kb.barrier(exclude=("ccs",))

        for qn in KB.QUEUES:
            kb.W(qn, ccs, 2)
        arena.push()
        rtmp = arena.alloc(NKL * HARM)
        sqf = arena.alloc(TT, F32, parts=HARM)

        def ld_rtmp(sem):
            dma("y", sem, rtmp.rearrange("p (kc h) -> p kc h", kc=NKL),
                cc1_out[:, 0:HARM].rearrange("(kc p) h -> p kc h", p=128))

        def ld_rva(sem):
            dma("y", sem,
                rva_sb.rearrange("p (kc x) -> p kc x", x=RVW)[:, :, 0:HARM],
                cc1_out[:, HARM:2 * HARM].rearrange("(kc p) h -> p kc h", p=128))

        rt_i = S_sL.load(ld_rtmp)
        rv_i = S_sL.load(ld_rva)
        S_sL.need("v", rt_i)
        S_sL.need("v", rv_i)
        kb.E("v", "memset",
             rva_sb.rearrange("p (kc x) -> p kc x", x=RVW)[:, :,
                                                          HARM:HARM + 1],
             1.0, inc=dv)
        kb.W("v", dv, dv.n)
        kb.E("v", "memset", rka_sb[HARM:HARM + 1, :], 1.0, inc=dv)
        rva_dv = dv.n
        S_sL.need("t", rt_i)
        S_sL.need("t", rv_i)
        for kc in range(NKL):
            tb = kc % 2
            bankw("t", tb)
            kb.E("t", "transpose",
                 PS(tb).bitcast(BF16)[0:HARM, 0:128],
                 rtmp[:, kc * HARM:(kc + 1) * HARM], id_bf[:, :], inc=mm)
            kb.W("v", mm, mm.n)
            kb.W("v", dv, dv.n)
            kb.E("v", "tensor_copy", rka_sb[0:HARM, kc * 128:(kc + 1) * 128],
                 PS(tb).bitcast(BF16)[0:HARM, 0:128], inc=dv)
            bankg(tb, dv)
        rkf_dv = dv.n
        S_sL.done(rt_i, mm, mm.n)
        S_sL.done(rv_i, dv, dv.n)

        CH = TT
        kb.W("t", dv, rkf_dv)
        for sc_ in range(S // CH):
            sl_ = rka_sb[0:HARM, sc_ * CH:(sc_ + 1) * CH]
            bankw("t", 4)
            kb.W("t", dv, dv.n)
            kb.E("t", "matmul", PSW(4, 0, 1, CH), ones_c_bf[0:HARM, 0:1], sl_,
                 start=True, stop=True, inc=mm)
            kb.W("a", mm, mm.n)
            kb.W("a", ac, ac.n)
            kb.E("a", "activation", sqf[:, 0:CH], sl_, AF.Square, bias=0.0,
                 scale=1.0, inc=ac)
            kb.W("t", ac, ac.n)
            bankw("t", 5)
            kb.E("t", "matmul", PSW(5, 0, 1, CH), ones_c_f[0:HARM, 0:1],
                 sqf[:, 0:CH], start=True, stop=True, inc=mm)
            kb.W("a", mm, mm.n)
            kb.E("a", "activation", row(0)[:, 0:CH], PSW(4, 0, 1, CH), AF.Copy,
                 bias=0.0, scale=1.0, inc=ac)
            bankg(4, ac)
            kb.W("a", ac, ac.n)
            kb.E("a", "activation", row(1)[:, 0:CH], row(0)[:, 0:CH], AF.Square,
                 bias=0.0, scale=1.0, inc=ac)
            kb.W("v", ac, ac.n)
            kb.W("v", mm, mm.n)
            kb.W("v", dv, dv.n)
            kb.E("v", "scalar_tensor_tensor", row(1)[:, 0:CH], row(1)[:, 0:CH],
                 -1.0 / 64.0, PSW(5, 0, 1, CH), ALU.mult, ALU.add, inc=dv)
            bankg(5, dv)
            kb.W("v", dv, dv.n)
            kb.E("v", "reduce_max", bstat[:, 3:4], row(1)[:, 0:CH], axis=AX,
                 inc=dv)
            kb.W("v", dv, dv.n)
            kb.E("v", "reduce_max", bstat[:, 4:5], row(0)[:, 0:CH], axis=AX,
                 inc=dv)
            kb.W("v", dv, dv.n)
            kb.E("v", "tensor_scalar", row(1)[:, 0:CH], row(0)[:, 0:CH], -1.0,
                 None, ALU.mult, inc=dv)
            kb.W("v", dv, dv.n)
            kb.E("v", "reduce_max", bstat[:, 6:7], row(1)[:, 0:CH], axis=AX,
                 inc=dv)
            kb.W("v", dv, dv.n)
            if sc_ == 0:
                kb.E("v", "tensor_copy", bstat[:, 0:1], bstat[:, 4:5], inc=dv)
                kb.W("v", dv, dv.n)
                kb.E("v", "tensor_copy", bstat[:, 1:2], bstat[:, 6:7], inc=dv)
                kb.W("v", dv, dv.n)
                kb.E("v", "tensor_copy", bstat[:, 2:3], bstat[:, 3:4], inc=dv)
            else:
                kb.E("v", "tensor_max", bstat[:, 0:1], bstat[:, 0:1],
                     bstat[:, 4:5], inc=dv)
                kb.W("v", dv, dv.n)
                kb.E("v", "tensor_max", bstat[:, 1:2], bstat[:, 1:2],
                     bstat[:, 6:7], inc=dv)
                kb.W("v", dv, dv.n)
                kb.E("v", "tensor_max", bstat[:, 2:3], bstat[:, 2:3],
                     bstat[:, 3:4], inc=dv)
            kb.W("v", dv, dv.n)
        kb.E("v", "tensor_scalar", bstat[:, 0:1], bstat[:, 0:1], 1.0 / 64.0,
             None, ALU.mult, inc=dv)
        kb.W("v", dv, dv.n)
        kb.E("v", "tensor_scalar", bstat[:, 1:2], bstat[:, 1:2], -1.0 / 64.0,
             None, ALU.mult, inc=dv)
        ch_dv = dv.n
        arena.pop()
        kb.barrier()


        for pss in range(PASSES):
            q0 = pss * TOKP
            GW = min(256, DE, DS)
            arena.push()
            res2 = arena.alloc(NF * TOKP, F32)
            xn1_t = arena.alloc(NF * TOKP)
            tmpa2 = [arena.alloc(TT)[0:HARM + 1, :] for _ in range(2)]
            sq5 = arena.alloc(TT, F32, parts=HARM)
            ubuf = arena.alloc(max(NKL, NDS, NDE) * TT)
            pb5 = ubuf[:, 0:NKL * TT]
            fuse5 = ubuf[:, 0:max(NDS, NDE) * TT]
            ctx64 = arena.alloc(TT)[0:64, :]
            outh = arena.alloc(TT)
            ores = [arena.alloc(TT) for _ in range(NTP)]
            rb5 = arena.alloc(TT, F32)
            x1t5 = [arena.alloc(TT, F32) for _ in range(2)]
            silu5 = [arena.alloc(TT, F32) for _ in range(2)]
            silu5b = arena.alloc(TT, F32)
            cbb = arena.alloc(TT, F32)
            crow2 = [arena.alloc(TT, F32, parts=1) for _ in range(2)]
            wgs = [arena.alloc(NF * GW) for _ in range(2)]
            wus = [arena.alloc(NF * GW) for _ in range(2)]
            wds = [arena.alloc(max(NDS, NDE) * 128) for _ in range(2)]

            exp_g5 = {0: 0, 2: 0}
            p5_pe = 0
            tmpa_pe = [0, 0]
            tmpa_done = [0, 0]
            idxs = [(h, tp) for h in range(NHL1) for tp in range(NTP)]

            kb.W("t", dv, rva_dv)
            kb.W("t", dv, ch_dv)

            def emit_tmpbound(i):
                h, tp = idxs[i]
                sl = i % 2
                bankw("t", 7)
                kb.W("v", mm, tmpa_pe[sl])
                kb.E("t", "matmul", PSW(7, 0, HARM, TT),
                     nh_sb[:, h * HARM:(h + 1) * HARM],
                     resq_sb[:, q0 + tp * TT:q0 + (tp + 1) * TT],
                     start=True, stop=True, inc=mm)
                kb.W("v", mm, mm.n)
                kb.W("v", dv, dv.n)
                kb.E("v", "tensor_copy", tmpa2[sl][0:HARM, :],
                     PSW(7, 0, HARM, TT), inc=dv)
                bankg(7, dv)
                tmp_dv = dv.n
                kb.W("t", dv, tmp_dv)
                bankw("t", 7)
                kb.E("t", "matmul", PSW(7, 0, 1, TT), ones_c_bf[0:HARM, 0:1],
                     tmpa2[sl][0:HARM, :], start=True, stop=True, inc=mm)
                w_mm = mm.n
                kb.W("a", dv, tmp_dv)
                kb.W("a", ac, ac.n)
                kb.E("a", "activation", sq5, tmpa2[sl][0:HARM, :], AF.Square,
                     bias=0.0, scale=1.0, inc=ac)
                kb.W("a", mm, w_mm)
                kb.E("a", "activation", row(0), PSW(7, 0, 1, TT), AF.Copy,
                     bias=0.0, scale=1.0, inc=ac)
                w_ac = ac.n
                kb.W("t", ac, w_ac)
                kb.E("t", "matmul", PSW(7, 0, 1, TT), ones_c_f[0:HARM, 0:1],
                     sq5, start=True, stop=True, inc=mm)
                t2_mm = mm.n
                kb.W("a", mm, t2_mm)
                kb.W("a", ac, ac.n)
                kb.E("a", "activation", row(1), PSW(7, 0, 1, TT), AF.Copy,
                     bias=0.0, scale=1.0, inc=ac)
                bankg(7, ac)
                rows_ac = ac.n
                kb.W("v", ac, rows_ac)
                kb.W("v", mm, prev5["rbmm5"])
                kb.W("v", dv, dv.n)
                kb.E("v", "tensor_scalar", row(3), row(0), bstat[:, 0:1], None,
                     ALU.mult, inc=dv)
                kb.W("v", dv, dv.n)
                kb.E("v", "tensor_scalar", row(2), row(0), bstat[:, 1:2], None,
                     ALU.mult, inc=dv)
                kb.W("v", dv, dv.n)
                kb.E("v", "tensor_max", row(3), row(3), row(2), inc=dv)
                kb.W("a", dv, dv.n)
                kb.E("a", "activation", row(2), row(0), AF.Square, bias=0.0,
                     scale=0.125, inc=ac)
                kb.W("v", ac, ac.n)
                kb.W("v", dv, dv.n)
                kb.E("v", "tensor_sub", row(2), row(1), row(2), inc=dv)
                kb.W("v", dv, dv.n)
                kb.E("v", "tensor_scalar_max", row(2), row(2), 0.0, inc=dv)
                kb.W("v", dv, dv.n)
                kb.E("v", "tensor_scalar", row(2), row(2), bstat[:, 2:3], None,
                     ALU.mult, inc=dv)
                kb.W("a", dv, dv.n)
                kb.E("a", "activation", row(2), row(2), AF.Sqrt, bias=0.0,
                     scale=1.0, inc=ac)
                kb.W("v", ac, ac.n)
                kb.W("v", dv, dv.n)
                kb.E("v", "tensor_add", row(3), row(3), row(2), inc=dv)
                kb.W("v", dv, dv.n)
                kb.E("v", "tensor_scalar", row(3), row(3), -1.0, -2.0,
                     ALU.mult, ALU.add, inc=dv)
                kb.W("v", dv, dv.n)
                kb.E("v", "tensor_copy", tmpa2[sl][HARM:HARM + 1, :], row(3),
                     inc=dv)
                tmpa_done[sl] = dv.n

            def emit_scores5(i, g, ng):
                sl = i % 2
                k0 = 2 * g
                kcs = [k0] if k0 == NKL - 1 else [k0, k0 + 1]
                pb_ = (g % 2) * 2
                kb.W("t", ac, exp_g5[pb_])
                if g == 0:
                    kb.W("t", dv, tmpa_done[sl])
                for j, kc in enumerate(kcs):
                    kb.E("t", "matmul", PS(pb_ + j, 128, TT),
                         rka_sb[:, kc * 128:(kc + 1) * 128],
                         tmpa2[sl], start=True, stop=True, inc=mm)
                kb.W("a", mm, mm.n)
                kb.W("a", mm, p5_pe)
                nk_ = len(kcs)
                src_ = psum[0:128, pb_ * 512:(pb_ + 2) * 512].rearrange(
                    "p (g r) -> p g r", g=2)[:, 0:nk_, 0:TT]
                dst_ = pb5[:, k0 * TT:(k0 + nk_) * TT].rearrange(
                    "p (g t) -> p g t", g=nk_)
                kb.E("a", "activation", dst_, src_, AF.Exp, bias=0.0,
                     scale=ISCALE, inc=ac)
                exp_g5[pb_] = ac.n
                return kcs

            ng5 = (NKL + 1) // 2
            pending5 = [None]
            prev5 = {"rb5": 0, "rbmm5": 0, "outh_pe": 0}
            emit_tmpbound(0)

            def tail5(i, h, tp, ctx_mm):
                kb.W("v", mm, ctx_mm)
                kb.W("v", dv, dv.n)
                kb.E("v", "tensor_copy", ctx64, PSW(4, 0, HARM, TT), inc=dv)
                kb.W("v", dv, dv.n)
                kb.E("v", "reciprocal", row(7), PSW(4, HARM, HARM + 1, TT),
                     inc=dv)
                ctx_dv = dv.n
                kb.W("t", dv, ctx_dv)
                kb.W("t", dv, prev5["rb5"])
                kb.E("t", "matmul", PS(4, 128, TT), ones_r_f[:, :], row(7),
                     start=True, stop=True, inc=mm)
                rbmm = mm.n
                prev5["rbmm5"] = rbmm
                bankw("t", 5)
                kb.E("t", "matmul", PS(5, 128, TT),
                     wTvo_sb[:, h * 128:(h + 1) * 128],
                     ctx64, start=True, stop=True, inc=mm)
                oh_mm = mm.n
                kb.W("v", mm, rbmm)
                kb.W("v", dv, dv.n)
                kb.E("v", "tensor_copy", rb5, PS(4, 128, TT), inc=dv)
                bankg(4, dv)
                prev5["rb5"] = dv.n
                kb.W("v", mm, oh_mm)
                kb.W("v", mm, prev5.get("outh_pe", 0))
                kb.W("v", dv, dv.n)
                kb.E("v", "tensor_mul", outh, PS(5, 128, TT), rb5, inc=dv)
                bankg(5, dv)
                outh_dv = dv.n
                kb.W("t", dv, outh_dv)
                if h == 0:
                    bankw("t", 6)
                kb.E("t", "matmul", PSW(6, 0, HARM, TT),
                     b4v[:, h, 1, :], outh,
                     start=(h == 0), stop=(h == NHL1 - 1), inc=mm)
                prev5["outh_pe"] = mm.n

            for i, (h, tp) in enumerate(idxs):
                emit_scores5(i, 0, ng5)
                if pending5[0] is not None:
                    pending5[0]()
                    pending5[0] = None
                for g in range(ng5):
                    if g + 1 < ng5:
                        emit_scores5(i, g + 1, ng5)
                    if g == 2 and i + 1 < len(idxs):
                        emit_tmpbound(i + 1)
                    kcs = [2 * g] if 2 * g == NKL - 1 else [2 * g, 2 * g + 1]
                    kb.W("t", ac, exp_g5[(g % 2) * 2])
                    if g == 0:
                        bankw("t", 4)
                    for kc in kcs:
                        kb.E("t", "matmul", PSW(4, 0, HARM + 1, TT),
                             rva_sb[:, kc * RVW:kc * RVW + HARM + 1],
                             pb5[:, kc * TT:(kc + 1) * TT],
                             start=(kc == 0), stop=(kc == NKL - 1),
                             inc=(mm if kc == NKL - 1 else None))
                if ng5 <= 2 and i + 1 < len(idxs):
                    emit_tmpbound(i + 1)
                ctx_mm = mm.n
                p5_pe = ctx_mm
                tmpa_pe[i % 2] = ctx_mm
                pending5[0] = (lambda i_=i, h_=h, tp_=tp, c_=ctx_mm:
                               tail5(i_, h_, tp_, c_))
            if pending5[0] is not None:
                pending5[0]()
                pending5[0] = None
            ores_mm = mm.n

            for tp in range(NTP):
                kb.W("v", mm, ores_mm)
                kb.W("v", dv, dv.n)
                kb.E("v", "tensor_copy", ores[tp][0:HARM, 0:TT],
                     PSW(6 + tp, 0, HARM, TT), inc=dv)
                bankg(6 + tp, dv)
            ores_dv = dv.n

            def ld_x1(tt, fo):
                def em(sem):
                    dma("y", sem, x1t5[S_sL.i % 2],
                        x1_scr[fo * 128:(fo + 1) * 128,
                               q0 + tt * TT:q0 + (tt + 1) * TT])
                return S_sL.load(em)

            x1_i0 = ld_x1(0, 0)
            for tp in range(NTP):
                for oc in range(NF):
                    ii = tp * NF + oc
                    S_sL.need("v", x1_i0 + ii)
                    if ii + 1 < NTP * NF:
                        ld_x1(*divmod(ii + 1, NF))
                    b_ = oc % 2
                    kb.W("t", dv, ores_dv)
                    bankw("t", b_)
                    kb.E("t", "matmul", PS(b_, 128, TT),
                         wTvo_sb[:, D + oc * 128:D + (oc + 1) * 128],
                         ores[tp][0:HARM, 0:TT], start=True, stop=True, inc=mm)
                    kb.W("v", mm, mm.n)
                    kb.W("v", dv, dv.n)
                    kb.E("v", "tensor_add",
                         res2[:, oc * TOKP + tp * TT:oc * TOKP + (tp + 1) * TT],
                         PS(b_, 128, TT), x1t5[(x1_i0 + ii) % 2], inc=dv)
                    bankg(b_, dv)
                    S_sL.done(x1_i0 + ii, dv, dv.n)

            def ld_xn1(sem):
                dma("y", sem, xn1_t.rearrange("p (f t) -> p f t", f=NF),
                    xn1_scr[:, q0:q0 + TOKP].rearrange("(f p) t -> p f t", p=128))

            xn1_i = S_kb.load(ld_xn1)
            S_kb.need("t", xn1_i)

            state = {"silu": {0: 0, 1: 0}, "fuse": {2: 0, 3: 0, "s": 0},
                     "fuse_pe": p5_pe, "cbb_dv": 0, "cbb_cons": 0}

            def mk_ldg(w_dram, groups, bufs, stream):
                def ld(ig):
                    og0, gw = groups[ig]

                    def em(sem):
                        dma(stream.qn, sem, bufs[stream.i % 2][:, 0:NF * gw],
                            w_dram[:, NF * og0:NF * og0 + NF * gw])
                    return stream.load(em)
                return ld

            def mk_ldd(w_dram, nic_, stream):
                def ld(fo):
                    def em(sem):
                        dma(stream.qn, sem, wds[stream.i % 2][:, 0:nic_ * 128],
                            w_dram[:, fo * nic_ * 128:(fo + 1) * nic_ * 128])
                    return stream.load(em)
                return ld

            def ffn_block(n_out_chunks, groups, ldg, ldu, ldd, use_comb,
                          e_idx, stA=None, stB=None, stC=None):
                for tp in range(NTP):
                    xo5 = lambda f: xn1_t[:, f * TOKP + tp * TT:
                                          f * TOKP + (tp + 1) * TT]
                    if use_comb:
                        def ld_c(sem):
                            dma("y", sem, crow2[S_sL.i % 2],
                                comb_scr[e_idx:e_idx + 1,
                                         q0 + tp * TT:q0 + (tp + 1) * TT])
                        ci_ = S_sL.load(ld_c)
                        S_sL.need("t", ci_)
                        bankw("t", 7)
                        kb.E("t", "matmul", PS(7, 128, TT), ones_r_f[:, :],
                             crow2[ci_ % 2], start=True, stop=True, inc=mm)
                        S_sL.done(ci_, mm, mm.n)
                        kb.W("v", mm, mm.n)
                        kb.W("v", dv, state["cbb_cons"])
                        kb.W("v", dv, dv.n)
                        kb.E("v", "tensor_copy", cbb, PS(7, 128, TT), inc=dv)
                        bankg(7, dv)
                        state["cbb_dv"] = dv.n
                    gis = []
                    for ig in range(len(groups)):
                        gis.append((ldg(ig), ldu(ig)))
                    for ig, (og0, gw) in enumerate(groups):
                        gi_, ui_ = gis[ig]
                        stA.need("t", gi_)
                        stB.need("t", ui_)
                        for oc in range(gw // 128):
                            gc = og0 // 128 + oc
                            gb = oc % 2
                            ub = 2 + oc % 2
                            ssl = gc % 2
                            kb.W("t", ac, state["silu"][gb])
                            kb.W("t", dv, state["fuse"][ub])
                            bankw("t", gb)
                            bankw("t", ub)
                            mml(PS(gb, 128, TT),
                                [(wgs[gi_ % 2][:, f * gw + oc * 128:
                                               f * gw + (oc + 1) * 128], xo5(f))
                                 for f in range(NF)], inc=mm)
                            g_mm = mm.n
                            mml(PS(ub, 128, TT),
                                [(wus[ui_ % 2][:, f * gw + oc * 128:
                                               f * gw + (oc + 1) * 128], xo5(f))
                                 for f in range(NF)], inc=mm)
                            u_mm = mm.n
                            kb.W("a", mm, g_mm)
                            kb.W("a", dv, state["fuse"].get(("s", ssl), 0))
                            kb.E("a", "activation", silu5[ssl], PS(gb, 128, TT),
                                 AF.Silu, bias=0.0, scale=1.0, inc=ac)
                            state["silu"][gb] = ac.n
                            bankg(gb, ac)
                            kb.W("v", ac, ac.n)
                            if use_comb:
                                kb.W("v", dv, state["cbb_dv"])
                                kb.W("v", dv, dv.n)
                                kb.E("v", "tensor_mul", silu5b, silu5[ssl], cbb,
                                     inc=dv)
                                state["cbb_cons"] = dv.n
                                sl_use = silu5b
                            else:
                                sl_use = silu5[ssl]
                            kb.W("v", mm, u_mm)
                            kb.W("v", mm, state["fuse_pe"])
                            kb.W("v", dv, dv.n)
                            kb.E("v", "tensor_mul",
                                 fuse5[:, gc * TT:(gc + 1) * TT],
                                 PS(ub, 128, TT), sl_use, inc=dv)
                            state["fuse"][ub] = dv.n
                            state[("fuse", "s", ssl)] = dv.n
                            state["fuse"][("s", ssl)] = dv.n
                            bankg(ub, dv)
                            state["fuse"]["s"] = dv.n
                        stA.done(gi_, mm, mm.n)
                        stB.done(ui_, mm, mm.n)
                    fuse_dv_ = dv.n
                    dls = [ldd(0)]
                    for fo in range(NF):
                        di_ = dls[fo]
                        if fo + 1 < NF:
                            dls.append(ldd(fo + 1))
                        stC.need("t", di_)
                        db = 4 + fo % 2
                        kb.W("t", dv, fuse_dv_)
                        bankw("t", db)
                        mml(PS(db, 128, TT),
                            [(wds[di_ % 2][:, ic * 128:(ic + 1) * 128],
                              fuse5[:, ic * TT:(ic + 1) * TT])
                             for ic in range(n_out_chunks)], inc=mm)
                        stC.done(di_, mm, mm.n)
                        kb.W("v", mm, mm.n)
                        kb.W("v", dv, dv.n)
                        kb.E("v", "tensor_add",
                             res2[:, fo * TOKP + tp * TT:
                                  fo * TOKP + (tp + 1) * TT],
                             PS(db, 128, TT),
                             res2[:, fo * TOKP + tp * TT:
                                  fo * TOKP + (tp + 1) * TT], inc=dv)
                        bankg(db, dv)
                    state["fuse_pe"] = mm.n

            ffn_block(NDS, ds_groups,
                      mk_ldg(wsg_b, ds_groups, wgs, S_yA),
                      mk_ldg(wsu_b, ds_groups, wus, S_yB),
                      mk_ldd(wsd_b, NDS, S_yC), False, 0,
                      stA=S_yA, stB=S_yB, stC=S_yC)

            for e in range(E):
                wege = weg_b[e * 128:(e + 1) * 128, :]
                weue = weu_b[e * 128:(e + 1) * 128, :]
                wede = wed_b[e * 128:(e + 1) * 128, :]
                ffn_block(NDE, de_groups,
                          mk_ldg(wege, de_groups, wgs, S_yA),
                          mk_ldg(weue, de_groups, wus, S_yB),
                          mk_ldd(wede, NDE, S_yC), True, e,
                          stA=S_yA, stB=S_yB, stC=S_yC)
            S_kb.done(xn1_i, mm, mm.n)

            out_dv = dv.n
            kb.W("y", dv, out_dv)
            for fo in range(NF):
                dma("y", ys, yout[fo * 128:(fo + 1) * 128, q0:q0 + TOKP],
                    res2[:, fo * TOKP:(fo + 1) * TOKP])
            arena.pop()
            kb.barrier()

        kb.W("y", ys, ys.n)

        @block.tensor
        def _(eng):
            for f in kb.q["t"]:
                f(eng)

        @block.vector
        def _(eng):
            for f in kb.q["v"]:
                f(eng)

        @block.scalar
        def _(eng):
            for f in kb.q["a"]:
                f(eng)

        @block.gpsimd
        def _(eng):
            for f in kb.q["g"]:
                f(eng)

        @block.sync
        def _(eng):
            for f in kb.q["y"]:
                f(eng)

    return nc



def make_masks(cfg):
    TT, NM = cfg["TT"], cfg["NM"]
    k = np.arange(NM * 128)[:, None]
    qq = np.arange(TT)[None, :]
    kk = k % 128 + (k // 128) * 128
    return np.where(qq >= kk, 0.0, NEGM).astype(np.float32)


def prep_core_inputs(cfg, I, core):
    S, D, TT = cfg["S"], cfg["D"], cfg["TT"]
    NH, HD, TOK = cfg["NH_CORE"], cfg["HD"], cfg["TOK"]
    NIC, NF = cfg["NIC"], cfg["NF"]
    DE, DS = cfg["DE"], cfg["DS"]
    b, r = core // 2, core % 2
    C = np.ascontiguousarray

    def tfm(w, gw):
        K_, O_ = w.shape
        return C(w.reshape(K_ // 128, 128, O_ // gw, gw)
                 .transpose(2, 1, 0, 3).reshape((O_ // gw) * 128,
                                                (K_ // 128) * gw))

    def tgrp(w, groups):
        Kc = w.shape[0] // 128
        parts = [w.reshape(Kc, 128, w.shape[1])[:, :, og0:og0 + gw]
                 .transpose(1, 0, 2).reshape(128, Kc * gw)
                 for og0, gw in groups]
        return np.concatenate(parts, axis=1)

    GW0 = min(256, DE, DS)
    ds_groups = []
    _o = 0
    while _o < DS:
        _w = min(GW0, DS - _o)
        ds_groups.append((_o, _w))
        _o += _w
    de_groups = []
    _o = 0
    while _o < DE:
        _w = min(GW0, DE - _o)
        de_groups.append((_o, _w))
        _o += _w
    fo_groups = [(fo * 128, 128) for fo in range(NF)]

    d = {}
    x = np.asarray(I["L0_attn_x"][b])
    d["xT"] = C(x.T)
    d["xoT"] = C(x.T[:, r * TOK:(r + 1) * TOK])
    hsl = slice(r * HD, (r + 1) * HD)
    d["wq_t"] = tfm(C(np.asarray(I["L0_attn_Wq"]).T[:, hsl]), 128)
    d["wk_t"] = tfm(C(np.asarray(I["L0_attn_Wk"]).T[:, hsl]), HD)
    d["wv_t"] = tfm(C(np.asarray(I["L0_attn_Wv"]).T[:, hsl]), HD)
    d["wo_t"] = tfm(C(np.asarray(I["L0_attn_Wo"]).T), 128)
    d["fg8t"] = tfm(np.asarray(I["L0_ffn_w_gate"]), 128)
    d["fu8t"] = tfm(np.asarray(I["L0_ffn_w_up"]), 128)
    d["fd8t"] = tfm(np.asarray(I["L0_ffn_w_down"]), 128)
    d["sgT"] = C(np.asarray(I["L0_ffn_s_gate"]).reshape(NIC, 128).T)
    d["suT"] = C(np.asarray(I["L0_ffn_s_up"]).reshape(NIC, 128).T)
    d["sdT"] = C(np.asarray(I["L0_ffn_s_down"]).reshape(NF, 128).T)
    d["bqT"] = C(np.asarray(I["L1_attn_basis_q"]).T)
    d["bkT"] = C(np.asarray(I["L1_attn_basis_k"]).T)
    d["bvT"] = C(np.asarray(I["L1_attn_basis_v"]).T)
    d["boT"] = C(np.asarray(I["L1_attn_basis_o"]).T)
    d["phT"] = C(np.concatenate(
        [np.asarray(I[f"L1_attn_phase_{x_}"]).T for x_ in "qkvo"], 0))
    d["amT"] = C(np.concatenate(
        [np.asarray(I[f"L1_attn_amp_{x_}"]).T for x_ in "qkvo"], 0))
    d["wr"] = C(np.asarray(I["L1_ffn_W_router"]))
    weg = np.asarray(I["L1_ffn_We_gate"])
    weu = np.asarray(I["L1_ffn_We_up"])
    wed = np.asarray(I["L1_ffn_We_down"])
    d["weg_t"] = C(np.stack([tgrp(weg[e], de_groups) for e in range(8)])
                   .reshape(8 * 128, -1))
    d["weu_t"] = C(np.stack([tgrp(weu[e], de_groups) for e in range(8)])
                   .reshape(8 * 128, -1))
    d["wed_t"] = C(np.stack([tgrp(wed[e], fo_groups) for e in range(8)])
                   .reshape(8 * 128, -1))
    d["wsg_t"] = C(tgrp(np.asarray(I["L1_ffn_Ws_gate"]), ds_groups))
    d["wsu_t"] = C(tgrp(np.asarray(I["L1_ffn_Ws_up"]), ds_groups))
    d["wsd_t"] = C(tgrp(np.asarray(I["L1_ffn_Ws_down"]), fo_groups))
    d["idn"] = np.eye(128, dtype=np.float32)
    d["cmask"] = make_masks(cfg)
    d["rolep"] = np.array([[r * TOK, 0]], np.int32)
    return d


_NC_CACHE = {}


def kernel(**inputs):
    cfg = make_cfg(mini=False)
    if "full" not in _NC_CACHE:
        _NC_CACHE["full"] = build_nc(cfg)
    nc = _NC_CACHE["full"]
    in_maps = [prep_core_inputs(cfg, inputs, c) for c in range(8)]
    res = run_bass_kernel_spmd(nc, in_maps, core_ids=list(range(8)))
    S, D, TOK = cfg["S"], cfg["D"], cfg["TOK"]
    out = np.empty((4, S, D), np.float32)
    for c in range(8):
        b, r = c // 2, c % 2
        out[b, r * TOK:(r + 1) * TOK, :] = res.results[c]["yout"].T
    return out


# version 08
# speedup vs baseline: 1.0128x; 1.0128x over previous
import contextlib
import numpy as np

import concourse.bass as bass
import concourse.mybir as mybir
from concourse.bass_utils import run_bass_kernel_spmd

F32 = mybir.dt.float32
BF16 = mybir.dt.bfloat16
I8 = mybir.dt.int8
I32 = mybir.dt.int32
AF = mybir.ActivationFunctionType
ALU = mybir.AluOpType
AX = mybir.AxisListType.X

NEGM = -1.0e9


def make_cfg(mini=False):
    if mini:
        c = dict(S=512, D=256, NH_CORE=1, INTER=512, DE=256, DS=384, E=8, TT=128)
    else:
        c = dict(S=4096, D=2048, NH_CORE=8, INTER=8192, DE=1408, DS=2816, E=8, TT=512)
    c["HARM"] = 64
    c["TOK"] = c["S"] // 2
    c["NF"] = c["D"] // 128
    c["NTT"] = c["S"] // c["TT"]
    c["NTTO"] = c["TOK"] // c["TT"]
    c["HD"] = c["NH_CORE"] * 128
    c["NM"] = c["TT"] // 128
    c["NIC"] = c["INTER"] // 128
    c["NDS"] = c["DS"] // 128
    c["NDE"] = c["DE"] // 128
    c["NHL1"] = c["D"] // 128
    c["NKL"] = c["S"] // 128
    c["TOKP"] = max(c["TT"], c["TOK"] // 4)
    c["PASSES"] = c["TOK"] // c["TOKP"]
    c["NTP"] = max(1, c["TOKP"] // c["TT"])
    return c


class Sem:
    __slots__ = ("h", "n", "name")

    def __init__(self, h, name):
        self.h, self.n, self.name = h, 0, name


class KB:
    QUEUES = ("t", "v", "a", "g", "y")

    def __init__(self, nc, stack):
        self.nc = nc
        self.q = {k: [] for k in self.QUEUES}
        self.sems = {}
        self._bar = {}
        self._stack = stack

    def sem(self, name):
        s = Sem(self._stack.enter_context(self.nc.semaphore(name)), name)
        self.sems[name] = s
        self._bar[name] = 0
        return s

    def E(self, qn, fn, *a, inc=None, iv=1, **k):
        if inc is not None:
            inc.n += iv
        import traceback
        site = traceback.extract_stack(limit=3)[0]

        def run(eng, fn=fn, a=a, k=k, inc=inc, iv=iv, site=site):
            try:
                ins = getattr(eng, fn)(*a, **k)
            except Exception:
                print(f"EMIT FAIL {fn} at {site.filename}:{site.lineno}")
                for x in a:
                    if hasattr(x, 'shape'):
                        print("   arg", x.shape, getattr(x, 'dtype', None))
                raise
            if inc is not None:
                ins.then_inc(inc.h, iv)

        self.q[qn].append(run)
        return inc.n if inc is not None else None

    def W(self, qn, sem, val):
        if val and val > 0:
            self.q[qn].append(lambda eng, s=sem.h, v=val: eng.wait_ge(s, v))

    def barrier(self, exclude=()):
        snap = [(s, s.n) for nm, s in self.sems.items()
                if s.n > self._bar[nm] and nm not in exclude]
        for nm, s in self.sems.items():
            if nm not in exclude:
                self._bar[nm] = s.n
        for qn in self.QUEUES:
            for s, v in snap:
                self.W(qn, s, v)


class Arena:
    def __init__(self, nc, stack, total):
        self.t = stack.enter_context(nc.sbuf_tensor("arena", [128, total], BF16))
        self.total = total
        self.lo = 0
        self.marks = []
        self.peak = 0

    def alloc(self, n_free, dtype=BF16, parts=128):
        need = n_free * (2 if dtype == F32 else 1)
        off = self.lo
        self.lo += (need + 63) & ~63
        self.peak = max(self.peak, self.lo)
        assert self.lo <= self.total, f"arena overflow {self.lo}>{self.total}"
        ap = self.t[0:parts, off:off + need]
        return ap.bitcast(F32) if dtype == F32 else ap

    def push(self):
        self.marks.append(self.lo)

    def pop(self):
        self.lo = self.marks.pop()


class Stream:

    def __init__(self, kb, sems, qn):
        self.kb, self.sems, self.qn = kb, sems, qn
        self.i = 0
        self.vals = []
        self.free = [None, None]
        self.lastv = [0, 0]

    def load(self, emit):
        sl = self.i % 2
        if self.free[sl] is not None:
            s, v = self.free[sl]
            self.kb.W(self.qn, s, v)
            self.free[sl] = None
        if self.lastv[sl]:
            self.kb.W(self.qn, self.sems[sl], self.lastv[sl])
        emit(self.sems[sl])
        self.vals.append(self.sems[sl].n)
        self.lastv[sl] = self.sems[sl].n
        self.i += 1
        return self.i - 1

    def need(self, qn, i):
        self.kb.W(qn, self.sems[i % 2], self.vals[i])

    def done(self, i, sem, val):
        self.free[i % 2] = (sem, val)


def build_nc(cfg):
    S, D, TT = cfg["S"], cfg["D"], cfg["TT"]
    NF, NTT, NTTO = cfg["NF"], cfg["NTT"], cfg["NTTO"]
    NH, HD, NM = cfg["NH_CORE"], cfg["HD"], cfg["NM"]
    INTER, NIC = cfg["INTER"], cfg["NIC"]
    DS, NDS, DE, NDE, E = cfg["DS"], cfg["NDS"], cfg["DE"], cfg["NDE"], cfg["E"]
    NHL1, NKL = cfg["NHL1"], cfg["NKL"]
    TOK, TOKP, NTP = cfg["TOK"], cfg["TOKP"], cfg["NTP"]
    PASSES = cfg["PASSES"]
    HARM = cfg["HARM"]
    ISCALE = float(1.0 / np.float32(np.sqrt(np.float32(128.0))))
    EPS = 1e-5

    nc = bass.Bass("TRN2", target_bir_lowering=False, debug=False)

    def din(name, shape, dt=F32):
        return nc.dram_tensor(name, shape, dt, kind="ExternalInput").ap()

    GW0 = min(256, DE, DS)
    ds_groups = []
    _o = 0
    while _o < DS:
        _w = min(GW0, DS - _o)
        ds_groups.append((_o, _w))
        _o += _w
    de_groups = []
    _o = 0
    while _o < DE:
        _w = min(GW0, DE - _o)
        de_groups.append((_o, _w))
        _o += _w
    xT = din("xT", [D, S])
    xoT = din("xoT", [D, TOK])
    wq_t = din("wq_t", [NH * 128, NF * 128])
    wk_t = din("wk_t", [128, NF * HD])
    wv_t = din("wv_t", [128, NF * HD])
    wo_t = din("wo_t", [NF * 128, NF * 128])
    fg8t = din("fg8t", [(INTER // 128) * 128, NF * 128], I8)
    fu8t = din("fu8t", [(INTER // 128) * 128, NF * 128], I8)
    fd8t = din("fd8t", [NF * 128, NIC * 128], I8)
    sgT = din("sgT", [128, NIC])
    suT = din("suT", [128, NIC])
    sdT = din("sdT", [128, NF])
    bqT = din("bqT", [D, HARM])
    bkT = din("bkT", [D, HARM])
    bvT = din("bvT", [D, HARM])
    boT = din("boT", [D, HARM])
    phT = din("phT", [4 * HARM, D])
    amT = din("amT", [4 * HARM, D])
    wr = din("wr", [D, E])
    weg_t = din("weg_t", [E * 128, NF * DE])
    weu_t = din("weu_t", [E * 128, NF * DE])
    wed_t = din("wed_t", [E * 128, NF * DE])
    wsg_t = din("wsg_t", [128, NF * DS])
    wsu_t = din("wsu_t", [128, NF * DS])
    wsd_t = din("wsd_t", [128, NF * DS])
    idn = din("idn", [128, 128])
    cmask = din("cmask", [NM * 128, TT])
    rolep = din("rolep", [1, 2], I32)

    yout = nc.dram_tensor("yout", [D, TOK], F32, kind="ExternalOutput").ap()

    def scr(name, shape, dt=BF16):
        return nc.dram_tensor(name, shape, dt, kind="Internal").ap()

    q_scr = scr("q_scr", [HD, S])
    k_scr = scr("k_scr", [HD, S])
    v_scr = scr("v_scr", [NH * S, 128])
    xn0_scr = scr("xn0_scr", [D, S])
    xn0_own = scr("xn0_own", [D, TOK])
    ctx_own = scr("ctx_own", [2 * HD, TOK])
    cc0_in = scr("cc0_in", [HD, S])
    cc0_out = scr("cc0_out", [2 * HD, S])
    res_scr = scr("res_scr", [D, TOK], F32)
    x1_scr = scr("x1_scr", [D, TOK], F32)
    xn1_scr = scr("xn1_scr", [D, TOK])
    cc1_in = scr("cc1_in", [TOK, 2 * HARM])
    cc1_out = scr("cc1_out", [S, 2 * HARM])
    comb_scr = scr("comb_scr", [E, TOK], F32)
    weg_b = scr("weg_b", [E * 128, NF * DE])
    weu_b = scr("weu_b", [E * 128, NF * DE])
    wed_b = scr("wed_b", [E * 128, NF * DE])
    wsg_b = scr("wsg_b", [128, NF * DS])
    wsu_b = scr("wsu_b", [128, NF * DS])
    wsd_b = scr("wsd_b", [128, NF * DS])

    stack = contextlib.ExitStack()
    with stack:
        kb = KB(nc, stack)

        mm = kb.sem("mm")
        dv = kb.sem("dv")
        ac = kb.sem("ac")
        ccs = kb.sem("ccs")
        gS = kb.sem("gS")
        yS = kb.sem("yS")
        pl = kb.sem("pl")
        rS = kb.sem("rS")
        S_wA = Stream(kb, [kb.sem("wA0"), kb.sem("wA1")], "g")
        S_wB = Stream(kb, [kb.sem("wB0"), kb.sem("wB1")], "g")
        S_wC = Stream(kb, [kb.sem("wC0"), kb.sem("wC1")], "g")
        S_xg = Stream(kb, [kb.sem("xg0"), kb.sem("xg1")], "g")
        S_xr = Stream(kb, [kb.sem("xr0"), kb.sem("xr1")], "g")
        S_kb = Stream(kb, [kb.sem("kb0"), kb.sem("kb1")], "y")
        S_vb = Stream(kb, [kb.sem("vb0"), kb.sem("vb1")], "y")
        S_qb = Stream(kb, [kb.sem("qb0"), kb.sem("qb1")], "y")
        S_sL = Stream(kb, [kb.sem("sL0"), kb.sem("sL1")], "y")
        S_yA = Stream(kb, [kb.sem("yA0"), kb.sem("yA1")], "y")
        S_yB = Stream(kb, [kb.sem("yB0"), kb.sem("yB1")], "y")
        S_yC = Stream(kb, [kb.sem("yC0"), kb.sem("yC1")], "y")
        S_wC2 = Stream(kb, [kb.sem("wc20"), kb.sem("wc21")], "y")
        stg = [kb.sem(f"st{i}") for i in range(4)]
        stg_free = [0, 0, 0, 0]
        stg_state = {"i": 0}
        ys = kb.sem("ys")

        AR = 76000
        arena = Arena(nc, stack, AR)

        sb = lambda n, shp, dt: stack.enter_context(nc.sbuf_tensor(n, shp, dt))
        id_bf = sb("id_bf", [128, 128], BF16)
        id_f32 = sb("id_f32", [128, 128], F32)
        ones_c_bf = sb("ones_c_bf", [128, 1], BF16)
        ones_c_f = sb("ones_c_f", [128, 1], F32)
        ones_r_f = sb("ones_r_f", [1, 128], F32)
        sg_sb = sb("sg_sb", [128, NIC], F32)
        su_sb = sb("su_sb", [128, NIC], F32)
        sd_sb = sb("sd_sb", [128, NF], F32)
        wr_sb = sb("wr_sb", [128, NF * E], F32)
        ncs_wr = sb("ncs_wr", [1, E], F32)
        role_sb = sb("role_sb", [1, 2], I32)
        wTvo_sb = sb("wTvo_sb", [HARM, 2 * D], BF16)
        b4_sb = sb("b4_sb", [128, NF * 2 * HARM], BF16)
        bkv_sb = sb("bkv_sb", [128, NF * 2 * HARM], BF16)
        resq_sb = sb("resq_sb", [HARM, TOK], F32)
        rka_sb = sb("rka_sb", [HARM + 1, S], BF16)
        nh_sb = sb("nh_sb", [HARM, NHL1 * HARM], F32)
        RVW = HARM + 8
        rva_sb = sb("rva_sb", [128, NKL * RVW], BF16)
        rows_t = sb("rows_t", [1, 8 * TT + 16], F32)
        cns = sb("cns", [128, 2], F32)
        b4v = b4_sb.rearrange("p (f n h) -> p f n h", f=NF, n=2)

        def row(i):
            return rows_t[:, i * TT:(i + 1) * TT]

        bstat = rows_t[:, 8 * TT:8 * TT + 16]

        psum = stack.enter_context(nc.psum_tensor("psum", [128, 4096], F32))

        def PS(bank, parts=128, width=512):
            return psum[0:parts, bank * 512:bank * 512 + width]

        def PSW(bank, p0, p1, width, off=0):
            return psum[p0:p1, bank * 512 + off:bank * 512 + off + width]

        pg = {}

        def bankw(qn, bank):
            if bank in pg:
                s, v = pg[bank]
                kb.W(qn, s, v)

        def bankg(bank, sem):
            pg[bank] = (sem, sem.n)

        block = stack.enter_context(nc.Block())

        def mml(ps_ap, pairs, inc=None):
            n = len(pairs)
            for i, (l, r) in enumerate(pairs):
                kb.E("t", "matmul", ps_ap, l, r, start=(i == 0), stop=(i == n - 1),
                     inc=(inc if i == n - 1 else None))

        def dma(qn, sem, out, in_):
            return kb.E(qn, "dma_start", inc=sem, iv=16, out=out, in_=in_)

        groles = {}

        def gdma_role(sem, out, h, static_off, pattern):
            sem.n += 16

            def run(eng, out=out, h=h, so=static_off, pat=pattern, s=sem):
                eng.reg_add(groles["tmp"], groles["roff"], so)
                try:
                    eng.dma_start(out=out, in_=bass.AP(h, groles["tmp"], pat)
                                  ).then_inc(s.h, 16)
                except Exception:
                    print("GDMA FAIL", h.name, so, pat, out.shape)
                    raise

            kb.q["g"].append(run)

        def stg_store(wq, out_ap, src_ap, producer_sem, producer_val):
            ss = stg_state["i"]
            kb.W("y", producer_sem, producer_val)
            kb.W("y", stg[ss], stg_free[ss])
            dma("y", stg[ss], out_ap, src_ap)
            stg_free[ss] = stg[ss].n
            stg_state["i"] = (ss + 1) % 4
            return (stg[ss], stg[ss].n)

        dma("y", rS, role_sb[:, :], rolep)

        def g_pre(eng):
            eng.wait_ge(rS.h, 16)
            groles["roff"] = eng.alloc_register("roff")
            groles["tmp"] = eng.alloc_register("rtmp")
            eng.reg_load(groles["roff"], role_sb[0:1, 0:1])

        kb.q["g"].append(g_pre)

        dma("g", gS, id_bf[:, :], idn)
        dma("y", yS, id_f32[:, :], idn)
        dma("y", yS, sg_sb[:, :], sgT)
        dma("y", yS, su_sb[:, :], suT)
        dma("y", yS, sd_sb[:, :], sdT)
        dma("y", yS, wr_sb.rearrange("p (f e) -> p f e", f=NF),
            wr.rearrange("(f p) e -> p f e", p=128))
        for i, b in enumerate((bqT, boT)):
            dma("g", gS, b4v[:, :, i, :], b.rearrange("(f p) h -> p f h", p=128))
        bkvv = bkv_sb.rearrange("p (f n h) -> p f n h", f=NF, n=2)
        dma("g", gS, bkvv[:, :, 0, :], bkT.rearrange("(f p) h -> p f h", p=128))
        dma("g", gS, bkvv[:, :, 1, :], bvT.rearrange("(f p) h -> p f h", p=128))

        kb.E("v", "memset", ones_c_bf[:, :], 1.0, inc=dv)
        kb.E("v", "memset", ones_c_f[:, :], 1.0, inc=dv)
        kb.E("v", "memset", ones_r_f[:, :], 1.0, inc=dv)
        kb.E("v", "memset", cns[:, 0:1], float(np.pi / 2), inc=dv)
        kb.E("v", "memset", cns[:, 1:2], EPS, inc=dv)
        ones_dv = dv.n

        arena.push()
        ph_sb = arena.alloc(4 * D, F32, parts=HARM)
        am_sb = arena.alloc(4 * D, F32, parts=HARM)
        cosb = arena.alloc(4 * D, F32, parts=HARM)
        wTt = arena.alloc(4 * D, BF16, parts=HARM)
        wqtr = arena.alloc(NHL1 * HARM)
        wktr = arena.alloc(NHL1 * HARM)
        for i in range(4):
            dma("y", yS, ph_sb[:, i * D:(i + 1) * D],
                phT[i * HARM:(i + 1) * HARM, :])
            dma("y", yS, am_sb[:, i * D:(i + 1) * D],
                amT[i * HARM:(i + 1) * HARM, :])
        kb.W("a", yS, yS.n)
        kb.W("a", dv, ones_dv)
        kb.E("a", "activation", cosb, ph_sb, AF.Sin, bias=cns[0:HARM, 0:1],
             scale=1.0, inc=ac)
        kb.W("v", ac, ac.n)
        kb.E("v", "tensor_mul", wTt, am_sb, cosb, inc=dv)
        kb.W("v", dv, dv.n)
        kb.E("v", "tensor_copy", wTvo_sb[:, :], wTt[:, 2 * D:4 * D], inc=dv)
        wt_dv = dv.n

        kb.W("t", dv, wt_dv)
        kb.W("t", gS, gS.n)
        for i, dstb in ((0, wqtr), (1, wktr)):
            for h in range(NHL1):
                tb = (i * NHL1 + h) % 2
                bankw("t", tb)
                kb.E("t", "transpose",
                     PS(tb).bitcast(BF16)[0:128, 0:HARM],
                     wTt[:, i * D + h * 128:i * D + (h + 1) * 128],
                     id_bf[0:HARM, 0:HARM], inc=mm)
                kb.W("v", mm, mm.n)
                kb.W("v", dv, dv.n)
                kb.E("v", "tensor_copy", dstb[:, h * HARM:(h + 1) * HARM],
                     PS(tb).bitcast(BF16)[0:128, 0:HARM], inc=dv)
                bankg(tb, dv)
        wtr_dv = dv.n
        kb.W("t", dv, wtr_dv)
        for h in range(NHL1):
            tb = h % 2
            bankw("t", tb)
            kb.E("t", "matmul", PSW(tb, 0, HARM, HARM),
                 wqtr[:, h * HARM:(h + 1) * HARM],
                 wktr[:, h * HARM:(h + 1) * HARM],
                 start=True, stop=True, inc=mm)
            kb.W("v", mm, mm.n)
            kb.W("v", dv, dv.n)
            kb.E("v", "tensor_copy", nh_sb[:, h * HARM:(h + 1) * HARM],
                 PSW(tb, 0, HARM, HARM), inc=dv)
            bankg(tb, dv)
        nh0_dv = dv.n

        kb.W("t", yS, yS.n)
        kb.W("t", dv, dv.n)
        mml(PS(7, 1, E), [(ones_c_f[:, :], wr_sb[:, f * E:(f + 1) * E])
                          for f in range(NF)], inc=mm)
        kb.W("a", mm, mm.n)
        kb.E("a", "activation", ncs_wr[:, :], PS(7, 1, E), AF.Copy, bias=0.0,
             scale=-1.0, inc=ac)
        arena.pop()
        kb.barrier()

        arena.push()
        x_bf = [arena.alloc(NF * TT) for _ in range(2)]
        xn_bf = [arena.alloc(NF * TT) for _ in range(2)]
        wkv_sb = arena.alloc(2 * NF * HD)
        wq_sl = [arena.alloc(NF * 128) for _ in range(2)]
        sq_bf = [arena.alloc(TT) for _ in range(2)]
        mu_b = arena.alloc(TT, F32)
        r_b = arena.alloc(TT, F32)
        tmp_f = arena.alloc(TT, F32)
        stg_sb = [arena.alloc(TT) for _ in range(4)]

        wk_of, wv_of = 0, NF * HD
        dma("g", gS, wkv_sb[:, wk_of:wk_of + NF * HD], wk_t)
        dma("g", gS, wkv_sb[:, wv_of:wv_of + NF * HD], wv_t)
        g_wkv = gS.n

        def ld_x(tt):
            def em(sem):
                dma("g", sem, x_bf[S_xg.i % 2].rearrange("p (f t) -> p f t", f=NF),
                    xT[:, tt * TT:(tt + 1) * TT].rearrange("(f p) t -> p f t", p=128))
            return S_xg.load(em)

        def ld_wq(oc):
            def em(sem):
                dma("g", sem, wq_sl[S_wA.i % 2],
                    wq_t[oc * 128:(oc + 1) * 128, :])
            return S_wA.load(em)

        x_i0 = ld_x(0)
        wq_i0 = ld_wq(0)
        sq_guard = [0, 0]
        xn_guard = [[], []]

        for tt in range(NTT):
            sl = tt % 2
            xi = x_i0 + tt
            if tt + 1 < NTT:
                ld_x(tt + 1)

            S_xg.need("t", xi)
            kb.W("t", dv, ones_dv)
            bankw("t", 0)
            bankw("t", 1)
            mml(PS(0, 1, TT), [(ones_c_bf[:, :], x_bf[sl][:, f * TT:(f + 1) * TT])
                               for f in range(NF)], inc=mm)
            S_xg.need("a", xi)
            for f in range(NF):
                ssl = f % 2
                kb.W("a", mm, sq_guard[ssl])
                kb.E("a", "activation", sq_bf[ssl],
                     x_bf[sl][:, f * TT:(f + 1) * TT], AF.Square,
                     bias=0.0, scale=1.0, inc=ac)
                kb.W("t", ac, ac.n)
                kb.E("t", "matmul", PS(1, 1, TT), ones_c_bf[:, :], sq_bf[ssl],
                     start=(f == 0), stop=(f == NF - 1), inc=mm)
                sq_guard[ssl] = mm.n
            sums_mm = mm.n

            kb.W("a", mm, sums_mm)
            kb.E("a", "activation", row(0), PS(0, 1, TT), AF.Copy, bias=0.0,
                 scale=1.0 / D, inc=ac)
            kb.E("a", "activation", row(1), PS(1, 1, TT), AF.Copy, bias=0.0,
                 scale=1.0 / D, inc=ac)
            bankg(0, ac)
            bankg(1, ac)
            kb.W("a", ac, ac.n)
            kb.E("a", "activation", row(2), row(0), AF.Square, bias=0.0, scale=1.0,
                 inc=ac)
            kb.W("v", ac, ac.n)
            kb.E("v", "tensor_sub", row(3), row(1), row(2), inc=dv)
            kb.W("a", dv, dv.n)
            kb.E("a", "activation", row(4), row(3), AF.Ln, bias=cns[0:1, 1:2],
                 scale=1.0, inc=ac)
            kb.W("a", ac, ac.n)
            kb.E("a", "activation", row(5), row(4), AF.Exp, bias=0.0, scale=-0.5,
                 inc=ac)
            r_ac = ac.n

            kb.W("t", ac, r_ac)
            bankw("t", 2)
            kb.E("t", "matmul", PS(2, 128, TT), ones_r_f[:, :], row(0),
                 start=True, stop=True, inc=mm)
            bankw("t", 3)
            kb.E("t", "matmul", PS(3, 128, TT), ones_r_f[:, :], row(5),
                 start=True, stop=True, inc=mm)
            kb.W("v", mm, mm.n)
            kb.W("v", dv, dv.n)
            kb.E("v", "tensor_copy", mu_b, PS(2, 128, TT), inc=dv)
            kb.W("v", dv, dv.n)
            kb.E("v", "tensor_copy", r_b, PS(3, 128, TT), inc=dv)
            bankg(2, dv)
            bankg(3, dv)
            bc_dv = dv.n

            for s_, v_ in xn_guard[sl]:
                kb.W("v", s_, v_)
                kb.W("g", s_, v_)
            xn_guard[sl] = []
            kb.W("v", dv, bc_dv)
            kb.W("g", dv, bc_dv)
            kb.W("g", pl, pl.n)
            for f in range(NF):
                if f % 2 == 0:
                    kb.W("v", dv, dv.n)
                    kb.E("v", "tensor_sub", tmp_f,
                         x_bf[sl][:, f * TT:(f + 1) * TT], mu_b, inc=dv)
                    kb.W("v", dv, dv.n)
                    kb.E("v", "tensor_mul", xn_bf[sl][:, f * TT:(f + 1) * TT],
                         tmp_f, r_b, inc=dv)
                else:
                    kb.W("g", pl, pl.n)
                    kb.E("g", "tensor_sub", xn_bf[sl][:, f * TT:(f + 1) * TT],
                         x_bf[sl][:, f * TT:(f + 1) * TT], mu_b, inc=pl)
                    kb.W("g", pl, pl.n)
                    kb.E("g", "tensor_mul", xn_bf[sl][:, f * TT:(f + 1) * TT],
                         xn_bf[sl][:, f * TT:(f + 1) * TT], r_b, inc=pl)
            xn_dv = dv.n
            xn_pl = pl.n

            kb.W("y", pl, xn_pl)
            st_s, st_v = stg_store(
                "y",
                xn0_scr[:, tt * TT:(tt + 1) * TT].rearrange("(f p) t -> p f t",
                                                            p=128),
                xn_bf[sl].rearrange("p (f t) -> p f t", f=NF), dv, xn_dv)
            xn_guard[sl].append((st_s, st_v))

            xn_ap = lambda f, s=sl: xn_bf[s][:, f * TT:(f + 1) * TT]

            kb.W("t", dv, xn_dv)
            kb.W("t", pl, xn_pl)
            kb.W("t", gS, g_wkv)
            for mat in range(2):
                for oc in range(NH):
                    bank = 4 + (oc % 2)
                    if mat == 1:
                        qi = wq_i0 + tt * NH + oc
                        S_wA.need("t", qi)
                        if qi + 1 < wq_i0 + NTT * NH:
                            ld_wq((tt * NH + oc + 1) % NH)
                    bankw("t", bank)
                    if mat == 0:
                        pairs = [(wkv_sb[:, wk_of + f * HD + oc * 128:
                                         wk_of + f * HD + oc * 128 + 128],
                                  xn_ap(f)) for f in range(NF)]
                    else:
                        wsl = (tt * NH + oc) % 2
                        pairs = [(wq_sl[wsl][:, f * 128:(f + 1) * 128],
                                  xn_ap(f)) for f in range(NF)]
                    mml(PS(bank, 128, TT), pairs, inc=mm)
                    if mat == 1:
                        S_wA.done(qi, mm, mm.n)
                    kb.W("v", mm, mm.n)
                    ss = stg_state["i"]
                    kb.W("v", stg[ss], stg_free[ss])
                    kb.W("v", dv, dv.n)
                    kb.E("v", "tensor_copy", stg_sb[ss], PS(bank, 128, TT), inc=dv)
                    bankg(bank, dv)
                    dst = k_scr if mat == 0 else q_scr
                    stg_store("y",
                              dst[oc * 128:(oc + 1) * 128, tt * TT:(tt + 1) * TT],
                              stg_sb[ss], dv, dv.n)

            VG = min(512, HD)
            NVG = HD // VG
            for sub in range(TT // 128):
                for vg in range(NVG):
                    bank = 4 + ((sub * NVG + vg) % 2)
                    bankw("t", bank)
                    mml(PS(bank, 128, VG),
                        [(xn_ap(f)[:, sub * 128:(sub + 1) * 128],
                          wkv_sb[:, wv_of + f * HD + vg * VG:
                                 wv_of + f * HD + (vg + 1) * VG])
                         for f in range(NF)], inc=mm)
                    kb.W("v", mm, mm.n)
                    ss = stg_state["i"]
                    kb.W("v", stg[ss], stg_free[ss])
                    kb.W("v", dv, dv.n)
                    kb.E("v", "tensor_copy", stg_sb[ss][:, 0:VG],
                         PS(bank, 128, VG), inc=dv)
                    bankg(bank, dv)
                    t0 = tt * TT + sub * 128
                    stg_store("y",
                              v_scr.rearrange("(h t) d -> t h d", h=NH)[
                                  t0:t0 + 128,
                                  vg * (VG // 128):(vg + 1) * (VG // 128), :],
                              stg_sb[ss][:, 0:VG].rearrange(
                                  "p (h d) -> p h d", h=VG // 128), dv, dv.n)

            S_xg.done(xi, mm, mm.n)
            xn_guard[sl].append((mm, mm.n))

        arena.pop()
        kb.barrier()

        gdma_role(gS, xn0_own, xn0_scr.tensor, 0, [[S, D], [1, TOK]])
        precast_jobs = [(wsg_b, wsg_t), (wsu_b, wsu_t), (wsd_b, wsd_t)]
        for e_ in range(E):
            precast_jobs.append((weg_b[e_ * 128:(e_ + 1) * 128, :],
                                 weg_t[e_ * 128:(e_ + 1) * 128, :]))
            precast_jobs.append((weu_b[e_ * 128:(e_ + 1) * 128, :],
                                 weu_t[e_ * 128:(e_ + 1) * 128, :]))
            precast_jobs.append((wed_b[e_ * 128:(e_ + 1) * 128, :],
                                 wed_t[e_ * 128:(e_ + 1) * 128, :]))
        precast_jobs.reverse()

        def precast(n):
            if precast_jobs:
                kb.W("g", mm, max(0, mm.n - 400))
            for _ in range(n):
                if precast_jobs:
                    d_, s_ = precast_jobs.pop()
                    dma("g", gS, d_, s_)
        arena.push()
        cm_sb = arena.alloc(NM * TT, F32)
        kbuf = [arena.alloc(S) for _ in range(2)]
        vbuf = [arena.alloc(NKL * 128) for _ in range(2)]
        qbuf = [arena.alloc(TT) for _ in range(2)]
        pbuf = arena.alloc(NKL * TT)
        rb_sb = arena.alloc(TT, F32)
        ctx_st = [arena.alloc(TT) for _ in range(2)]

        dma("y", yS, cm_sb.rearrange("p (m t) -> p m t", m=NM),
            cmask.rearrange("(m p) t -> p m t", p=128))
        cm_v = yS.n

        def ld_kv(h):
            def emk(sem):
                dma("y", sem, kbuf[S_kb.i % 2], k_scr[h * 128:(h + 1) * 128, :])

            def emv(sem):
                dma("y", sem, vbuf[S_vb.i % 2].rearrange("p (kc d) -> p kc d",
                                                         kc=NKL),
                    v_scr.rearrange("(h t) d -> h t d", h=NH)[h, :, :].rearrange(
                        "(kc p) d -> p kc d", p=128))

            return S_kb.load(emk), S_vb.load(emv)

        def ld_q(h, qt):
            def em(sem):
                dma("y", sem, qbuf[S_qb.i % 2],
                    q_scr[h * 128:(h + 1) * 128, qt * TT:(qt + 1) * TT])

            return S_qb.load(em)

        ld_kv(0)
        ld_q(0, 0)
        exp_guard = {0: 0, 2: 0}
        p_pe_guard = 0
        ctx_guard = [None, None]

        def emit_scores(hsl, qsl, qt, g, nkc):
            k0 = 2 * g
            kcs = [k0] if k0 == nkc - 1 else [k0, k0 + 1]
            pb = (g % 2) * 2
            kb.W("t", ac, exp_guard[pb])
            s_mm = []
            for j, kc in enumerate(kcs):
                kb.E("t", "matmul", PS(pb + j, 128, TT),
                     kbuf[hsl][:, kc * 128:(kc + 1) * 128],
                     qbuf[qsl], start=True, stop=True, inc=mm)
                s_mm.append((j, kc, mm.n))
            need_dv = False
            for j, kc, v in s_mm:
                if kc >= qt * NM:
                    tgt = PS(pb + j, 128, TT)
                    kb.W("v", mm, v)
                    kb.W("v", dv, dv.n)
                    kb.E("v", "tensor_add", tgt, tgt,
                         cm_sb[:, (kc - qt * NM) * TT:
                               (kc - qt * NM + 1) * TT], inc=dv)
                    need_dv = True
            kb.W("a", mm, mm.n)
            if need_dv:
                kb.W("a", dv, dv.n)
            kb.W("a", mm, p_pe_guard)
            nk_ = len(kcs)
            src_ = psum[0:128, pb * 512:(pb + 2) * 512].rearrange(
                "p (g r) -> p g r", g=2)[:, 0:nk_, 0:TT]
            dst_ = pbuf[:, k0 * TT:(k0 + nk_) * TT].rearrange(
                "p (g t) -> p g t", g=nk_)
            kb.E("a", "activation", dst_, src_, AF.Exp, bias=0.0, scale=ISCALE,
                 inc=ac)
            exp_guard[pb] = ac.n
            return kcs

        pending2 = [None]

        def tail2(h, qt, cb, csl, av_mm):
            kb.W("v", mm, av_mm)
            kb.W("v", mm, prev2["rb_mm"])
            kb.W("v", dv, dv.n)
            kb.E("v", "reciprocal", row(6), PS(4, 1, TT), inc=dv)
            prev2["sums_rd"] = dv.n
            kb.W("t", dv, dv.n)
            kb.W("t", dv, prev2["rb_dv"])
            bankw("t", 6)
            kb.E("t", "matmul", PS(6, 128, TT), ones_r_f[:, :], row(6),
                 start=True, stop=True, inc=mm)
            prev2["rb_mm"] = mm.n
            kb.W("v", mm, mm.n)
            kb.W("v", dv, dv.n)
            kb.E("v", "tensor_copy", rb_sb, PS(6, 128, TT), inc=dv)
            bankg(6, dv)
            prev2["rb_dv"] = dv.n
            if ctx_guard[csl] is not None:
                kb.W("v", ctx_guard[csl][0], ctx_guard[csl][1])
            kb.W("v", dv, dv.n)
            kb.E("v", "tensor_mul", ctx_st[csl], PS(cb, 128, TT), rb_sb, inc=dv)
            bankg(cb, dv)
            ctx_guard[csl] = stg_store(
                "y", cc0_in[h * 128:(h + 1) * 128, qt * TT:(qt + 1) * TT],
                ctx_st[csl], dv, dv.n)

        prev2 = {"rb_mm": 0, "rb_dv": 0, "sums_rd": 0}
        for h in range(NH):
            hsl = h % 2
            if h + 1 < NH:
                ld_kv(h + 1)
            for qt in range(NTT):
                qi = h * NTT + qt
                cb = 5 if qi % 2 == 0 else 7
                if qi + 1 < NH * NTT:
                    nh_, nqt = divmod(qi + 1, NTT)
                    ld_q(nh_, nqt)
                nkc = (qt + 1) * NM
                S_kb.need("t", h)
                S_vb.need("t", h)
                S_qb.need("t", qi)
                kb.W("t", yS, cm_v)
                precast(1)
                ng = (nkc + 1) // 2
                emit_scores(hsl, qi % 2, qt, 0, nkc)
                if pending2[0] is not None:
                    pending2[0]()
                    pending2[0] = None
                first_av = True
                for g in range(ng):
                    if g + 1 < ng:
                        emit_scores(hsl, qi % 2, qt, g + 1, nkc)
                    kcs = [2 * g] if 2 * g == nkc - 1 else [2 * g, 2 * g + 1]
                    kb.W("t", ac, exp_guard[(g % 2) * 2])
                    if first_av:
                        kb.W("t", dv, prev2["sums_rd"])
                        bankw("t", cb)
                        first_av = False
                    for kc in kcs:
                        kb.E("t", "matmul", PS(4, 1, TT), ones_c_bf[:, :],
                             pbuf[:, kc * TT:(kc + 1) * TT],
                             start=(kc == 0), stop=(kc == nkc - 1),
                             inc=(mm if kc == nkc - 1 else None))
                        kb.E("t", "matmul", PS(cb, 128, TT),
                             vbuf[hsl][:, kc * 128:(kc + 1) * 128],
                             pbuf[:, kc * TT:(kc + 1) * TT],
                             start=(kc == 0), stop=(kc == nkc - 1),
                             inc=(mm if kc == nkc - 1 else None))
                av_mm = mm.n
                p_pe_guard = av_mm
                pending2[0] = (lambda h_=h, qt_=qt, cb_=cb, csl_=qi % 2,
                               av_=av_mm: tail2(h_, qt_, cb_, csl_, av_))
                S_qb.done(qi, mm, av_mm)
            S_kb.done(h, mm, mm.n)
            S_vb.done(h, mm, mm.n)
        if pending2[0] is not None:
            pending2[0]()
            pending2[0] = None

        for s in stg:
            kb.W("g", s, s.n)

        def fire_ag1(eng):
            eng.collective_compute(
                "AllGather", ALU.bypass,
                replica_groups=[[0, 1], [2, 3], [4, 5], [6, 7]],
                ins=[cc0_in.opt()], outs=[cc0_out.opt()],
            ).then_inc(ccs.h, 1)

        ccs.n += 1
        kb.q["g"].append(fire_ag1)
        arena.pop()
        kb.barrier(exclude=("ccs",))

        arena.push()
        xn_t = [arena.alloc(NF * TT) for _ in range(2)]
        OG = min(128, INTER)
        NOG = INTER // OG
        wg_sl = [arena.alloc(NF * OG) for _ in range(2)]
        wu_sl = [arena.alloc(NF * OG) for _ in range(2)]
        NICH = NIC // 2
        wd_sl = [arena.alloc(NICH * 128) for _ in range(2)]
        fuse = arena.alloc(NIC * TT)
        silu_st = [arena.alloc(TT, F32) for _ in range(2)]
        xo_t = [arena.alloc(TT, F32) for _ in range(2)]
        rst_sb = [arena.alloc(TT, F32) for _ in range(2)]
        rst_guard = [None, None]

        def ld_xn(tt):
            def em(sem):
                gdma_role(sem, xn_t[S_xg.i % 2].rearrange("p (f t) -> p f t", f=NF),
                          xn0_scr.tensor, tt * TT,
                          [[S, 128], [S * 128, NF], [1, TT]])
            return S_xg.load(em)

        def ld_wgu(og):
            def emg(sem):
                dma("g", sem, wg_sl[S_wA.i % 2],
                    fg8t[og * 128:(og + 1) * 128, :])

            def emu(sem):
                dma("g", sem, wu_sl[S_wB.i % 2],
                    fu8t[og * 128:(og + 1) * 128, :])

            return S_wA.load(emg), S_wB.load(emu)

        def ld_wd(fo, half):
            def em(sem):
                dma("g", sem, wd_sl[S_wC.i % 2],
                    fd8t[fo * 128:(fo + 1) * 128,
                         half * NICH * 128:(half + 1) * NICH * 128])
            return S_wC.load(em)

        def ld_xo(tt, fo):
            def em(sem):
                dma("g", sem, xo_t[S_xr.i % 2],
                    xoT[fo * 128:(fo + 1) * 128, tt * TT:(tt + 1) * TT])
            return S_xr.load(em)

        xn_i0 = ld_xn(0)
        wgu_idx = [ld_wgu(0)]
        wd_i0 = ld_wd(0, 0)
        ld_wd(0, 1)
        xo_i0 = ld_xo(0, 0)
        ctx_ex = {}
        silu_g = {4 + 0: 0, 4 + 1: 0, 0: 0, 1: 0}
        fuse_g = {2: 0, 3: 0, "s": 0}
        fuse_pe = 0

        for tt in range(NTTO):
            if tt == min(1, NTTO - 1) and "v" not in ctx_ex:
                kb.W("g", ccs, 1)
                gdma_role(gS, ctx_own, cc0_out.tensor, 0,
                          [[S, 2 * HD], [1, TOK]])
                ctx_ex["v"] = gS.n
            precast(-(-len(precast_jobs) // max(1, NTTO - tt)))
            xi = xn_i0 + tt
            if tt + 1 < NTTO:
                ld_xn(tt + 1)
            S_xg.need("t", xi)
            xnf = lambda f, s=tt % 2: xn_t[s][:, f * TT:(f + 1) * TT]

            for og in range(NOG):
                gidx0 = tt * NOG + og
                gi, gbi = wgu_idx[gidx0]
                S_wA.need("t", gi)
                S_wB.need("t", gbi)
                if gidx0 + 1 < NTTO * NOG:
                    wgu_idx.append(ld_wgu((gidx0 + 1) % NOG))
                for oc in range(OG // 128):
                    gc = og * (OG // 128) + oc
                    gb = oc % 2
                    ub = 2 + oc % 2
                    kb.W("t", ac, silu_g[gb])
                    kb.W("t", dv, fuse_g[ub])
                    wsl = gi % 2
                    usl = gbi % 2
                    mml(PS(gb, 128, TT),
                        [(wg_sl[wsl][:, f * OG + oc * 128:f * OG + (oc + 1) * 128],
                          xnf(f)) for f in range(NF)], inc=mm)
                    g_mm = mm.n
                    mml(PS(ub, 128, TT),
                        [(wu_sl[usl][:, f * OG + oc * 128:f * OG + (oc + 1) * 128],
                          xnf(f)) for f in range(NF)], inc=mm)
                    u_mm = mm.n
                    ssl = gc % 2
                    kb.W("a", mm, g_mm)
                    kb.W("a", dv, fuse_g.get(("s", ssl), 0))
                    kb.E("a", "activation", silu_st[ssl], PS(gb, 128, TT),
                         AF.Silu, bias=0.0, scale=sg_sb[:, gc:gc + 1], inc=ac)
                    silu_g[gb] = ac.n
                    kb.W("v", mm, u_mm)
                    kb.W("v", ac, ac.n)
                    kb.W("v", mm, fuse_pe)
                    kb.W("v", dv, dv.n)
                    kb.E("v", "scalar_tensor_tensor",
                         fuse[:, gc * TT:(gc + 1) * TT], PS(ub, 128, TT),
                         su_sb[:, gc:gc + 1], silu_st[ssl], ALU.mult, ALU.mult,
                         inc=dv)
                    fuse_g[ub] = dv.n
                    fuse_g[("s", ssl)] = dv.n
                S_wA.done(gi, mm, mm.n)
                S_wB.done(gbi, mm, mm.n)
            fuse_dv = dv.n

            for fo in range(NF):
                di0 = tt * NF + fo
                dia = wd_i0 + 2 * di0
                dib = dia + 1
                S_xr.need("v", xo_i0 + di0)
                if di0 + 1 < NTTO * NF:
                    ld_xo(*divmod(di0 + 1, NF))
                db = 4 + (fo % 2)
                kb.W("t", dv, fuse_dv)
                bankw("t", db)
                S_wC.need("t", dia)
                for ic in range(NICH):
                    kb.E("t", "matmul", PS(db, 128, TT),
                         wd_sl[dia % 2][:, ic * 128:(ic + 1) * 128],
                         fuse[:, ic * TT:(ic + 1) * TT],
                         start=(ic == 0), stop=False,
                         inc=(mm if ic == NICH - 1 else None))
                S_wC.done(dia, mm, mm.n)
                if di0 + 1 < NTTO * NF:
                    ld_wd((di0 + 1) % NF, 0)
                S_wC.need("t", dib)
                for ic in range(NICH):
                    gic = NICH + ic
                    kb.E("t", "matmul", PS(db, 128, TT),
                         wd_sl[dib % 2][:, ic * 128:(ic + 1) * 128],
                         fuse[:, gic * TT:(gic + 1) * TT],
                         start=False, stop=(ic == NICH - 1),
                         inc=(mm if ic == NICH - 1 else None))
                S_wC.done(dib, mm, mm.n)
                if di0 + 1 < NTTO * NF:
                    ld_wd((di0 + 1) % NF, 1)
                kb.W("v", mm, mm.n)
                rsl = di0 % 2
                if rst_guard[rsl] is not None:
                    kb.W("v", rst_guard[rsl][0], rst_guard[rsl][1])
                kb.W("v", dv, dv.n)
                kb.E("v", "scalar_tensor_tensor", rst_sb[rsl], PS(db, 128, TT),
                     sd_sb[:, fo:fo + 1], xo_t[(xo_i0 + di0) % 2],
                     ALU.mult, ALU.add, inc=dv)
                bankg(db, dv)
                S_xr.done(xo_i0 + di0, dv, dv.n)
                rst_guard[rsl] = stg_store(
                    "y", res_scr[fo * 128:(fo + 1) * 128, tt * TT:(tt + 1) * TT],
                    rst_sb[rsl], dv, dv.n)
            fuse_pe = mm.n
            S_xg.done(xi, mm, mm.n)

        arena.pop()
        kb.barrier(exclude=("ccs",))

        for qn in KB.QUEUES:
            kb.W(qn, ccs, 1)
        arena.push()
        ctx_t = [arena.alloc(NF * TT) for _ in range(2)]
        wo_sl = [arena.alloc(NF * 128) for _ in range(2)]
        x1_sb = arena.alloc(NF * TT, F32)
        xn1_sb = arena.alloc(NF * TT)
        res_t = [arena.alloc(TT, F32) for _ in range(2)]
        mu4 = arena.alloc(TT, F32)
        r4 = arena.alloc(TT, F32)
        tmp4 = arena.alloc(TT, F32)
        tmp4b = arena.alloc(TT, F32)
        tmp4g = arena.alloc(TT, F32)
        rcol = arena.alloc(4, F32)
        lg_sb = arena.alloc(E, F32)
        pr_sb = arena.alloc(E, F32)
        m1c = arena.alloc(8, F32)
        tmp8 = arena.alloc(E, F32)
        comb8 = arena.alloc(TOK, F32, parts=E)
        stg4 = [arena.alloc(2 * HARM) for _ in range(4)]

        def ld_ctx(tt):
            def em(sem):
                gdma_role(sem, ctx_t[S_xg.i % 2].rearrange("p (f t) -> p f t", f=NF),
                          cc0_out.tensor, tt * TT,
                          [[S, 128], [S * 128, NF], [1, TT]])
            return S_xg.load(em)

        def ld_wo(oc):
            def em(sem):
                dma("g", sem, wo_sl[S_wA.i % 2],
                    wo_t[oc * 128:(oc + 1) * 128, :])
            return S_wA.load(em)

        def ld_res(tt, fo):
            def em(sem):
                dma("y", sem, res_t[S_sL.i % 2],
                    res_scr[fo * 128:(fo + 1) * 128, tt * TT:(tt + 1) * TT])
            return S_sL.load(em)

        ctx_i0 = ld_ctx(0)
        wo_i0 = ld_wo(0)
        res_i0 = ld_res(0, 0)
        x1_consumed = 0
        x1_pe = 0
        x1_store_g = []
        xn1_pe = 0

        for tt in range(NTTO):
            ci = ctx_i0 + tt
            if tt + 1 < NTTO:
                ld_ctx(tt + 1)
            S_xg.need("t", ci)
            ctxf = lambda f, s=tt % 2: ctx_t[s][:, f * TT:(f + 1) * TT]

            for oc in range(NF):
                oi = tt * NF + oc
                S_wA.need("t", wo_i0 + oi)
                S_sL.need("v", res_i0 + oi)
                if oi + 1 < NTTO * NF:
                    ld_wo((oi + 1) % NF)
                    ld_res(*divmod(oi + 1, NF))
                ob = oc % 2
                bankw("t", ob)
                wsl = (wo_i0 + oi) % 2
                mml(PS(ob, 128, TT),
                    [(wo_sl[wsl][:, f * 128:(f + 1) * 128], ctxf(f))
                     for f in range(NF)], inc=mm)
                S_wA.done(wo_i0 + oi, mm, mm.n)
                kb.W("v", mm, mm.n)
                kb.W("v", dv, x1_consumed)
                kb.W("v", mm, x1_pe)
                kb.W("v", mm, xn1_pe)
                for s_, v_ in x1_store_g:
                    kb.W("v", s_, v_)
                kb.W("v", dv, dv.n)
                kb.E("v", "tensor_add", x1_sb[:, oc * TT:(oc + 1) * TT],
                     PS(ob, 128, TT), res_t[(res_i0 + oi) % 2], inc=dv)
                bankg(ob, dv)
                S_sL.done(res_i0 + oi, dv, dv.n)
            x1_dv = dv.n
            S_xg.done(ci, mm, mm.n)

            kb.W("t", dv, x1_dv)
            bankw("t", 2)
            bankw("t", 3)
            mml(PS(2, 1, TT), [(ones_c_f[:, :], x1_sb[:, f * TT:(f + 1) * TT])
                               for f in range(NF)], inc=mm)
            sq_pe4 = [0, 0]
            for f in range(NF):
                tb4 = tmp4 if f % 2 == 0 else tmp4b
                kb.W("a", dv, x1_dv)
                kb.W("a", mm, sq_pe4[f % 2])
                kb.E("a", "activation", tb4, x1_sb[:, f * TT:(f + 1) * TT],
                     AF.Square, bias=0.0, scale=1.0, inc=ac)
                kb.W("t", ac, ac.n)
                kb.E("t", "matmul", PS(3, 1, TT), ones_c_f[:, :], tb4,
                     start=(f == 0), stop=(f == NF - 1), inc=mm)
                sq_pe4[f % 2] = mm.n
            sums4 = mm.n
            kb.W("a", mm, sums4)
            kb.E("a", "activation", row(0), PS(2, 1, TT), AF.Copy, bias=0.0,
                 scale=1.0 / D, inc=ac)
            kb.E("a", "activation", row(1), PS(3, 1, TT), AF.Copy, bias=0.0,
                 scale=1.0 / D, inc=ac)
            bankg(2, ac)
            bankg(3, ac)
            kb.W("a", ac, ac.n)
            kb.E("a", "activation", row(2), row(0), AF.Square, bias=0.0,
                 scale=1.0, inc=ac)
            kb.W("v", ac, ac.n)
            kb.W("v", dv, dv.n)
            kb.E("v", "tensor_sub", row(3), row(1), row(2), inc=dv)
            kb.W("a", dv, dv.n)
            kb.E("a", "activation", row(4), row(3), AF.Ln, bias=cns[0:1, 1:2],
                 scale=1.0, inc=ac)
            kb.W("a", ac, ac.n)
            kb.E("a", "activation", row(5), row(4), AF.Exp, bias=0.0, scale=-0.5,
                 inc=ac)
            r4_ac = ac.n
            kb.W("v", ac, r4_ac)
            kb.W("v", dv, dv.n)
            kb.E("v", "tensor_mul", row(6), row(0), row(5), inc=dv)
            mr_dv = dv.n

            kb.W("t", ac, r4_ac)
            bankw("t", 4)
            kb.E("t", "matmul", PS(4, 128, TT), ones_r_f[:, :], row(0),
                 start=True, stop=True, inc=mm)
            bankw("t", 5)
            kb.E("t", "matmul", PS(5, 128, TT), ones_r_f[:, :], row(5),
                 start=True, stop=True, inc=mm)
            kb.W("v", mm, mm.n)
            kb.W("v", dv, dv.n)
            kb.E("v", "tensor_copy", mu4, PS(4, 128, TT), inc=dv)
            kb.W("v", dv, dv.n)
            kb.E("v", "tensor_copy", r4, PS(5, 128, TT), inc=dv)
            bankg(4, dv)
            bankg(5, dv)
            bc4 = dv.n

            kb.W("v", dv, bc4)
            kb.W("g", dv, bc4)
            kb.W("g", pl, pl.n)
            for f in range(NF):
                if f % 2 == 0:
                    kb.W("v", dv, dv.n)
                    kb.E("v", "tensor_sub", tmp4,
                         x1_sb[:, f * TT:(f + 1) * TT], mu4, inc=dv)
                    kb.W("v", dv, dv.n)
                    kb.E("v", "tensor_mul", xn1_sb[:, f * TT:(f + 1) * TT],
                         tmp4, r4, inc=dv)
                else:
                    kb.W("g", pl, pl.n)
                    kb.E("g", "tensor_sub", tmp4g,
                         x1_sb[:, f * TT:(f + 1) * TT], mu4, inc=pl)
                    kb.W("g", pl, pl.n)
                    kb.E("g", "tensor_mul", xn1_sb[:, f * TT:(f + 1) * TT],
                         tmp4g, r4, inc=pl)
            xn1_dv = dv.n
            xn1_pl = pl.n

            kb.W("y", pl, xn1_pl)
            x1_store_g = [
                stg_store("y",
                          x1_scr[:, tt * TT:(tt + 1) * TT].rearrange(
                              "(f p) t -> p f t", p=128),
                          x1_sb.rearrange("p (f t) -> p f t", f=NF), dv, xn1_dv),
                stg_store("y",
                          xn1_scr[:, tt * TT:(tt + 1) * TT].rearrange(
                              "(f p) t -> p f t", p=128),
                          xn1_sb.rearrange("p (f t) -> p f t", f=NF), dv, xn1_dv),
            ]

            kb.W("t", dv, xn1_dv)
            kb.W("t", pl, xn1_pl)
            bankw("t", 6)
            mml(PSW(6, 0, HARM, TT),
                [(b4v[:, f, 0, :], xn1_sb[:, f * TT:(f + 1) * TT])
                 for f in range(NF)], inc=mm)
            kb.W("v", mm, mm.n)
            kb.W("v", dv, dv.n)
            kb.E("v", "tensor_copy", resq_sb[:, tt * TT:(tt + 1) * TT],
                 PSW(6, 0, HARM, TT), inc=dv)
            bankg(6, dv)

            for sub in range(TT // 128):
                rb_ = sub % 2
                bankw("t", rb_)
                mml(PS(rb_, 128, 2 * HARM),
                    [(xn1_sb[:, f * TT + sub * 128:f * TT + (sub + 1) * 128],
                      bkv_sb[:, f * 2 * HARM:(f + 1) * 2 * HARM])
                     for f in range(NF)], inc=mm)
                kb.W("v", mm, mm.n)
                ss = stg_state["i"]
                kb.W("v", stg[ss], stg_free[ss])
                kb.W("v", dv, dv.n)
                kb.E("v", "tensor_copy", stg4[ss], PS(rb_, 128, 2 * HARM), inc=dv)
                bankg(rb_, dv)
                stg_store("y",
                          cc1_in[tt * TT + sub * 128:tt * TT + (sub + 1) * 128, :],
                          stg4[ss], dv, dv.n)

            for sub in range(TT // 128):
                lb = 6 if sub % 2 == 0 else 7
                tb = 7 if lb == 6 else 6
                bankw("t", lb)
                kb.W("t", dv, mr_dv)
                for f in range(NF):
                    kb.E("t", "matmul", PS(lb, 128, E),
                         x1_sb[:, f * TT + sub * 128:f * TT + (sub + 1) * 128],
                         wr_sb[:, f * E:(f + 1) * E],
                         start=(f == 0), stop=False)
                kb.E("t", "matmul", PS(lb, 128, E),
                     row(6)[:, sub * 128:(sub + 1) * 128], ncs_wr[:, :],
                     start=False, stop=True, inc=mm)
                lg_mm = mm.n
                bankw("t", tb)
                kb.E("t", "matmul", PSW(tb, 0, 128, 1),
                     row(5)[:, sub * 128:(sub + 1) * 128], ones_c_f[0:1, 0:1],
                     start=True, stop=True, inc=mm)
                kb.W("v", mm, mm.n)
                kb.W("v", dv, dv.n)
                kb.E("v", "tensor_copy", rcol[:, 0:1], PSW(tb, 0, 128, 1), inc=dv)
                bankg(tb, dv)
                kb.W("v", mm, lg_mm)
                kb.W("v", dv, dv.n)
                kb.E("v", "tensor_scalar", lg_sb, PS(lb, 128, E), rcol[:, 0:1],
                     None, ALU.mult, inc=dv)
                bankg(lb, dv)
                kb.W("v", dv, dv.n)
                kb.E("v", "reduce_max", m1c[:, 0:1], lg_sb, axis=AX, negate=True,
                     inc=dv)
                kb.W("a", dv, dv.n)
                kb.E("a", "activation", pr_sb, lg_sb, AF.Exp,
                     bias=m1c[:, 0:1], scale=1.0, accum_out=m1c[:, 1:2], inc=ac)
                kb.W("v", ac, ac.n)
                kb.E("v", "reciprocal", m1c[:, 2:3], m1c[:, 1:2], inc=dv)
                kb.W("v", dv, dv.n)
                kb.E("v", "tensor_scalar", pr_sb, pr_sb, m1c[:, 2:3], None,
                     ALU.mult, inc=dv)
                kb.W("v", dv, dv.n)
                kb.E("v", "reduce_max", m1c[:, 3:4], pr_sb, axis=AX, inc=dv)
                kb.W("v", dv, dv.n)
                kb.E("v", "tensor_scalar", tmp8, pr_sb, m1c[:, 3:4], None,
                     ALU.is_ge, inc=dv)
                kb.W("v", dv, dv.n)
                kb.E("v", "scalar_tensor_tensor", tmp8, tmp8, -1e9, pr_sb,
                     ALU.mult, ALU.add, inc=dv)
                kb.W("v", dv, dv.n)
                kb.E("v", "reduce_max", m1c[:, 4:5], tmp8, axis=AX, inc=dv)
                kb.W("v", dv, dv.n)
                kb.E("v", "tensor_scalar", tmp8, pr_sb, m1c[:, 4:5], None,
                     ALU.is_ge, inc=dv)
                kb.W("v", dv, dv.n)
                kb.E("v", "tensor_mul", lg_sb, pr_sb, tmp8, inc=dv)
                comb_dv = dv.n
                kb.W("t", dv, comb_dv)
                bankw("t", tb)
                kb.E("t", "transpose", PSW(tb, 0, E, 128), lg_sb, id_f32[:, :],
                     inc=mm)
                kb.W("v", mm, mm.n)
                kb.W("v", dv, dv.n)
                kb.E("v", "tensor_copy",
                     comb8[:, tt * TT + sub * 128:tt * TT + (sub + 1) * 128],
                     PSW(tb, 0, E, 128), inc=dv)
                bankg(tb, dv)
            x1_consumed = dv.n
            x1_pe = mm.n
            xn1_pe = mm.n

        stg_store("y", comb_scr, comb8[:, :], dv, dv.n)
        for s in stg:
            kb.W("g", s, s.n)

        def fire_ag2(eng):
            eng.collective_compute(
                "AllGather", ALU.bypass,
                replica_groups=[[0, 1], [2, 3], [4, 5], [6, 7]],
                ins=[cc1_in.opt()], outs=[cc1_out.opt()],
            ).then_inc(ccs.h, 1)

        ccs.n += 1
        kb.q["g"].append(fire_ag2)
        arena.pop()
        kb.barrier(exclude=("ccs",))

        for qn in KB.QUEUES:
            kb.W(qn, ccs, 2)
        arena.push()
        rtmp = arena.alloc(NKL * HARM)
        sqf = arena.alloc(TT, F32, parts=HARM)

        def ld_rtmp(sem):
            dma("y", sem, rtmp.rearrange("p (kc h) -> p kc h", kc=NKL),
                cc1_out[:, 0:HARM].rearrange("(kc p) h -> p kc h", p=128))

        def ld_rva(sem):
            dma("y", sem,
                rva_sb.rearrange("p (kc x) -> p kc x", x=RVW)[:, :, 0:HARM],
                cc1_out[:, HARM:2 * HARM].rearrange("(kc p) h -> p kc h", p=128))

        rt_i = S_sL.load(ld_rtmp)
        rv_i = S_sL.load(ld_rva)
        S_sL.need("v", rt_i)
        S_sL.need("v", rv_i)
        kb.E("v", "memset",
             rva_sb.rearrange("p (kc x) -> p kc x", x=RVW)[:, :,
                                                          HARM:HARM + 1],
             1.0, inc=dv)
        kb.W("v", dv, dv.n)
        kb.E("v", "memset", rka_sb[HARM:HARM + 1, :], 1.0, inc=dv)
        rva_dv = dv.n
        S_sL.need("t", rt_i)
        S_sL.need("t", rv_i)
        for kc in range(NKL):
            tb = kc % 2
            bankw("t", tb)
            kb.E("t", "transpose",
                 PS(tb).bitcast(BF16)[0:HARM, 0:128],
                 rtmp[:, kc * HARM:(kc + 1) * HARM], id_bf[:, :], inc=mm)
            kb.W("v", mm, mm.n)
            kb.W("v", dv, dv.n)
            kb.E("v", "tensor_copy", rka_sb[0:HARM, kc * 128:(kc + 1) * 128],
                 PS(tb).bitcast(BF16)[0:HARM, 0:128], inc=dv)
            bankg(tb, dv)
        rkf_dv = dv.n
        S_sL.done(rt_i, mm, mm.n)
        S_sL.done(rv_i, dv, dv.n)

        CH = TT
        kb.W("t", dv, rkf_dv)
        for sc_ in range(S // CH):
            sl_ = rka_sb[0:HARM, sc_ * CH:(sc_ + 1) * CH]
            bankw("t", 4)
            kb.W("t", dv, dv.n)
            kb.E("t", "matmul", PSW(4, 0, 1, CH), ones_c_bf[0:HARM, 0:1], sl_,
                 start=True, stop=True, inc=mm)
            kb.W("a", mm, mm.n)
            kb.W("a", ac, ac.n)
            kb.E("a", "activation", sqf[:, 0:CH], sl_, AF.Square, bias=0.0,
                 scale=1.0, inc=ac)
            kb.W("t", ac, ac.n)
            bankw("t", 5)
            kb.E("t", "matmul", PSW(5, 0, 1, CH), ones_c_f[0:HARM, 0:1],
                 sqf[:, 0:CH], start=True, stop=True, inc=mm)
            kb.W("a", mm, mm.n)
            kb.E("a", "activation", row(0)[:, 0:CH], PSW(4, 0, 1, CH), AF.Copy,
                 bias=0.0, scale=1.0, inc=ac)
            bankg(4, ac)
            kb.W("a", ac, ac.n)
            kb.E("a", "activation", row(1)[:, 0:CH], row(0)[:, 0:CH], AF.Square,
                 bias=0.0, scale=1.0, inc=ac)
            kb.W("v", ac, ac.n)
            kb.W("v", mm, mm.n)
            kb.W("v", dv, dv.n)
            kb.E("v", "scalar_tensor_tensor", row(1)[:, 0:CH], row(1)[:, 0:CH],
                 -1.0 / 64.0, PSW(5, 0, 1, CH), ALU.mult, ALU.add, inc=dv)
            bankg(5, dv)
            kb.W("v", dv, dv.n)
            kb.E("v", "reduce_max", bstat[:, 3:4], row(1)[:, 0:CH], axis=AX,
                 inc=dv)
            kb.W("v", dv, dv.n)
            kb.E("v", "reduce_max", bstat[:, 4:5], row(0)[:, 0:CH], axis=AX,
                 inc=dv)
            kb.W("v", dv, dv.n)
            kb.E("v", "tensor_scalar", row(1)[:, 0:CH], row(0)[:, 0:CH], -1.0,
                 None, ALU.mult, inc=dv)
            kb.W("v", dv, dv.n)
            kb.E("v", "reduce_max", bstat[:, 6:7], row(1)[:, 0:CH], axis=AX,
                 inc=dv)
            kb.W("v", dv, dv.n)
            if sc_ == 0:
                kb.E("v", "tensor_copy", bstat[:, 0:1], bstat[:, 4:5], inc=dv)
                kb.W("v", dv, dv.n)
                kb.E("v", "tensor_copy", bstat[:, 1:2], bstat[:, 6:7], inc=dv)
                kb.W("v", dv, dv.n)
                kb.E("v", "tensor_copy", bstat[:, 2:3], bstat[:, 3:4], inc=dv)
            else:
                kb.E("v", "tensor_max", bstat[:, 0:1], bstat[:, 0:1],
                     bstat[:, 4:5], inc=dv)
                kb.W("v", dv, dv.n)
                kb.E("v", "tensor_max", bstat[:, 1:2], bstat[:, 1:2],
                     bstat[:, 6:7], inc=dv)
                kb.W("v", dv, dv.n)
                kb.E("v", "tensor_max", bstat[:, 2:3], bstat[:, 2:3],
                     bstat[:, 3:4], inc=dv)
            kb.W("v", dv, dv.n)
        kb.E("v", "tensor_scalar", bstat[:, 0:1], bstat[:, 0:1], 1.0 / 64.0,
             None, ALU.mult, inc=dv)
        kb.W("v", dv, dv.n)
        kb.E("v", "tensor_scalar", bstat[:, 1:2], bstat[:, 1:2], -1.0 / 64.0,
             None, ALU.mult, inc=dv)
        ch_dv = dv.n
        arena.pop()
        kb.barrier()


        for pss in range(PASSES):
            q0 = pss * TOKP
            GW = min(256, DE, DS)
            arena.push()
            res2 = arena.alloc(NF * TOKP, F32)
            xn1_t = arena.alloc(NF * TOKP)
            tmpa2 = [arena.alloc(TT)[0:HARM + 1, :] for _ in range(2)]
            sq5 = arena.alloc(TT, F32, parts=HARM)
            ubuf = arena.alloc(max(NKL, NDS, NDE) * TT)
            pb5 = ubuf[:, 0:NKL * TT]
            fuse5 = ubuf[:, 0:max(NDS, NDE) * TT]
            ctx64 = arena.alloc(TT)[0:64, :]
            outh = arena.alloc(TT)
            ores = [arena.alloc(TT) for _ in range(NTP)]
            rb5 = arena.alloc(TT, F32)
            x1t5 = [arena.alloc(TT, F32) for _ in range(2)]
            silu5 = [arena.alloc(TT, F32) for _ in range(2)]
            silu5b = arena.alloc(TT, F32)
            cbb = arena.alloc(TT, F32)
            crow2 = [arena.alloc(TT, F32, parts=1) for _ in range(2)]
            wgs = [arena.alloc(NF * GW) for _ in range(2)]
            wus = [arena.alloc(NF * GW) for _ in range(2)]
            wds = [arena.alloc(max(NDS, NDE) * 128) for _ in range(2)]

            exp_g5 = {0: 0, 2: 0}
            p5_pe = 0
            tmpa_pe = [0, 0]
            tmpa_done = [0, 0]
            idxs = [(h, tp) for h in range(NHL1) for tp in range(NTP)]

            kb.W("t", dv, rva_dv)
            kb.W("t", dv, ch_dv)

            def emit_tmpbound(i):
                h, tp = idxs[i]
                sl = i % 2
                bankw("t", 7)
                kb.W("v", mm, tmpa_pe[sl])
                kb.E("t", "matmul", PSW(7, 0, HARM, TT),
                     nh_sb[:, h * HARM:(h + 1) * HARM],
                     resq_sb[:, q0 + tp * TT:q0 + (tp + 1) * TT],
                     start=True, stop=True, inc=mm)
                kb.W("v", mm, mm.n)
                kb.W("v", dv, dv.n)
                kb.E("v", "tensor_copy", tmpa2[sl][0:HARM, :],
                     PSW(7, 0, HARM, TT), inc=dv)
                bankg(7, dv)
                tmp_dv = dv.n
                kb.W("t", dv, tmp_dv)
                bankw("t", 7)
                kb.E("t", "matmul", PSW(7, 0, 1, TT), ones_c_bf[0:HARM, 0:1],
                     tmpa2[sl][0:HARM, :], start=True, stop=True, inc=mm)
                w_mm = mm.n
                kb.W("a", dv, tmp_dv)
                kb.W("a", ac, ac.n)
                kb.E("a", "activation", sq5, tmpa2[sl][0:HARM, :], AF.Square,
                     bias=0.0, scale=1.0, inc=ac)
                kb.W("a", mm, w_mm)
                kb.E("a", "activation", row(0), PSW(7, 0, 1, TT), AF.Copy,
                     bias=0.0, scale=1.0, inc=ac)
                w_ac = ac.n
                kb.W("t", ac, w_ac)
                kb.E("t", "matmul", PSW(7, 0, 1, TT), ones_c_f[0:HARM, 0:1],
                     sq5, start=True, stop=True, inc=mm)
                t2_mm = mm.n
                kb.W("a", mm, t2_mm)
                kb.W("a", ac, ac.n)
                kb.E("a", "activation", row(1), PSW(7, 0, 1, TT), AF.Copy,
                     bias=0.0, scale=1.0, inc=ac)
                bankg(7, ac)
                rows_ac = ac.n
                kb.W("v", ac, rows_ac)
                kb.W("v", mm, prev5["rbmm5"])
                kb.W("v", dv, dv.n)
                kb.E("v", "tensor_scalar", row(3), row(0), bstat[:, 0:1], None,
                     ALU.mult, inc=dv)
                kb.W("v", dv, dv.n)
                kb.E("v", "tensor_scalar", row(2), row(0), bstat[:, 1:2], None,
                     ALU.mult, inc=dv)
                kb.W("v", dv, dv.n)
                kb.E("v", "tensor_max", row(3), row(3), row(2), inc=dv)
                kb.W("a", dv, dv.n)
                kb.E("a", "activation", row(2), row(0), AF.Square, bias=0.0,
                     scale=0.125, inc=ac)
                kb.W("v", ac, ac.n)
                kb.W("v", dv, dv.n)
                kb.E("v", "tensor_sub", row(2), row(1), row(2), inc=dv)
                kb.W("v", dv, dv.n)
                kb.E("v", "tensor_scalar_max", row(2), row(2), 0.0, inc=dv)
                kb.W("v", dv, dv.n)
                kb.E("v", "tensor_scalar", row(2), row(2), bstat[:, 2:3], None,
                     ALU.mult, inc=dv)
                kb.W("a", dv, dv.n)
                kb.E("a", "activation", row(2), row(2), AF.Sqrt, bias=0.0,
                     scale=1.0, inc=ac)
                kb.W("v", ac, ac.n)
                kb.W("v", dv, dv.n)
                kb.E("v", "tensor_add", row(3), row(3), row(2), inc=dv)
                kb.W("v", dv, dv.n)
                kb.E("v", "tensor_scalar", row(3), row(3), -1.0, -2.0,
                     ALU.mult, ALU.add, inc=dv)
                kb.W("v", dv, dv.n)
                kb.E("v", "tensor_copy", tmpa2[sl][HARM:HARM + 1, :], row(3),
                     inc=dv)
                tmpa_done[sl] = dv.n

            def emit_scores5(i, g, ng):
                sl = i % 2
                k0 = 2 * g
                kcs = [k0] if k0 == NKL - 1 else [k0, k0 + 1]
                pb_ = (g % 2) * 2
                kb.W("t", ac, exp_g5[pb_])
                if g == 0:
                    kb.W("t", dv, tmpa_done[sl])
                for j, kc in enumerate(kcs):
                    kb.E("t", "matmul", PS(pb_ + j, 128, TT),
                         rka_sb[:, kc * 128:(kc + 1) * 128],
                         tmpa2[sl], start=True, stop=True, inc=mm)
                kb.W("a", mm, mm.n)
                kb.W("a", mm, p5_pe)
                nk_ = len(kcs)
                src_ = psum[0:128, pb_ * 512:(pb_ + 2) * 512].rearrange(
                    "p (g r) -> p g r", g=2)[:, 0:nk_, 0:TT]
                dst_ = pb5[:, k0 * TT:(k0 + nk_) * TT].rearrange(
                    "p (g t) -> p g t", g=nk_)
                kb.E("a", "activation", dst_, src_, AF.Exp, bias=0.0,
                     scale=ISCALE, inc=ac)
                exp_g5[pb_] = ac.n
                return kcs

            ng5 = (NKL + 1) // 2
            pending5 = [None]
            prev5 = {"rb5": 0, "rbmm5": 0, "outh_pe": 0}
            emit_tmpbound(0)

            def tail5(i, h, tp, ctx_mm):
                kb.W("v", mm, ctx_mm)
                kb.W("v", dv, dv.n)
                kb.E("v", "tensor_copy", ctx64, PSW(4, 0, HARM, TT), inc=dv)
                kb.W("v", dv, dv.n)
                kb.E("v", "reciprocal", row(7), PSW(4, HARM, HARM + 1, TT),
                     inc=dv)
                ctx_dv = dv.n
                kb.W("t", dv, ctx_dv)
                kb.W("t", dv, prev5["rb5"])
                kb.E("t", "matmul", PS(4, 128, TT), ones_r_f[:, :], row(7),
                     start=True, stop=True, inc=mm)
                rbmm = mm.n
                prev5["rbmm5"] = rbmm
                bankw("t", 5)
                kb.E("t", "matmul", PS(5, 128, TT),
                     wTvo_sb[:, h * 128:(h + 1) * 128],
                     ctx64, start=True, stop=True, inc=mm)
                oh_mm = mm.n
                kb.W("v", mm, rbmm)
                kb.W("v", dv, dv.n)
                kb.E("v", "tensor_copy", rb5, PS(4, 128, TT), inc=dv)
                bankg(4, dv)
                prev5["rb5"] = dv.n
                kb.W("v", mm, oh_mm)
                kb.W("v", mm, prev5.get("outh_pe", 0))
                kb.W("v", dv, dv.n)
                kb.E("v", "tensor_mul", outh, PS(5, 128, TT), rb5, inc=dv)
                bankg(5, dv)
                outh_dv = dv.n
                kb.W("t", dv, outh_dv)
                if h == 0:
                    bankw("t", 6)
                kb.E("t", "matmul", PSW(6, 0, HARM, TT),
                     b4v[:, h, 1, :], outh,
                     start=(h == 0), stop=(h == NHL1 - 1), inc=mm)
                prev5["outh_pe"] = mm.n

            for i, (h, tp) in enumerate(idxs):
                emit_scores5(i, 0, ng5)
                if pending5[0] is not None:
                    pending5[0]()
                    pending5[0] = None
                for g in range(ng5):
                    if g + 1 < ng5:
                        emit_scores5(i, g + 1, ng5)
                    if g == 2 and i + 1 < len(idxs):
                        emit_tmpbound(i + 1)
                    kcs = [2 * g] if 2 * g == NKL - 1 else [2 * g, 2 * g + 1]
                    kb.W("t", ac, exp_g5[(g % 2) * 2])
                    if g == 0:
                        bankw("t", 4)
                    for kc in kcs:
                        kb.E("t", "matmul", PSW(4, 0, HARM + 1, TT),
                             rva_sb[:, kc * RVW:kc * RVW + HARM + 1],
                             pb5[:, kc * TT:(kc + 1) * TT],
                             start=(kc == 0), stop=(kc == NKL - 1),
                             inc=(mm if kc == NKL - 1 else None))
                if ng5 <= 2 and i + 1 < len(idxs):
                    emit_tmpbound(i + 1)
                ctx_mm = mm.n
                p5_pe = ctx_mm
                tmpa_pe[i % 2] = ctx_mm
                pending5[0] = (lambda i_=i, h_=h, tp_=tp, c_=ctx_mm:
                               tail5(i_, h_, tp_, c_))
            if pending5[0] is not None:
                pending5[0]()
                pending5[0] = None
            ores_mm = mm.n

            for tp in range(NTP):
                kb.W("v", mm, ores_mm)
                kb.W("v", dv, dv.n)
                kb.E("v", "tensor_copy", ores[tp][0:HARM, 0:TT],
                     PSW(6 + tp, 0, HARM, TT), inc=dv)
                bankg(6 + tp, dv)
            ores_dv = dv.n

            def ld_x1(tt, fo):
                def em(sem):
                    dma("y", sem, x1t5[S_sL.i % 2],
                        x1_scr[fo * 128:(fo + 1) * 128,
                               q0 + tt * TT:q0 + (tt + 1) * TT])
                return S_sL.load(em)

            x1_i0 = ld_x1(0, 0)
            for tp in range(NTP):
                for oc in range(NF):
                    ii = tp * NF + oc
                    S_sL.need("v", x1_i0 + ii)
                    if ii + 1 < NTP * NF:
                        ld_x1(*divmod(ii + 1, NF))
                    b_ = oc % 2
                    kb.W("t", dv, ores_dv)
                    bankw("t", b_)
                    kb.E("t", "matmul", PS(b_, 128, TT),
                         wTvo_sb[:, D + oc * 128:D + (oc + 1) * 128],
                         ores[tp][0:HARM, 0:TT], start=True, stop=True, inc=mm)
                    kb.W("v", mm, mm.n)
                    kb.W("v", dv, dv.n)
                    kb.E("v", "tensor_add",
                         res2[:, oc * TOKP + tp * TT:oc * TOKP + (tp + 1) * TT],
                         PS(b_, 128, TT), x1t5[(x1_i0 + ii) % 2], inc=dv)
                    bankg(b_, dv)
                    S_sL.done(x1_i0 + ii, dv, dv.n)

            def ld_xn1(sem):
                dma("y", sem, xn1_t.rearrange("p (f t) -> p f t", f=NF),
                    xn1_scr[:, q0:q0 + TOKP].rearrange("(f p) t -> p f t", p=128))

            xn1_i = S_kb.load(ld_xn1)
            S_kb.need("t", xn1_i)

            state = {"silu": {0: 0, 1: 0}, "fuse": {2: 0, 3: 0, "s": 0},
                     "fuse_pe": p5_pe, "cbb_dv": 0, "cbb_cons": 0}

            def mk_ldg(w_dram, groups, bufs, stream):
                def ld(ig):
                    og0, gw = groups[ig]

                    def em(sem):
                        dma(stream.qn, sem, bufs[stream.i % 2][:, 0:NF * gw],
                            w_dram[:, NF * og0:NF * og0 + NF * gw])
                    return stream.load(em)
                return ld

            def mk_ldd(w_dram, nic_, stream):
                def ld(fo):
                    def em(sem):
                        dma(stream.qn, sem, wds[stream.i % 2][:, 0:nic_ * 128],
                            w_dram[:, fo * nic_ * 128:(fo + 1) * nic_ * 128])
                    return stream.load(em)
                return ld

            def ffn_block(n_out_chunks, groups, ldg, ldu, ldd, use_comb,
                          e_idx, stA=None, stB=None, stC=None):
                for tp in range(NTP):
                    xo5 = lambda f: xn1_t[:, f * TOKP + tp * TT:
                                          f * TOKP + (tp + 1) * TT]
                    if use_comb:
                        def ld_c(sem):
                            dma("y", sem, crow2[S_sL.i % 2],
                                comb_scr[e_idx:e_idx + 1,
                                         q0 + tp * TT:q0 + (tp + 1) * TT])
                        ci_ = S_sL.load(ld_c)
                        S_sL.need("t", ci_)
                        bankw("t", 7)
                        kb.E("t", "matmul", PS(7, 128, TT), ones_r_f[:, :],
                             crow2[ci_ % 2], start=True, stop=True, inc=mm)
                        S_sL.done(ci_, mm, mm.n)
                        kb.W("v", mm, mm.n)
                        kb.W("v", dv, state["cbb_cons"])
                        kb.W("v", dv, dv.n)
                        kb.E("v", "tensor_copy", cbb, PS(7, 128, TT), inc=dv)
                        bankg(7, dv)
                        state["cbb_dv"] = dv.n
                    gis = []
                    for ig in range(len(groups)):
                        gis.append((ldg(ig), ldu(ig)))
                    for ig, (og0, gw) in enumerate(groups):
                        gi_, ui_ = gis[ig]
                        stA.need("t", gi_)
                        stB.need("t", ui_)
                        for oc in range(gw // 128):
                            gc = og0 // 128 + oc
                            gb = oc % 2
                            ub = 2 + oc % 2
                            ssl = gc % 2
                            kb.W("t", ac, state["silu"][gb])
                            kb.W("t", dv, state["fuse"][ub])
                            bankw("t", gb)
                            bankw("t", ub)
                            mml(PS(gb, 128, TT),
                                [(wgs[gi_ % 2][:, f * gw + oc * 128:
                                               f * gw + (oc + 1) * 128], xo5(f))
                                 for f in range(NF)], inc=mm)
                            g_mm = mm.n
                            mml(PS(ub, 128, TT),
                                [(wus[ui_ % 2][:, f * gw + oc * 128:
                                               f * gw + (oc + 1) * 128], xo5(f))
                                 for f in range(NF)], inc=mm)
                            u_mm = mm.n
                            kb.W("a", mm, g_mm)
                            kb.W("a", dv, state["fuse"].get(("s", ssl), 0))
                            kb.E("a", "activation", silu5[ssl], PS(gb, 128, TT),
                                 AF.Silu, bias=0.0, scale=1.0, inc=ac)
                            state["silu"][gb] = ac.n
                            bankg(gb, ac)
                            kb.W("v", ac, ac.n)
                            if use_comb:
                                kb.W("v", dv, state["cbb_dv"])
                                kb.W("v", dv, dv.n)
                                kb.E("v", "tensor_mul", silu5b, silu5[ssl], cbb,
                                     inc=dv)
                                state["cbb_cons"] = dv.n
                                sl_use = silu5b
                            else:
                                sl_use = silu5[ssl]
                            kb.W("v", mm, u_mm)
                            kb.W("v", mm, state["fuse_pe"])
                            kb.W("v", dv, dv.n)
                            kb.E("v", "tensor_mul",
                                 fuse5[:, gc * TT:(gc + 1) * TT],
                                 PS(ub, 128, TT), sl_use, inc=dv)
                            state["fuse"][ub] = dv.n
                            state[("fuse", "s", ssl)] = dv.n
                            state["fuse"][("s", ssl)] = dv.n
                            bankg(ub, dv)
                            state["fuse"]["s"] = dv.n
                        stA.done(gi_, mm, mm.n)
                        stB.done(ui_, mm, mm.n)
                    fuse_dv_ = dv.n
                    dls = [ldd(0)]
                    for fo in range(NF):
                        di_ = dls[fo]
                        if fo + 1 < NF:
                            dls.append(ldd(fo + 1))
                        stC.need("t", di_)
                        db = 4 + fo % 2
                        kb.W("t", dv, fuse_dv_)
                        bankw("t", db)
                        mml(PS(db, 128, TT),
                            [(wds[di_ % 2][:, ic * 128:(ic + 1) * 128],
                              fuse5[:, ic * TT:(ic + 1) * TT])
                             for ic in range(n_out_chunks)], inc=mm)
                        stC.done(di_, mm, mm.n)
                        kb.W("v", mm, mm.n)
                        kb.W("v", dv, dv.n)
                        kb.E("v", "tensor_add",
                             res2[:, fo * TOKP + tp * TT:
                                  fo * TOKP + (tp + 1) * TT],
                             PS(db, 128, TT),
                             res2[:, fo * TOKP + tp * TT:
                                  fo * TOKP + (tp + 1) * TT], inc=dv)
                        bankg(db, dv)
                    state["fuse_pe"] = mm.n

            ffn_block(NDS, ds_groups,
                      mk_ldg(wsg_b, ds_groups, wgs, S_yA),
                      mk_ldg(wsu_b, ds_groups, wus, S_yB),
                      mk_ldd(wsd_b, NDS, S_yC), False, 0,
                      stA=S_yA, stB=S_yB, stC=S_yC)

            for e in range(E):
                wege = weg_b[e * 128:(e + 1) * 128, :]
                weue = weu_b[e * 128:(e + 1) * 128, :]
                wede = wed_b[e * 128:(e + 1) * 128, :]
                ffn_block(NDE, de_groups,
                          mk_ldg(wege, de_groups, wgs, S_yA),
                          mk_ldg(weue, de_groups, wus, S_yB),
                          mk_ldd(wede, NDE, S_yC), True, e,
                          stA=S_yA, stB=S_yB, stC=S_yC)
            S_kb.done(xn1_i, mm, mm.n)

            out_dv = dv.n
            kb.W("y", dv, out_dv)
            for fo in range(NF):
                dma("y", ys, yout[fo * 128:(fo + 1) * 128, q0:q0 + TOKP],
                    res2[:, fo * TOKP:(fo + 1) * TOKP])
            arena.pop()
            kb.barrier()

        kb.W("y", ys, ys.n)

        @block.tensor
        def _(eng):
            for f in kb.q["t"]:
                f(eng)

        @block.vector
        def _(eng):
            for f in kb.q["v"]:
                f(eng)

        @block.scalar
        def _(eng):
            for f in kb.q["a"]:
                f(eng)

        @block.gpsimd
        def _(eng):
            for f in kb.q["g"]:
                f(eng)

        @block.sync
        def _(eng):
            for f in kb.q["y"]:
                f(eng)

    return nc



def make_masks(cfg):
    TT, NM = cfg["TT"], cfg["NM"]
    k = np.arange(NM * 128)[:, None]
    qq = np.arange(TT)[None, :]
    kk = k % 128 + (k // 128) * 128
    return np.where(qq >= kk, 0.0, NEGM).astype(np.float32)


def prep_core_inputs(cfg, I, core):
    S, D, TT = cfg["S"], cfg["D"], cfg["TT"]
    NH, HD, TOK = cfg["NH_CORE"], cfg["HD"], cfg["TOK"]
    NIC, NF = cfg["NIC"], cfg["NF"]
    DE, DS = cfg["DE"], cfg["DS"]
    b, r = core // 2, core % 2
    C = np.ascontiguousarray

    def tfm(w, gw):
        K_, O_ = w.shape
        return C(w.reshape(K_ // 128, 128, O_ // gw, gw)
                 .transpose(2, 1, 0, 3).reshape((O_ // gw) * 128,
                                                (K_ // 128) * gw))

    def tgrp(w, groups):
        Kc = w.shape[0] // 128
        parts = [w.reshape(Kc, 128, w.shape[1])[:, :, og0:og0 + gw]
                 .transpose(1, 0, 2).reshape(128, Kc * gw)
                 for og0, gw in groups]
        return np.concatenate(parts, axis=1)

    GW0 = min(256, DE, DS)
    ds_groups = []
    _o = 0
    while _o < DS:
        _w = min(GW0, DS - _o)
        ds_groups.append((_o, _w))
        _o += _w
    de_groups = []
    _o = 0
    while _o < DE:
        _w = min(GW0, DE - _o)
        de_groups.append((_o, _w))
        _o += _w
    fo_groups = [(fo * 128, 128) for fo in range(NF)]

    d = {}
    x = np.asarray(I["L0_attn_x"][b])
    d["xT"] = C(x.T)
    d["xoT"] = C(x.T[:, r * TOK:(r + 1) * TOK])
    hsl = slice(r * HD, (r + 1) * HD)
    d["wq_t"] = tfm(C(np.asarray(I["L0_attn_Wq"]).T[:, hsl]), 128)
    d["wk_t"] = tfm(C(np.asarray(I["L0_attn_Wk"]).T[:, hsl]), HD)
    d["wv_t"] = tfm(C(np.asarray(I["L0_attn_Wv"]).T[:, hsl]), HD)
    d["wo_t"] = tfm(C(np.asarray(I["L0_attn_Wo"]).T), 128)
    d["fg8t"] = tfm(np.asarray(I["L0_ffn_w_gate"]), 128)
    d["fu8t"] = tfm(np.asarray(I["L0_ffn_w_up"]), 128)
    d["fd8t"] = tfm(np.asarray(I["L0_ffn_w_down"]), 128)
    d["sgT"] = C(np.asarray(I["L0_ffn_s_gate"]).reshape(NIC, 128).T)
    d["suT"] = C(np.asarray(I["L0_ffn_s_up"]).reshape(NIC, 128).T)
    d["sdT"] = C(np.asarray(I["L0_ffn_s_down"]).reshape(NF, 128).T)
    d["bqT"] = C(np.asarray(I["L1_attn_basis_q"]).T)
    d["bkT"] = C(np.asarray(I["L1_attn_basis_k"]).T)
    d["bvT"] = C(np.asarray(I["L1_attn_basis_v"]).T)
    d["boT"] = C(np.asarray(I["L1_attn_basis_o"]).T)
    d["phT"] = C(np.concatenate(
        [np.asarray(I[f"L1_attn_phase_{x_}"]).T for x_ in "qkvo"], 0))
    d["amT"] = C(np.concatenate(
        [np.asarray(I[f"L1_attn_amp_{x_}"]).T for x_ in "qkvo"], 0))
    d["wr"] = C(np.asarray(I["L1_ffn_W_router"]))
    weg = np.asarray(I["L1_ffn_We_gate"])
    weu = np.asarray(I["L1_ffn_We_up"])
    wed = np.asarray(I["L1_ffn_We_down"])
    d["weg_t"] = C(np.stack([tgrp(weg[e], de_groups) for e in range(8)])
                   .reshape(8 * 128, -1))
    d["weu_t"] = C(np.stack([tgrp(weu[e], de_groups) for e in range(8)])
                   .reshape(8 * 128, -1))
    d["wed_t"] = C(np.stack([tgrp(wed[e], fo_groups) for e in range(8)])
                   .reshape(8 * 128, -1))
    d["wsg_t"] = C(tgrp(np.asarray(I["L1_ffn_Ws_gate"]), ds_groups))
    d["wsu_t"] = C(tgrp(np.asarray(I["L1_ffn_Ws_up"]), ds_groups))
    d["wsd_t"] = C(tgrp(np.asarray(I["L1_ffn_Ws_down"]), fo_groups))
    d["idn"] = np.eye(128, dtype=np.float32)
    d["cmask"] = make_masks(cfg)
    d["rolep"] = np.array([[r * TOK, 0]], np.int32)
    return d


_NC_CACHE = {}


def kernel(**inputs):
    cfg = make_cfg(mini=False)
    if "full" not in _NC_CACHE:
        _NC_CACHE["full"] = build_nc(cfg)
    nc = _NC_CACHE["full"]
    in_maps = [prep_core_inputs(cfg, inputs, c) for c in range(8)]
    res = run_bass_kernel_spmd(nc, in_maps, core_ids=list(range(8)))
    S, D, TOK = cfg["S"], cfg["D"], cfg["TOK"]
    out = np.empty((4, S, D), np.float32)
    for c in range(8):
        b, r = c // 2, c % 2
        out[b, r * TOK:(r + 1) * TOK, :] = res.results[c]["yout"].T
    return out


# version 09 (final)
# speedup vs baseline: 1.0645x; 1.0511x over previous
import contextlib
import numpy as np

import concourse.bass as bass
import concourse.mybir as mybir
from concourse.bass_utils import run_bass_kernel_spmd

F32 = mybir.dt.float32
BF16 = mybir.dt.bfloat16
I8 = mybir.dt.int8
I32 = mybir.dt.int32
AF = mybir.ActivationFunctionType
ALU = mybir.AluOpType
AX = mybir.AxisListType.X

NEGM = -1.0e9


def make_cfg(mini=False):
    if mini:
        c = dict(S=512, D=256, NH_CORE=1, INTER=512, DE=256, DS=384, E=8, TT=128)
    else:
        c = dict(S=4096, D=2048, NH_CORE=8, INTER=8192, DE=1408, DS=2816, E=8, TT=512)
    c["HARM"] = 64
    c["TOK"] = c["S"] // 2
    c["NF"] = c["D"] // 128
    c["NTT"] = c["S"] // c["TT"]
    c["NTTO"] = c["TOK"] // c["TT"]
    c["HD"] = c["NH_CORE"] * 128
    c["NM"] = c["TT"] // 128
    c["NIC"] = c["INTER"] // 128
    c["NDS"] = c["DS"] // 128
    c["NDE"] = c["DE"] // 128
    c["NHL1"] = c["D"] // 128
    c["NKL"] = c["S"] // 128
    c["TOKP"] = max(c["TT"], c["TOK"] // 4)
    c["PASSES"] = c["TOK"] // c["TOKP"]
    c["NTP"] = max(1, c["TOKP"] // c["TT"])
    return c


class Sem:
    __slots__ = ("h", "n", "name")

    def __init__(self, h, name):
        self.h, self.n, self.name = h, 0, name


class KB:
    QUEUES = ("t", "v", "a", "g", "y")

    def __init__(self, nc, stack):
        self.nc = nc
        self.q = {k: [] for k in self.QUEUES}
        self.sems = {}
        self._bar = {}
        self._stack = stack

    def sem(self, name):
        s = Sem(self._stack.enter_context(self.nc.semaphore(name)), name)
        self.sems[name] = s
        self._bar[name] = 0
        return s

    def E(self, qn, fn, *a, inc=None, iv=1, **k):
        if inc is not None:
            inc.n += iv
        import traceback
        site = traceback.extract_stack(limit=3)[0]

        def run(eng, fn=fn, a=a, k=k, inc=inc, iv=iv, site=site):
            try:
                ins = getattr(eng, fn)(*a, **k)
            except Exception:
                print(f"EMIT FAIL {fn} at {site.filename}:{site.lineno}")
                for x in a:
                    if hasattr(x, 'shape'):
                        print("   arg", x.shape, getattr(x, 'dtype', None))
                raise
            if inc is not None:
                ins.then_inc(inc.h, iv)

        self.q[qn].append(run)
        return inc.n if inc is not None else None

    def W(self, qn, sem, val):
        if val and val > 0:
            self.q[qn].append(lambda eng, s=sem.h, v=val: eng.wait_ge(s, v))

    def barrier(self, exclude=()):
        snap = [(s, s.n) for nm, s in self.sems.items()
                if s.n > self._bar[nm] and nm not in exclude]
        for nm, s in self.sems.items():
            if nm not in exclude:
                self._bar[nm] = s.n
        for qn in self.QUEUES:
            for s, v in snap:
                self.W(qn, s, v)


class Arena:
    def __init__(self, nc, stack, total):
        self.t = stack.enter_context(nc.sbuf_tensor("arena", [128, total], BF16))
        self.total = total
        self.lo = 0
        self.marks = []
        self.peak = 0

    def alloc(self, n_free, dtype=BF16, parts=128):
        need = n_free * (2 if dtype == F32 else 1)
        off = self.lo
        self.lo += (need + 63) & ~63
        self.peak = max(self.peak, self.lo)
        assert self.lo <= self.total, f"arena overflow {self.lo}>{self.total}"
        ap = self.t[0:parts, off:off + need]
        return ap.bitcast(F32) if dtype == F32 else ap

    def push(self):
        self.marks.append(self.lo)

    def pop(self):
        self.lo = self.marks.pop()


class Stream:

    def __init__(self, kb, sems, qn):
        self.kb, self.sems, self.qn = kb, sems, qn
        self.i = 0
        self.vals = []
        self.free = [None, None]
        self.lastv = [0, 0]

    def load(self, emit):
        sl = self.i % 2
        if self.free[sl] is not None:
            s, v = self.free[sl]
            self.kb.W(self.qn, s, v)
            self.free[sl] = None
        if self.lastv[sl]:
            self.kb.W(self.qn, self.sems[sl], self.lastv[sl])
        emit(self.sems[sl])
        self.vals.append(self.sems[sl].n)
        self.lastv[sl] = self.sems[sl].n
        self.i += 1
        return self.i - 1

    def need(self, qn, i):
        self.kb.W(qn, self.sems[i % 2], self.vals[i])

    def done(self, i, sem, val):
        self.free[i % 2] = (sem, val)


def build_nc(cfg):
    S, D, TT = cfg["S"], cfg["D"], cfg["TT"]
    NF, NTT, NTTO = cfg["NF"], cfg["NTT"], cfg["NTTO"]
    NH, HD, NM = cfg["NH_CORE"], cfg["HD"], cfg["NM"]
    INTER, NIC = cfg["INTER"], cfg["NIC"]
    DS, NDS, DE, NDE, E = cfg["DS"], cfg["NDS"], cfg["DE"], cfg["NDE"], cfg["E"]
    NHL1, NKL = cfg["NHL1"], cfg["NKL"]
    TOK, TOKP, NTP = cfg["TOK"], cfg["TOKP"], cfg["NTP"]
    PASSES = cfg["PASSES"]
    HARM = cfg["HARM"]
    ISCALE = float(1.0 / np.float32(np.sqrt(np.float32(128.0))))
    EPS = 1e-5

    nc = bass.Bass("TRN2", target_bir_lowering=False, debug=False)

    def din(name, shape, dt=F32):
        return nc.dram_tensor(name, shape, dt, kind="ExternalInput").ap()

    GW0 = min(256, DE, DS)
    ds_groups = []
    _o = 0
    while _o < DS:
        _w = min(GW0, DS - _o)
        ds_groups.append((_o, _w))
        _o += _w
    de_groups = []
    _o = 0
    while _o < DE:
        _w = min(GW0, DE - _o)
        de_groups.append((_o, _w))
        _o += _w
    xT = din("xT", [D, S])
    xoT = din("xoT", [D, TOK])
    wq_t = din("wq_t", [NH * 128, NF * 128])
    wk_t = din("wk_t", [128, NF * HD])
    wv_t = din("wv_t", [128, NF * HD])
    wo_t = din("wo_t", [NF * 128, NF * 128])
    fg8t = din("fg8t", [(INTER // 128) * 128, NF * 128], I8)
    fu8t = din("fu8t", [(INTER // 128) * 128, NF * 128], I8)
    fd8t = din("fd8t", [NF * 128, NIC * 128], I8)
    sgT = din("sgT", [128, NIC])
    suT = din("suT", [128, NIC])
    sdT = din("sdT", [128, NF])
    bqT = din("bqT", [D, HARM])
    bkT = din("bkT", [D, HARM])
    bvT = din("bvT", [D, HARM])
    boT = din("boT", [D, HARM])
    phT = din("phT", [4 * HARM, D])
    amT = din("amT", [4 * HARM, D])
    wr = din("wr", [D, E])
    weg_t = din("weg_t", [E * 128, NF * DE])
    weu_t = din("weu_t", [E * 128, NF * DE])
    wed_t = din("wed_t", [E * 128, NF * DE])
    wsg_t = din("wsg_t", [128, NF * DS])
    wsu_t = din("wsu_t", [128, NF * DS])
    wsd_t = din("wsd_t", [128, NF * DS])
    idn = din("idn", [128, 128])
    cmask = din("cmask", [NM * 128, TT])
    rolep = din("rolep", [1, 2], I32)

    yout = nc.dram_tensor("yout", [D, TOK], F32, kind="ExternalOutput").ap()

    def scr(name, shape, dt=BF16):
        return nc.dram_tensor(name, shape, dt, kind="Internal").ap()

    q_scr = scr("q_scr", [HD, S])
    k_scr = scr("k_scr", [HD, S])
    v_scr = scr("v_scr", [128, NH * S])
    xn0_scr = scr("xn0_scr", [D, S])
    xn0_own = scr("xn0_own", [D, TOK])
    ctx_own = scr("ctx_own", [2 * HD, TOK])
    cc0_in = scr("cc0_in", [HD, S])
    cc0_out = scr("cc0_out", [2 * HD, S])
    res_scr = scr("res_scr", [D, TOK], F32)
    x1_scr = scr("x1_scr", [D, TOK], F32)
    xn1_scr = scr("xn1_scr", [D, TOK])
    cc1_in = scr("cc1_in", [TOK, 2 * HARM])
    cc1_out = scr("cc1_out", [S, 2 * HARM])
    comb_scr = scr("comb_scr", [E, TOK], F32)
    weg_b = scr("weg_b", [E * 128, NF * DE])
    weu_b = scr("weu_b", [E * 128, NF * DE])
    wed_b = scr("wed_b", [E * 128, NF * DE])
    wsg_b = scr("wsg_b", [128, NF * DS])
    wsu_b = scr("wsu_b", [128, NF * DS])
    wsd_b = scr("wsd_b", [128, NF * DS])

    stack = contextlib.ExitStack()
    with stack:
        kb = KB(nc, stack)

        mm = kb.sem("mm")
        dv = kb.sem("dv")
        ac = kb.sem("ac")
        ccs = kb.sem("ccs")
        gS = kb.sem("gS")
        yS = kb.sem("yS")
        pl = kb.sem("pl")
        rS = kb.sem("rS")
        S_wA = Stream(kb, [kb.sem("wA0"), kb.sem("wA1")], "g")
        S_wB = Stream(kb, [kb.sem("wB0"), kb.sem("wB1")], "g")
        S_wC = Stream(kb, [kb.sem("wC0"), kb.sem("wC1")], "g")
        S_xg = Stream(kb, [kb.sem("xg0"), kb.sem("xg1")], "g")
        S_xr = Stream(kb, [kb.sem("xr0"), kb.sem("xr1")], "g")
        S_kb = Stream(kb, [kb.sem("kb0"), kb.sem("kb1")], "y")
        S_vb = Stream(kb, [kb.sem("vb0"), kb.sem("vb1")], "y")
        S_qb = Stream(kb, [kb.sem("qb0"), kb.sem("qb1")], "y")
        S_sL = Stream(kb, [kb.sem("sL0"), kb.sem("sL1")], "y")
        S_yA = Stream(kb, [kb.sem("yA0"), kb.sem("yA1")], "y")
        S_yB = Stream(kb, [kb.sem("yB0"), kb.sem("yB1")], "y")
        S_yC = Stream(kb, [kb.sem("yC0"), kb.sem("yC1")], "y")
        S_wC2 = Stream(kb, [kb.sem("wc20"), kb.sem("wc21")], "y")
        stg = [kb.sem(f"st{i}") for i in range(4)]
        stg_free = [0, 0, 0, 0]
        stg_state = {"i": 0}
        ys = kb.sem("ys")

        AR = 76000
        arena = Arena(nc, stack, AR)

        sb = lambda n, shp, dt: stack.enter_context(nc.sbuf_tensor(n, shp, dt))
        id_bf = sb("id_bf", [128, 128], BF16)
        id_f32 = sb("id_f32", [128, 128], F32)
        ones_c_bf = sb("ones_c_bf", [128, 1], BF16)
        ones_c_f = sb("ones_c_f", [128, 1], F32)
        ones_r_f = sb("ones_r_f", [1, 128], F32)
        sg_sb = sb("sg_sb", [128, NIC], F32)
        su_sb = sb("su_sb", [128, NIC], F32)
        sd_sb = sb("sd_sb", [128, NF], F32)
        wr_sb = sb("wr_sb", [128, NF * E], F32)
        ncs_wr = sb("ncs_wr", [1, E], F32)
        role_sb = sb("role_sb", [1, 2], I32)
        wTvo_sb = sb("wTvo_sb", [HARM, 2 * D], BF16)
        b4_sb = sb("b4_sb", [128, NF * 2 * HARM], BF16)
        bkv_sb = sb("bkv_sb", [128, NF * 2 * HARM], BF16)
        resq_sb = sb("resq_sb", [HARM, TOK], F32)
        rka_sb = sb("rka_sb", [HARM + 1, S], BF16)
        nh_sb = sb("nh_sb", [HARM, NHL1 * HARM], F32)
        RVW = HARM + 8
        rva_sb = sb("rva_sb", [128, NKL * RVW], BF16)
        rows_t = sb("rows_t", [1, 8 * TT + 16], F32)
        cns = sb("cns", [128, 2], F32)
        b4v = b4_sb.rearrange("p (f n h) -> p f n h", f=NF, n=2)

        def row(i):
            return rows_t[:, i * TT:(i + 1) * TT]

        bstat = rows_t[:, 8 * TT:8 * TT + 16]

        psum = stack.enter_context(nc.psum_tensor("psum", [128, 4096], F32))

        def PS(bank, parts=128, width=512):
            return psum[0:parts, bank * 512:bank * 512 + width]

        def PSW(bank, p0, p1, width, off=0):
            return psum[p0:p1, bank * 512 + off:bank * 512 + off + width]

        pg = {}

        def bankw(qn, bank):
            if bank in pg:
                s, v = pg[bank]
                kb.W(qn, s, v)

        def bankg(bank, sem):
            pg[bank] = (sem, sem.n)

        block = stack.enter_context(nc.Block())

        def mml(ps_ap, pairs, inc=None):
            n = len(pairs)
            for i, (l, r) in enumerate(pairs):
                kb.E("t", "matmul", ps_ap, l, r, start=(i == 0), stop=(i == n - 1),
                     inc=(inc if i == n - 1 else None))

        def dma(qn, sem, out, in_):
            return kb.E(qn, "dma_start", inc=sem, iv=16, out=out, in_=in_)

        groles = {}

        def gdma_role(sem, out, h, static_off, pattern):
            sem.n += 16

            def run(eng, out=out, h=h, so=static_off, pat=pattern, s=sem):
                eng.reg_add(groles["tmp"], groles["roff"], so)
                try:
                    eng.dma_start(out=out, in_=bass.AP(h, groles["tmp"], pat)
                                  ).then_inc(s.h, 16)
                except Exception:
                    print("GDMA FAIL", h.name, so, pat, out.shape)
                    raise

            kb.q["g"].append(run)

        def stg_store(wq, out_ap, src_ap, producer_sem, producer_val):
            ss = stg_state["i"]
            kb.W("y", producer_sem, producer_val)
            kb.W("y", stg[ss], stg_free[ss])
            dma("y", stg[ss], out_ap, src_ap)
            stg_free[ss] = stg[ss].n
            stg_state["i"] = (ss + 1) % 4
            return (stg[ss], stg[ss].n)

        dma("y", rS, role_sb[:, :], rolep)

        def g_pre(eng):
            eng.wait_ge(rS.h, 16)
            groles["roff"] = eng.alloc_register("roff")
            groles["tmp"] = eng.alloc_register("rtmp")
            eng.reg_load(groles["roff"], role_sb[0:1, 0:1])

        kb.q["g"].append(g_pre)

        dma("g", gS, id_bf[:, :], idn)
        dma("y", yS, id_f32[:, :], idn)
        dma("y", yS, sg_sb[:, :], sgT)
        dma("y", yS, su_sb[:, :], suT)
        dma("y", yS, sd_sb[:, :], sdT)
        dma("y", yS, wr_sb.rearrange("p (f e) -> p f e", f=NF),
            wr.rearrange("(f p) e -> p f e", p=128))
        for i, b in enumerate((bqT, boT)):
            dma("g", gS, b4v[:, :, i, :], b.rearrange("(f p) h -> p f h", p=128))
        bkvv = bkv_sb.rearrange("p (f n h) -> p f n h", f=NF, n=2)
        dma("g", gS, bkvv[:, :, 0, :], bkT.rearrange("(f p) h -> p f h", p=128))
        dma("g", gS, bkvv[:, :, 1, :], bvT.rearrange("(f p) h -> p f h", p=128))

        kb.E("v", "memset", ones_c_bf[:, :], 1.0, inc=dv)
        kb.E("v", "memset", ones_c_f[:, :], 1.0, inc=dv)
        kb.E("v", "memset", ones_r_f[:, :], 1.0, inc=dv)
        kb.E("v", "memset", cns[:, 0:1], float(np.pi / 2), inc=dv)
        kb.E("v", "memset", cns[:, 1:2], EPS, inc=dv)
        ones_dv = dv.n

        arena.push()
        ph_sb = arena.alloc(4 * D, F32, parts=HARM)
        am_sb = arena.alloc(4 * D, F32, parts=HARM)
        cosb = arena.alloc(4 * D, F32, parts=HARM)
        wTt = arena.alloc(4 * D, BF16, parts=HARM)
        wqtr = arena.alloc(NHL1 * HARM)
        wktr = arena.alloc(NHL1 * HARM)
        for i in range(4):
            dma("y", yS, ph_sb[:, i * D:(i + 1) * D],
                phT[i * HARM:(i + 1) * HARM, :])
            dma("y", yS, am_sb[:, i * D:(i + 1) * D],
                amT[i * HARM:(i + 1) * HARM, :])
        kb.W("a", yS, yS.n)
        kb.W("a", dv, ones_dv)
        kb.E("a", "activation", cosb, ph_sb, AF.Sin, bias=cns[0:HARM, 0:1],
             scale=1.0, inc=ac)
        kb.W("v", ac, ac.n)
        kb.E("v", "tensor_mul", wTt, am_sb, cosb, inc=dv)
        kb.W("v", dv, dv.n)
        kb.E("v", "tensor_copy", wTvo_sb[:, :], wTt[:, 2 * D:4 * D], inc=dv)
        wt_dv = dv.n

        kb.W("t", dv, wt_dv)
        kb.W("t", gS, gS.n)
        for i, dstb in ((0, wqtr), (1, wktr)):
            for h in range(NHL1):
                tb = (i * NHL1 + h) % 2
                bankw("t", tb)
                kb.E("t", "transpose",
                     PS(tb).bitcast(BF16)[0:128, 0:HARM],
                     wTt[:, i * D + h * 128:i * D + (h + 1) * 128],
                     id_bf[0:HARM, 0:HARM], inc=mm)
                kb.W("v", mm, mm.n)
                kb.W("v", dv, dv.n)
                kb.E("v", "tensor_copy", dstb[:, h * HARM:(h + 1) * HARM],
                     PS(tb).bitcast(BF16)[0:128, 0:HARM], inc=dv)
                bankg(tb, dv)
        wtr_dv = dv.n
        kb.W("t", dv, wtr_dv)
        for h in range(NHL1):
            tb = h % 2
            bankw("t", tb)
            kb.E("t", "matmul", PSW(tb, 0, HARM, HARM),
                 wqtr[:, h * HARM:(h + 1) * HARM],
                 wktr[:, h * HARM:(h + 1) * HARM],
                 start=True, stop=True, inc=mm)
            kb.W("v", mm, mm.n)
            kb.W("v", dv, dv.n)
            kb.E("v", "tensor_copy", nh_sb[:, h * HARM:(h + 1) * HARM],
                 PSW(tb, 0, HARM, HARM), inc=dv)
            bankg(tb, dv)
        nh0_dv = dv.n

        kb.W("t", yS, yS.n)
        kb.W("t", dv, dv.n)
        mml(PS(7, 1, E), [(ones_c_f[:, :], wr_sb[:, f * E:(f + 1) * E])
                          for f in range(NF)], inc=mm)
        kb.W("a", mm, mm.n)
        kb.E("a", "activation", ncs_wr[:, :], PS(7, 1, E), AF.Copy, bias=0.0,
             scale=-1.0, inc=ac)
        arena.pop()
        kb.barrier()

        arena.push()
        x_bf = [arena.alloc(NF * TT) for _ in range(2)]
        xn_bf = [arena.alloc(NF * TT) for _ in range(2)]
        wkv_sb = arena.alloc(2 * NF * HD)
        wq_sl = [arena.alloc(NF * 128) for _ in range(2)]
        sq_bf = [arena.alloc(TT) for _ in range(2)]
        mu_b = arena.alloc(TT, F32)
        r_b = arena.alloc(TT, F32)
        tmp_f = arena.alloc(TT, F32)
        stg_sb = [arena.alloc(TT) for _ in range(4)]

        wk_of, wv_of = 0, NF * HD
        dma("g", gS, wkv_sb[:, wk_of:wk_of + NF * HD], wk_t)
        dma("g", gS, wkv_sb[:, wv_of:wv_of + NF * HD], wv_t)
        g_wkv = gS.n

        def ld_x(tt):
            def em(sem):
                dma("g", sem, x_bf[S_xg.i % 2].rearrange("p (f t) -> p f t", f=NF),
                    xT[:, tt * TT:(tt + 1) * TT].rearrange("(f p) t -> p f t", p=128))
            return S_xg.load(em)

        def ld_wq(oc):
            def em(sem):
                dma("g", sem, wq_sl[S_wA.i % 2],
                    wq_t[oc * 128:(oc + 1) * 128, :])
            return S_wA.load(em)

        precast_jobs = []
        for dst_, src_, nspl in ([(wsg_b, wsg_t, 4), (wsu_b, wsu_t, 4),
                                  (wsd_b, wsd_t, 4)] +
                                 [p for e_ in range(E) for p in
                                  [(weg_b[e_ * 128:(e_ + 1) * 128, :],
                                    weg_t[e_ * 128:(e_ + 1) * 128, :], 2),
                                   (weu_b[e_ * 128:(e_ + 1) * 128, :],
                                    weu_t[e_ * 128:(e_ + 1) * 128, :], 2),
                                   (wed_b[e_ * 128:(e_ + 1) * 128, :],
                                    wed_t[e_ * 128:(e_ + 1) * 128, :], 2)]]):
            W_ = dst_.shape[-1]
            cw = -(-W_ // nspl)
            for c0 in range(0, W_, cw):
                precast_jobs.append((dst_[:, c0:min(W_, c0 + cw)],
                                     src_[:, c0:min(W_, c0 + cw)]))
        precast_jobs.reverse()
        pcS = kb.sem("pcS")
        pc_last = [0]

        def precast(n):
            if precast_jobs:
                kb.W("g", pcS, pc_last[0])
                kb.W("g", mm, max(0, mm.n - 400))
            for _ in range(n):
                if precast_jobs:
                    d_, s_ = precast_jobs.pop()
                    dma("g", pcS, d_, s_)
                    pc_last[0] = pcS.n

        x_i0 = ld_x(0)
        wq_i0 = ld_wq(0)
        sq_guard = [0, 0]
        xn_guard = [[], []]

        for tt in range(NTT):
            sl = tt % 2
            xi = x_i0 + tt
            if tt + 1 < NTT:
                ld_x(tt + 1)

            S_xg.need("t", xi)
            kb.W("t", dv, ones_dv)
            bankw("t", 0)
            bankw("t", 1)
            mml(PS(0, 1, TT), [(ones_c_bf[:, :], x_bf[sl][:, f * TT:(f + 1) * TT])
                               for f in range(NF)], inc=mm)
            S_xg.need("a", xi)
            for f in range(NF):
                ssl = f % 2
                kb.W("a", mm, sq_guard[ssl])
                kb.E("a", "activation", sq_bf[ssl],
                     x_bf[sl][:, f * TT:(f + 1) * TT], AF.Square,
                     bias=0.0, scale=1.0, inc=ac)
                kb.W("t", ac, ac.n)
                kb.E("t", "matmul", PS(1, 1, TT), ones_c_bf[:, :], sq_bf[ssl],
                     start=(f == 0), stop=(f == NF - 1), inc=mm)
                sq_guard[ssl] = mm.n
            sums_mm = mm.n

            kb.W("a", mm, sums_mm)
            kb.E("a", "activation", row(0), PS(0, 1, TT), AF.Copy, bias=0.0,
                 scale=1.0 / D, inc=ac)
            kb.E("a", "activation", row(1), PS(1, 1, TT), AF.Copy, bias=0.0,
                 scale=1.0 / D, inc=ac)
            bankg(0, ac)
            bankg(1, ac)
            kb.W("a", ac, ac.n)
            kb.E("a", "activation", row(2), row(0), AF.Square, bias=0.0, scale=1.0,
                 inc=ac)
            kb.W("v", ac, ac.n)
            kb.E("v", "tensor_sub", row(3), row(1), row(2), inc=dv)
            kb.W("a", dv, dv.n)
            kb.E("a", "activation", row(4), row(3), AF.Ln, bias=cns[0:1, 1:2],
                 scale=1.0, inc=ac)
            kb.W("a", ac, ac.n)
            kb.E("a", "activation", row(5), row(4), AF.Exp, bias=0.0, scale=-0.5,
                 inc=ac)
            r_ac = ac.n

            kb.W("t", ac, r_ac)
            bankw("t", 2)
            kb.E("t", "matmul", PS(2, 128, TT), ones_r_f[:, :], row(0),
                 start=True, stop=True, inc=mm)
            bankw("t", 3)
            kb.E("t", "matmul", PS(3, 128, TT), ones_r_f[:, :], row(5),
                 start=True, stop=True, inc=mm)
            kb.W("v", mm, mm.n)
            kb.W("v", dv, dv.n)
            kb.E("v", "tensor_copy", mu_b, PS(2, 128, TT), inc=dv)
            kb.W("v", dv, dv.n)
            kb.E("v", "tensor_copy", r_b, PS(3, 128, TT), inc=dv)
            bankg(2, dv)
            bankg(3, dv)
            bc_dv = dv.n

            for s_, v_ in xn_guard[sl]:
                kb.W("v", s_, v_)
                kb.W("g", s_, v_)
            xn_guard[sl] = []
            kb.W("v", dv, bc_dv)
            kb.W("g", dv, bc_dv)
            kb.W("g", pl, pl.n)
            for f in range(NF):
                if f % 2 == 0:
                    kb.W("v", dv, dv.n)
                    kb.E("v", "tensor_sub", tmp_f,
                         x_bf[sl][:, f * TT:(f + 1) * TT], mu_b, inc=dv)
                    kb.W("v", dv, dv.n)
                    kb.E("v", "tensor_mul", xn_bf[sl][:, f * TT:(f + 1) * TT],
                         tmp_f, r_b, inc=dv)
                else:
                    kb.W("g", pl, pl.n)
                    kb.E("g", "tensor_sub", xn_bf[sl][:, f * TT:(f + 1) * TT],
                         x_bf[sl][:, f * TT:(f + 1) * TT], mu_b, inc=pl)
                    kb.W("g", pl, pl.n)
                    kb.E("g", "tensor_mul", xn_bf[sl][:, f * TT:(f + 1) * TT],
                         xn_bf[sl][:, f * TT:(f + 1) * TT], r_b, inc=pl)
            xn_dv = dv.n
            xn_pl = pl.n

            kb.W("y", pl, xn_pl)
            st_s, st_v = stg_store(
                "y",
                xn0_scr[:, tt * TT:(tt + 1) * TT].rearrange("(f p) t -> p f t",
                                                            p=128),
                xn_bf[sl].rearrange("p (f t) -> p f t", f=NF), dv, xn_dv)
            xn_guard[sl].append((st_s, st_v))

            xn_ap = lambda f, s=sl: xn_bf[s][:, f * TT:(f + 1) * TT]

            kb.W("t", dv, xn_dv)
            kb.W("t", pl, xn_pl)
            kb.W("t", gS, g_wkv)
            for mat in range(2):
                for oc in range(NH):
                    bank = 4 + (oc % 2)
                    if mat == 1:
                        qi = wq_i0 + tt * NH + oc
                        S_wA.need("t", qi)
                        if qi + 1 < wq_i0 + NTT * NH:
                            ld_wq((tt * NH + oc + 1) % NH)
                        if oc % 2 == 1:
                            precast(1)
                    bankw("t", bank)
                    if mat == 0:
                        pairs = [(wkv_sb[:, wk_of + f * HD + oc * 128:
                                         wk_of + f * HD + oc * 128 + 128],
                                  xn_ap(f)) for f in range(NF)]
                    else:
                        wsl = (tt * NH + oc) % 2
                        pairs = [(wq_sl[wsl][:, f * 128:(f + 1) * 128],
                                  xn_ap(f)) for f in range(NF)]
                    mml(PS(bank, 128, TT), pairs, inc=mm)
                    if mat == 1:
                        S_wA.done(qi, mm, mm.n)
                    kb.W("v", mm, mm.n)
                    ss = stg_state["i"]
                    kb.W("v", stg[ss], stg_free[ss])
                    kb.W("v", dv, dv.n)
                    kb.E("v", "tensor_copy", stg_sb[ss], PS(bank, 128, TT), inc=dv)
                    bankg(bank, dv)
                    dst = k_scr if mat == 0 else q_scr
                    stg_store("y",
                              dst[oc * 128:(oc + 1) * 128, tt * TT:(tt + 1) * TT],
                              stg_sb[ss], dv, dv.n)

            VG = min(512, HD)
            NVG = HD // VG
            for sub in range(TT // 128):
                for vg in range(NVG):
                    bank = 4 + ((sub * NVG + vg) % 2)
                    bankw("t", bank)
                    mml(PS(bank, 128, VG),
                        [(xn_ap(f)[:, sub * 128:(sub + 1) * 128],
                          wkv_sb[:, wv_of + f * HD + vg * VG:
                                 wv_of + f * HD + (vg + 1) * VG])
                         for f in range(NF)], inc=mm)
                    kb.W("v", mm, mm.n)
                    ss = stg_state["i"]
                    kb.W("v", stg[ss], stg_free[ss])
                    kb.W("v", dv, dv.n)
                    kb.E("v", "tensor_copy", stg_sb[ss][:, 0:VG],
                         PS(bank, 128, VG), inc=dv)
                    bankg(bank, dv)
                    t0 = tt * TT + sub * 128
                    hpg = VG // 128
                    stg_store("y",
                              v_scr.rearrange("p (h kc d) -> p h kc d", h=NH,
                                              d=128)[:, vg * hpg:(vg + 1) * hpg,
                                                     t0 // 128, :],
                              stg_sb[ss][:, 0:VG].rearrange(
                                  "p (h d) -> p h d", h=hpg), dv, dv.n)

            S_xg.done(xi, mm, mm.n)
            xn_guard[sl].append((mm, mm.n))

        arena.pop()
        kb.barrier()

        gdma_role(gS, xn0_own, xn0_scr.tensor, 0, [[S, D], [1, TOK]])
        arena.push()
        cm_sb = arena.alloc(NM * TT, F32)
        kbuf = [arena.alloc(S) for _ in range(2)]
        vbuf = [arena.alloc(NKL * 128) for _ in range(2)]
        qbuf = [arena.alloc(TT) for _ in range(2)]
        pbuf = arena.alloc(NKL * TT)
        rb_sb = arena.alloc(TT, F32)
        ctx_st = [arena.alloc(TT) for _ in range(2)]

        dma("y", yS, cm_sb.rearrange("p (m t) -> p m t", m=NM),
            cmask.rearrange("(m p) t -> p m t", p=128))
        cm_v = yS.n

        def ld_kv(h):
            def emk(sem):
                dma("y", sem, kbuf[S_kb.i % 2], k_scr[h * 128:(h + 1) * 128, :])

            def emv(sem):
                dma("y", sem, vbuf[S_vb.i % 2],
                    v_scr[:, h * S:(h + 1) * S])

            return S_kb.load(emk), S_vb.load(emv)

        def ld_q(h, qt):
            def em(sem):
                dma("y", sem, qbuf[S_qb.i % 2],
                    q_scr[h * 128:(h + 1) * 128, qt * TT:(qt + 1) * TT])

            return S_qb.load(em)

        ld_kv(0)
        ld_q(0, 0)
        exp_guard = {0: 0, 2: 0}
        p_pe_guard = 0
        ctx_guard = [None, None]

        def emit_scores(hsl, qsl, qt, g, nkc):
            k0 = 2 * g
            kcs = [k0] if k0 == nkc - 1 else [k0, k0 + 1]
            pb = (g % 2) * 2
            kb.W("t", ac, exp_guard[pb])
            s_mm = []
            for j, kc in enumerate(kcs):
                kb.E("t", "matmul", PS(pb + j, 128, TT),
                     kbuf[hsl][:, kc * 128:(kc + 1) * 128],
                     qbuf[qsl], start=True, stop=True, inc=mm)
                s_mm.append((j, kc, mm.n))
            need_dv = False
            for j, kc, v in s_mm:
                if kc >= qt * NM:
                    tgt = PS(pb + j, 128, TT)
                    kb.W("v", mm, v)
                    kb.W("v", dv, dv.n)
                    kb.E("v", "tensor_add", tgt, tgt,
                         cm_sb[:, (kc - qt * NM) * TT:
                               (kc - qt * NM + 1) * TT], inc=dv)
                    need_dv = True
            kb.W("a", mm, mm.n)
            if need_dv:
                kb.W("a", dv, dv.n)
            kb.W("a", mm, p_pe_guard)
            nk_ = len(kcs)
            src_ = psum[0:128, pb * 512:(pb + 2) * 512].rearrange(
                "p (g r) -> p g r", g=2)[:, 0:nk_, 0:TT]
            dst_ = pbuf[:, k0 * TT:(k0 + nk_) * TT].rearrange(
                "p (g t) -> p g t", g=nk_)
            kb.E("a", "activation", dst_, src_, AF.Exp, bias=0.0, scale=ISCALE,
                 inc=ac)
            exp_guard[pb] = ac.n
            return kcs

        pending2 = [None]

        def tail2(h, qt, cb, csl, av_mm):
            kb.W("v", mm, av_mm)
            kb.W("v", mm, prev2["rb_mm"])
            kb.W("v", dv, dv.n)
            kb.E("v", "reciprocal", row(6), PS(4, 1, TT), inc=dv)
            prev2["sums_rd"] = dv.n
            kb.W("t", dv, dv.n)
            kb.W("t", dv, prev2["rb_dv"])
            bankw("t", 6)
            kb.E("t", "matmul", PS(6, 128, TT), ones_r_f[:, :], row(6),
                 start=True, stop=True, inc=mm)
            prev2["rb_mm"] = mm.n
            kb.W("v", mm, mm.n)
            kb.W("v", dv, dv.n)
            kb.E("v", "tensor_copy", rb_sb, PS(6, 128, TT), inc=dv)
            bankg(6, dv)
            prev2["rb_dv"] = dv.n
            if ctx_guard[csl] is not None:
                kb.W("v", ctx_guard[csl][0], ctx_guard[csl][1])
            kb.W("v", dv, dv.n)
            kb.E("v", "tensor_mul", ctx_st[csl], PS(cb, 128, TT), rb_sb, inc=dv)
            bankg(cb, dv)
            ctx_guard[csl] = stg_store(
                "y", cc0_in[h * 128:(h + 1) * 128, qt * TT:(qt + 1) * TT],
                ctx_st[csl], dv, dv.n)

        prev2 = {"rb_mm": 0, "rb_dv": 0, "sums_rd": 0}
        for h in range(NH):
            hsl = h % 2
            if h + 1 < NH:
                ld_kv(h + 1)
            for qt in range(NTT):
                qi = h * NTT + qt
                cb = 5 if qi % 2 == 0 else 7
                if qi + 1 < NH * NTT:
                    nh_, nqt = divmod(qi + 1, NTT)
                    ld_q(nh_, nqt)
                nkc = (qt + 1) * NM
                S_kb.need("t", h)
                S_vb.need("t", h)
                S_qb.need("t", qi)
                kb.W("t", yS, cm_v)
                precast(1)
                ng = (nkc + 1) // 2
                emit_scores(hsl, qi % 2, qt, 0, nkc)
                if pending2[0] is not None:
                    pending2[0]()
                    pending2[0] = None
                first_av = True
                for g in range(ng):
                    if g + 1 < ng:
                        emit_scores(hsl, qi % 2, qt, g + 1, nkc)
                    kcs = [2 * g] if 2 * g == nkc - 1 else [2 * g, 2 * g + 1]
                    kb.W("t", ac, exp_guard[(g % 2) * 2])
                    if first_av:
                        kb.W("t", dv, prev2["sums_rd"])
                        bankw("t", cb)
                        first_av = False
                    for kc in kcs:
                        kb.E("t", "matmul", PS(4, 1, TT), ones_c_bf[:, :],
                             pbuf[:, kc * TT:(kc + 1) * TT],
                             start=(kc == 0), stop=(kc == nkc - 1),
                             inc=(mm if kc == nkc - 1 else None))
                        kb.E("t", "matmul", PS(cb, 128, TT),
                             vbuf[hsl][:, kc * 128:(kc + 1) * 128],
                             pbuf[:, kc * TT:(kc + 1) * TT],
                             start=(kc == 0), stop=(kc == nkc - 1),
                             inc=(mm if kc == nkc - 1 else None))
                av_mm = mm.n
                p_pe_guard = av_mm
                pending2[0] = (lambda h_=h, qt_=qt, cb_=cb, csl_=qi % 2,
                               av_=av_mm: tail2(h_, qt_, cb_, csl_, av_))
                S_qb.done(qi, mm, av_mm)
            S_kb.done(h, mm, mm.n)
            S_vb.done(h, mm, mm.n)
        if pending2[0] is not None:
            pending2[0]()
            pending2[0] = None

        for s in stg:
            kb.W("g", s, s.n)

        def fire_ag1(eng):
            eng.collective_compute(
                "AllGather", ALU.bypass,
                replica_groups=[[0, 1], [2, 3], [4, 5], [6, 7]],
                ins=[cc0_in.opt()], outs=[cc0_out.opt()],
            ).then_inc(ccs.h, 1)

        ccs.n += 1
        kb.q["g"].append(fire_ag1)
        arena.pop()
        kb.barrier(exclude=("ccs",))

        arena.push()
        xn_t = [arena.alloc(NF * TT) for _ in range(2)]
        OG = min(128, INTER)
        NOG = INTER // OG
        wg_sl = [arena.alloc(NF * OG) for _ in range(2)]
        wu_sl = [arena.alloc(NF * OG) for _ in range(2)]
        NICH = NIC // 2
        wd_sl = [arena.alloc(NICH * 128) for _ in range(2)]
        fuse = arena.alloc(NIC * TT)
        silu_st = [arena.alloc(TT, F32) for _ in range(2)]
        xo_t = [arena.alloc(TT, F32) for _ in range(2)]
        rst_sb = [arena.alloc(TT, F32) for _ in range(2)]
        rst_guard = [None, None]

        def ld_xn(tt):
            def em(sem):
                gdma_role(sem, xn_t[S_xg.i % 2].rearrange("p (f t) -> p f t", f=NF),
                          xn0_scr.tensor, tt * TT,
                          [[S, 128], [S * 128, NF], [1, TT]])
            return S_xg.load(em)

        def ld_wgu(og):
            def emg(sem):
                dma("g", sem, wg_sl[S_wA.i % 2],
                    fg8t[og * 128:(og + 1) * 128, :])

            def emu(sem):
                dma("g", sem, wu_sl[S_wB.i % 2],
                    fu8t[og * 128:(og + 1) * 128, :])

            return S_wA.load(emg), S_wB.load(emu)

        def ld_wd(fo, half):
            def em(sem):
                dma("g", sem, wd_sl[S_wC.i % 2],
                    fd8t[fo * 128:(fo + 1) * 128,
                         half * NICH * 128:(half + 1) * NICH * 128])
            return S_wC.load(em)

        def ld_xo(tt, fo):
            def em(sem):
                dma("g", sem, xo_t[S_xr.i % 2],
                    xoT[fo * 128:(fo + 1) * 128, tt * TT:(tt + 1) * TT])
            return S_xr.load(em)

        xn_i0 = ld_xn(0)
        wgu_idx = [ld_wgu(0)]
        wd_i0 = ld_wd(0, 0)
        ld_wd(0, 1)
        xo_i0 = ld_xo(0, 0)
        ctx_ex = {}
        silu_g = {4 + 0: 0, 4 + 1: 0, 0: 0, 1: 0}
        fuse_g = {2: 0, 3: 0, "s": 0}
        fuse_pe = 0

        for tt in range(NTTO):
            if tt == min(1, NTTO - 1) and "v" not in ctx_ex:
                kb.W("g", ccs, 1)
                gdma_role(gS, ctx_own, cc0_out.tensor, 0,
                          [[S, 2 * HD], [1, TOK]])
                ctx_ex["v"] = gS.n
            precast(-(-len(precast_jobs) // max(1, NTTO - tt)))
            xi = xn_i0 + tt
            if tt + 1 < NTTO:
                ld_xn(tt + 1)
            S_xg.need("t", xi)
            xnf = lambda f, s=tt % 2: xn_t[s][:, f * TT:(f + 1) * TT]

            for og in range(NOG):
                gidx0 = tt * NOG + og
                gi, gbi = wgu_idx[gidx0]
                S_wA.need("t", gi)
                S_wB.need("t", gbi)
                if gidx0 + 1 < NTTO * NOG:
                    wgu_idx.append(ld_wgu((gidx0 + 1) % NOG))
                for oc in range(OG // 128):
                    gc = og * (OG // 128) + oc
                    gb = oc % 2
                    ub = 2 + oc % 2
                    kb.W("t", ac, silu_g[gb])
                    kb.W("t", dv, fuse_g[ub])
                    wsl = gi % 2
                    usl = gbi % 2
                    mml(PS(gb, 128, TT),
                        [(wg_sl[wsl][:, f * OG + oc * 128:f * OG + (oc + 1) * 128],
                          xnf(f)) for f in range(NF)], inc=mm)
                    g_mm = mm.n
                    mml(PS(ub, 128, TT),
                        [(wu_sl[usl][:, f * OG + oc * 128:f * OG + (oc + 1) * 128],
                          xnf(f)) for f in range(NF)], inc=mm)
                    u_mm = mm.n
                    ssl = gc % 2
                    kb.W("a", mm, g_mm)
                    kb.W("a", dv, fuse_g.get(("s", ssl), 0))
                    kb.E("a", "activation", silu_st[ssl], PS(gb, 128, TT),
                         AF.Silu, bias=0.0, scale=sg_sb[:, gc:gc + 1], inc=ac)
                    silu_g[gb] = ac.n
                    kb.W("v", mm, u_mm)
                    kb.W("v", ac, ac.n)
                    kb.W("v", mm, fuse_pe)
                    kb.W("v", dv, dv.n)
                    kb.E("v", "scalar_tensor_tensor",
                         fuse[:, gc * TT:(gc + 1) * TT], PS(ub, 128, TT),
                         su_sb[:, gc:gc + 1], silu_st[ssl], ALU.mult, ALU.mult,
                         inc=dv)
                    fuse_g[ub] = dv.n
                    fuse_g[("s", ssl)] = dv.n
                S_wA.done(gi, mm, mm.n)
                S_wB.done(gbi, mm, mm.n)
            fuse_dv = dv.n

            for fo in range(NF):
                di0 = tt * NF + fo
                dia = wd_i0 + 2 * di0
                dib = dia + 1
                S_xr.need("v", xo_i0 + di0)
                if di0 + 1 < NTTO * NF:
                    ld_xo(*divmod(di0 + 1, NF))
                db = 4 + (fo % 2)
                kb.W("t", dv, fuse_dv)
                bankw("t", db)
                S_wC.need("t", dia)
                for ic in range(NICH):
                    kb.E("t", "matmul", PS(db, 128, TT),
                         wd_sl[dia % 2][:, ic * 128:(ic + 1) * 128],
                         fuse[:, ic * TT:(ic + 1) * TT],
                         start=(ic == 0), stop=False,
                         inc=(mm if ic == NICH - 1 else None))
                S_wC.done(dia, mm, mm.n)
                if di0 + 1 < NTTO * NF:
                    ld_wd((di0 + 1) % NF, 0)
                S_wC.need("t", dib)
                for ic in range(NICH):
                    gic = NICH + ic
                    kb.E("t", "matmul", PS(db, 128, TT),
                         wd_sl[dib % 2][:, ic * 128:(ic + 1) * 128],
                         fuse[:, gic * TT:(gic + 1) * TT],
                         start=False, stop=(ic == NICH - 1),
                         inc=(mm if ic == NICH - 1 else None))
                S_wC.done(dib, mm, mm.n)
                if di0 + 1 < NTTO * NF:
                    ld_wd((di0 + 1) % NF, 1)
                kb.W("v", mm, mm.n)
                rsl = di0 % 2
                if rst_guard[rsl] is not None:
                    kb.W("v", rst_guard[rsl][0], rst_guard[rsl][1])
                kb.W("v", dv, dv.n)
                kb.E("v", "scalar_tensor_tensor", rst_sb[rsl], PS(db, 128, TT),
                     sd_sb[:, fo:fo + 1], xo_t[(xo_i0 + di0) % 2],
                     ALU.mult, ALU.add, inc=dv)
                bankg(db, dv)
                S_xr.done(xo_i0 + di0, dv, dv.n)
                rst_guard[rsl] = stg_store(
                    "y", res_scr[fo * 128:(fo + 1) * 128, tt * TT:(tt + 1) * TT],
                    rst_sb[rsl], dv, dv.n)
            fuse_pe = mm.n
            S_xg.done(xi, mm, mm.n)

        arena.pop()
        kb.barrier(exclude=("ccs",))

        for qn in KB.QUEUES:
            kb.W(qn, ccs, 1)
        arena.push()
        ctx_t = [arena.alloc(NF * TT) for _ in range(2)]
        wo_sl = [arena.alloc(NF * 128) for _ in range(2)]
        x1_sb = arena.alloc(NF * TT, F32)
        xn1_sb = arena.alloc(NF * TT)
        res_t = [arena.alloc(TT, F32) for _ in range(2)]
        mu4 = arena.alloc(TT, F32)
        r4 = arena.alloc(TT, F32)
        tmp4 = arena.alloc(TT, F32)
        tmp4b = arena.alloc(TT, F32)
        tmp4g = arena.alloc(TT, F32)
        rcol = arena.alloc(4, F32)
        lg_sb = arena.alloc(E, F32)
        pr_sb = arena.alloc(E, F32)
        m1c = arena.alloc(8, F32)
        tmp8 = arena.alloc(E, F32)
        comb8 = arena.alloc(TOK, F32, parts=E)
        stg4 = [arena.alloc(2 * HARM) for _ in range(4)]

        def ld_ctx(tt):
            def em(sem):
                gdma_role(sem, ctx_t[S_xg.i % 2].rearrange("p (f t) -> p f t", f=NF),
                          cc0_out.tensor, tt * TT,
                          [[S, 128], [S * 128, NF], [1, TT]])
            return S_xg.load(em)

        def ld_wo(oc):
            def em(sem):
                dma("g", sem, wo_sl[S_wA.i % 2],
                    wo_t[oc * 128:(oc + 1) * 128, :])
            return S_wA.load(em)

        def ld_res(tt, fo):
            def em(sem):
                dma("y", sem, res_t[S_sL.i % 2],
                    res_scr[fo * 128:(fo + 1) * 128, tt * TT:(tt + 1) * TT])
            return S_sL.load(em)

        ctx_i0 = ld_ctx(0)
        wo_i0 = ld_wo(0)
        res_i0 = ld_res(0, 0)
        x1_consumed = 0
        x1_pe = 0
        x1_store_g = []
        xn1_pe = 0

        for tt in range(NTTO):
            ci = ctx_i0 + tt
            if tt + 1 < NTTO:
                ld_ctx(tt + 1)
            S_xg.need("t", ci)
            ctxf = lambda f, s=tt % 2: ctx_t[s][:, f * TT:(f + 1) * TT]

            for oc in range(NF):
                oi = tt * NF + oc
                S_wA.need("t", wo_i0 + oi)
                S_sL.need("v", res_i0 + oi)
                if oi + 1 < NTTO * NF:
                    ld_wo((oi + 1) % NF)
                    ld_res(*divmod(oi + 1, NF))
                ob = oc % 2
                bankw("t", ob)
                wsl = (wo_i0 + oi) % 2
                mml(PS(ob, 128, TT),
                    [(wo_sl[wsl][:, f * 128:(f + 1) * 128], ctxf(f))
                     for f in range(NF)], inc=mm)
                S_wA.done(wo_i0 + oi, mm, mm.n)
                kb.W("v", mm, mm.n)
                kb.W("v", dv, x1_consumed)
                kb.W("v", mm, x1_pe)
                kb.W("v", mm, xn1_pe)
                for s_, v_ in x1_store_g:
                    kb.W("v", s_, v_)
                kb.W("v", dv, dv.n)
                kb.E("v", "tensor_add", x1_sb[:, oc * TT:(oc + 1) * TT],
                     PS(ob, 128, TT), res_t[(res_i0 + oi) % 2], inc=dv)
                bankg(ob, dv)
                S_sL.done(res_i0 + oi, dv, dv.n)
            x1_dv = dv.n
            S_xg.done(ci, mm, mm.n)

            kb.W("t", dv, x1_dv)
            bankw("t", 2)
            bankw("t", 3)
            mml(PS(2, 1, TT), [(ones_c_f[:, :], x1_sb[:, f * TT:(f + 1) * TT])
                               for f in range(NF)], inc=mm)
            sq_pe4 = [0, 0]
            for f in range(NF):
                tb4 = tmp4 if f % 2 == 0 else tmp4b
                kb.W("a", dv, x1_dv)
                kb.W("a", mm, sq_pe4[f % 2])
                kb.E("a", "activation", tb4, x1_sb[:, f * TT:(f + 1) * TT],
                     AF.Square, bias=0.0, scale=1.0, inc=ac)
                kb.W("t", ac, ac.n)
                kb.E("t", "matmul", PS(3, 1, TT), ones_c_f[:, :], tb4,
                     start=(f == 0), stop=(f == NF - 1), inc=mm)
                sq_pe4[f % 2] = mm.n
            sums4 = mm.n
            kb.W("a", mm, sums4)
            kb.E("a", "activation", row(0), PS(2, 1, TT), AF.Copy, bias=0.0,
                 scale=1.0 / D, inc=ac)
            kb.E("a", "activation", row(1), PS(3, 1, TT), AF.Copy, bias=0.0,
                 scale=1.0 / D, inc=ac)
            bankg(2, ac)
            bankg(3, ac)
            kb.W("a", ac, ac.n)
            kb.E("a", "activation", row(2), row(0), AF.Square, bias=0.0,
                 scale=1.0, inc=ac)
            kb.W("v", ac, ac.n)
            kb.W("v", dv, dv.n)
            kb.E("v", "tensor_sub", row(3), row(1), row(2), inc=dv)
            kb.W("a", dv, dv.n)
            kb.E("a", "activation", row(4), row(3), AF.Ln, bias=cns[0:1, 1:2],
                 scale=1.0, inc=ac)
            kb.W("a", ac, ac.n)
            kb.E("a", "activation", row(5), row(4), AF.Exp, bias=0.0, scale=-0.5,
                 inc=ac)
            r4_ac = ac.n
            kb.W("v", ac, r4_ac)
            kb.W("v", dv, dv.n)
            kb.E("v", "tensor_mul", row(6), row(0), row(5), inc=dv)
            mr_dv = dv.n

            kb.W("t", ac, r4_ac)
            bankw("t", 4)
            kb.E("t", "matmul", PS(4, 128, TT), ones_r_f[:, :], row(0),
                 start=True, stop=True, inc=mm)
            bankw("t", 5)
            kb.E("t", "matmul", PS(5, 128, TT), ones_r_f[:, :], row(5),
                 start=True, stop=True, inc=mm)
            kb.W("v", mm, mm.n)
            kb.W("v", dv, dv.n)
            kb.E("v", "tensor_copy", mu4, PS(4, 128, TT), inc=dv)
            kb.W("v", dv, dv.n)
            kb.E("v", "tensor_copy", r4, PS(5, 128, TT), inc=dv)
            bankg(4, dv)
            bankg(5, dv)
            bc4 = dv.n

            kb.W("v", dv, bc4)
            kb.W("g", dv, bc4)
            kb.W("g", pl, pl.n)
            for f in range(NF):
                if f % 2 == 0:
                    kb.W("v", dv, dv.n)
                    kb.E("v", "tensor_sub", tmp4,
                         x1_sb[:, f * TT:(f + 1) * TT], mu4, inc=dv)
                    kb.W("v", dv, dv.n)
                    kb.E("v", "tensor_mul", xn1_sb[:, f * TT:(f + 1) * TT],
                         tmp4, r4, inc=dv)
                else:
                    kb.W("g", pl, pl.n)
                    kb.E("g", "tensor_sub", tmp4g,
                         x1_sb[:, f * TT:(f + 1) * TT], mu4, inc=pl)
                    kb.W("g", pl, pl.n)
                    kb.E("g", "tensor_mul", xn1_sb[:, f * TT:(f + 1) * TT],
                         tmp4g, r4, inc=pl)
            xn1_dv = dv.n
            xn1_pl = pl.n

            kb.W("y", pl, xn1_pl)
            x1_store_g = [
                stg_store("y",
                          x1_scr[:, tt * TT:(tt + 1) * TT].rearrange(
                              "(f p) t -> p f t", p=128),
                          x1_sb.rearrange("p (f t) -> p f t", f=NF), dv, xn1_dv),
                stg_store("y",
                          xn1_scr[:, tt * TT:(tt + 1) * TT].rearrange(
                              "(f p) t -> p f t", p=128),
                          xn1_sb.rearrange("p (f t) -> p f t", f=NF), dv, xn1_dv),
            ]

            kb.W("t", dv, xn1_dv)
            kb.W("t", pl, xn1_pl)
            bankw("t", 6)
            mml(PSW(6, 0, HARM, TT),
                [(b4v[:, f, 0, :], xn1_sb[:, f * TT:(f + 1) * TT])
                 for f in range(NF)], inc=mm)
            kb.W("v", mm, mm.n)
            kb.W("v", dv, dv.n)
            kb.E("v", "tensor_copy", resq_sb[:, tt * TT:(tt + 1) * TT],
                 PSW(6, 0, HARM, TT), inc=dv)
            bankg(6, dv)

            for sub in range(TT // 128):
                rb_ = sub % 2
                bankw("t", rb_)
                mml(PS(rb_, 128, 2 * HARM),
                    [(xn1_sb[:, f * TT + sub * 128:f * TT + (sub + 1) * 128],
                      bkv_sb[:, f * 2 * HARM:(f + 1) * 2 * HARM])
                     for f in range(NF)], inc=mm)
                kb.W("v", mm, mm.n)
                ss = stg_state["i"]
                kb.W("v", stg[ss], stg_free[ss])
                kb.W("v", dv, dv.n)
                kb.E("v", "tensor_copy", stg4[ss], PS(rb_, 128, 2 * HARM), inc=dv)
                bankg(rb_, dv)
                stg_store("y",
                          cc1_in[tt * TT + sub * 128:tt * TT + (sub + 1) * 128, :],
                          stg4[ss], dv, dv.n)

            for sub in range(TT // 128):
                lb = 6 if sub % 2 == 0 else 7
                tb = 7 if lb == 6 else 6
                bankw("t", lb)
                kb.W("t", dv, mr_dv)
                for f in range(NF):
                    kb.E("t", "matmul", PS(lb, 128, E),
                         x1_sb[:, f * TT + sub * 128:f * TT + (sub + 1) * 128],
                         wr_sb[:, f * E:(f + 1) * E],
                         start=(f == 0), stop=False)
                kb.E("t", "matmul", PS(lb, 128, E),
                     row(6)[:, sub * 128:(sub + 1) * 128], ncs_wr[:, :],
                     start=False, stop=True, inc=mm)
                lg_mm = mm.n
                bankw("t", tb)
                kb.E("t", "matmul", PSW(tb, 0, 128, 1),
                     row(5)[:, sub * 128:(sub + 1) * 128], ones_c_f[0:1, 0:1],
                     start=True, stop=True, inc=mm)
                kb.W("v", mm, mm.n)
                kb.W("v", dv, dv.n)
                kb.E("v", "tensor_copy", rcol[:, 0:1], PSW(tb, 0, 128, 1), inc=dv)
                bankg(tb, dv)
                kb.W("v", mm, lg_mm)
                kb.W("v", dv, dv.n)
                kb.E("v", "tensor_scalar", lg_sb, PS(lb, 128, E), rcol[:, 0:1],
                     None, ALU.mult, inc=dv)
                bankg(lb, dv)
                kb.W("v", dv, dv.n)
                kb.E("v", "reduce_max", m1c[:, 0:1], lg_sb, axis=AX, negate=True,
                     inc=dv)
                kb.W("a", dv, dv.n)
                kb.E("a", "activation", pr_sb, lg_sb, AF.Exp,
                     bias=m1c[:, 0:1], scale=1.0, accum_out=m1c[:, 1:2], inc=ac)
                kb.W("v", ac, ac.n)
                kb.E("v", "reciprocal", m1c[:, 2:3], m1c[:, 1:2], inc=dv)
                kb.W("v", dv, dv.n)
                kb.E("v", "tensor_scalar", pr_sb, pr_sb, m1c[:, 2:3], None,
                     ALU.mult, inc=dv)
                kb.W("v", dv, dv.n)
                kb.E("v", "reduce_max", m1c[:, 3:4], pr_sb, axis=AX, inc=dv)
                kb.W("v", dv, dv.n)
                kb.E("v", "tensor_scalar", tmp8, pr_sb, m1c[:, 3:4], None,
                     ALU.is_ge, inc=dv)
                kb.W("v", dv, dv.n)
                kb.E("v", "scalar_tensor_tensor", tmp8, tmp8, -1e9, pr_sb,
                     ALU.mult, ALU.add, inc=dv)
                kb.W("v", dv, dv.n)
                kb.E("v", "reduce_max", m1c[:, 4:5], tmp8, axis=AX, inc=dv)
                kb.W("v", dv, dv.n)
                kb.E("v", "tensor_scalar", tmp8, pr_sb, m1c[:, 4:5], None,
                     ALU.is_ge, inc=dv)
                kb.W("v", dv, dv.n)
                kb.E("v", "tensor_mul", lg_sb, pr_sb, tmp8, inc=dv)
                comb_dv = dv.n
                kb.W("t", dv, comb_dv)
                bankw("t", tb)
                kb.E("t", "transpose", PSW(tb, 0, E, 128), lg_sb, id_f32[:, :],
                     inc=mm)
                kb.W("v", mm, mm.n)
                kb.W("v", dv, dv.n)
                kb.E("v", "tensor_copy",
                     comb8[:, tt * TT + sub * 128:tt * TT + (sub + 1) * 128],
                     PSW(tb, 0, E, 128), inc=dv)
                bankg(tb, dv)
            x1_consumed = dv.n
            x1_pe = mm.n
            xn1_pe = mm.n

        stg_store("y", comb_scr, comb8[:, :], dv, dv.n)
        for s in stg:
            kb.W("g", s, s.n)

        def fire_ag2(eng):
            eng.collective_compute(
                "AllGather", ALU.bypass,
                replica_groups=[[0, 1], [2, 3], [4, 5], [6, 7]],
                ins=[cc1_in.opt()], outs=[cc1_out.opt()],
            ).then_inc(ccs.h, 1)

        ccs.n += 1
        kb.q["g"].append(fire_ag2)
        arena.pop()
        kb.barrier(exclude=("ccs",))

        for qn in KB.QUEUES:
            kb.W(qn, ccs, 2)
        arena.push()
        rtmp = arena.alloc(NKL * HARM)
        sqf = arena.alloc(TT, F32, parts=HARM)

        def ld_rtmp(sem):
            dma("y", sem, rtmp.rearrange("p (kc h) -> p kc h", kc=NKL),
                cc1_out[:, 0:HARM].rearrange("(kc p) h -> p kc h", p=128))

        def ld_rva(sem):
            dma("y", sem,
                rva_sb.rearrange("p (kc x) -> p kc x", x=RVW)[:, :, 0:HARM],
                cc1_out[:, HARM:2 * HARM].rearrange("(kc p) h -> p kc h", p=128))

        rt_i = S_sL.load(ld_rtmp)
        rv_i = S_sL.load(ld_rva)
        S_sL.need("v", rt_i)
        S_sL.need("v", rv_i)
        kb.E("v", "memset",
             rva_sb.rearrange("p (kc x) -> p kc x", x=RVW)[:, :,
                                                          HARM:HARM + 1],
             1.0, inc=dv)
        kb.W("v", dv, dv.n)
        kb.E("v", "memset", rka_sb[HARM:HARM + 1, :], 1.0, inc=dv)
        rva_dv = dv.n
        S_sL.need("t", rt_i)
        S_sL.need("t", rv_i)
        for kc in range(NKL):
            tb = kc % 2
            bankw("t", tb)
            kb.E("t", "transpose",
                 PS(tb).bitcast(BF16)[0:HARM, 0:128],
                 rtmp[:, kc * HARM:(kc + 1) * HARM], id_bf[:, :], inc=mm)
            kb.W("v", mm, mm.n)
            kb.W("v", dv, dv.n)
            kb.E("v", "tensor_copy", rka_sb[0:HARM, kc * 128:(kc + 1) * 128],
                 PS(tb).bitcast(BF16)[0:HARM, 0:128], inc=dv)
            bankg(tb, dv)
        rkf_dv = dv.n
        S_sL.done(rt_i, mm, mm.n)
        S_sL.done(rv_i, dv, dv.n)

        CH = TT
        kb.W("t", dv, rkf_dv)
        for sc_ in range(S // CH):
            sl_ = rka_sb[0:HARM, sc_ * CH:(sc_ + 1) * CH]
            bankw("t", 4)
            kb.W("t", dv, dv.n)
            kb.E("t", "matmul", PSW(4, 0, 1, CH), ones_c_bf[0:HARM, 0:1], sl_,
                 start=True, stop=True, inc=mm)
            kb.W("a", mm, mm.n)
            kb.W("a", ac, ac.n)
            kb.E("a", "activation", sqf[:, 0:CH], sl_, AF.Square, bias=0.0,
                 scale=1.0, inc=ac)
            kb.W("t", ac, ac.n)
            bankw("t", 5)
            kb.E("t", "matmul", PSW(5, 0, 1, CH), ones_c_f[0:HARM, 0:1],
                 sqf[:, 0:CH], start=True, stop=True, inc=mm)
            kb.W("a", mm, mm.n)
            kb.E("a", "activation", row(0)[:, 0:CH], PSW(4, 0, 1, CH), AF.Copy,
                 bias=0.0, scale=1.0, inc=ac)
            bankg(4, ac)
            kb.W("a", ac, ac.n)
            kb.E("a", "activation", row(1)[:, 0:CH], row(0)[:, 0:CH], AF.Square,
                 bias=0.0, scale=1.0, inc=ac)
            kb.W("v", ac, ac.n)
            kb.W("v", mm, mm.n)
            kb.W("v", dv, dv.n)
            kb.E("v", "scalar_tensor_tensor", row(1)[:, 0:CH], row(1)[:, 0:CH],
                 -1.0 / 64.0, PSW(5, 0, 1, CH), ALU.mult, ALU.add, inc=dv)
            bankg(5, dv)
            kb.W("v", dv, dv.n)
            kb.E("v", "reduce_max", bstat[:, 3:4], row(1)[:, 0:CH], axis=AX,
                 inc=dv)
            kb.W("v", dv, dv.n)
            kb.E("v", "reduce_max", bstat[:, 4:5], row(0)[:, 0:CH], axis=AX,
                 inc=dv)
            kb.W("v", dv, dv.n)
            kb.E("v", "tensor_scalar", row(1)[:, 0:CH], row(0)[:, 0:CH], -1.0,
                 None, ALU.mult, inc=dv)
            kb.W("v", dv, dv.n)
            kb.E("v", "reduce_max", bstat[:, 6:7], row(1)[:, 0:CH], axis=AX,
                 inc=dv)
            kb.W("v", dv, dv.n)
            if sc_ == 0:
                kb.E("v", "tensor_copy", bstat[:, 0:1], bstat[:, 4:5], inc=dv)
                kb.W("v", dv, dv.n)
                kb.E("v", "tensor_copy", bstat[:, 1:2], bstat[:, 6:7], inc=dv)
                kb.W("v", dv, dv.n)
                kb.E("v", "tensor_copy", bstat[:, 2:3], bstat[:, 3:4], inc=dv)
            else:
                kb.E("v", "tensor_max", bstat[:, 0:1], bstat[:, 0:1],
                     bstat[:, 4:5], inc=dv)
                kb.W("v", dv, dv.n)
                kb.E("v", "tensor_max", bstat[:, 1:2], bstat[:, 1:2],
                     bstat[:, 6:7], inc=dv)
                kb.W("v", dv, dv.n)
                kb.E("v", "tensor_max", bstat[:, 2:3], bstat[:, 2:3],
                     bstat[:, 3:4], inc=dv)
            kb.W("v", dv, dv.n)
        kb.E("v", "tensor_scalar", bstat[:, 0:1], bstat[:, 0:1], 1.0 / 64.0,
             None, ALU.mult, inc=dv)
        kb.W("v", dv, dv.n)
        kb.E("v", "tensor_scalar", bstat[:, 1:2], bstat[:, 1:2], -1.0 / 64.0,
             None, ALU.mult, inc=dv)
        ch_dv = dv.n
        arena.pop()
        kb.barrier()


        for pss in range(PASSES):
            q0 = pss * TOKP
            GW = min(256, DE, DS)
            arena.push()
            res2 = arena.alloc(NF * TOKP, F32)
            xn1_t = arena.alloc(NF * TOKP)
            tmpa2 = [arena.alloc(TT)[0:HARM + 1, :] for _ in range(2)]
            sq5 = arena.alloc(TT, F32, parts=HARM)
            ubuf = arena.alloc(max(NKL, NDS, NDE) * TT)
            pb5 = ubuf[:, 0:NKL * TT]
            fuse5 = ubuf[:, 0:max(NDS, NDE) * TT]
            ctx64 = arena.alloc(TT)[0:64, :]
            outh = arena.alloc(TT)
            ores = [arena.alloc(TT) for _ in range(NTP)]
            rb5 = arena.alloc(TT, F32)
            x1t5 = [arena.alloc(TT, F32) for _ in range(2)]
            silu5 = [arena.alloc(TT, F32) for _ in range(2)]
            silu5b = arena.alloc(TT, F32)
            cbb = arena.alloc(TT, F32)
            crow2 = [arena.alloc(TT, F32, parts=1) for _ in range(2)]
            wgs = [arena.alloc(NF * GW) for _ in range(2)]
            wus = [arena.alloc(NF * GW) for _ in range(2)]
            wds = [arena.alloc(max(NDS, NDE) * 128) for _ in range(2)]

            exp_g5 = {0: 0, 2: 0}
            p5_pe = 0
            tmpa_pe = [0, 0]
            tmpa_done = [0, 0]
            idxs = [(h, tp) for h in range(NHL1) for tp in range(NTP)]

            kb.W("t", dv, rva_dv)
            kb.W("t", dv, ch_dv)

            def emit_tmpbound(i):
                h, tp = idxs[i]
                sl = i % 2
                bankw("t", 7)
                kb.W("v", mm, tmpa_pe[sl])
                kb.E("t", "matmul", PSW(7, 0, HARM, TT),
                     nh_sb[:, h * HARM:(h + 1) * HARM],
                     resq_sb[:, q0 + tp * TT:q0 + (tp + 1) * TT],
                     start=True, stop=True, inc=mm)
                kb.W("v", mm, mm.n)
                kb.W("v", dv, dv.n)
                kb.E("v", "tensor_copy", tmpa2[sl][0:HARM, :],
                     PSW(7, 0, HARM, TT), inc=dv)
                bankg(7, dv)
                tmp_dv = dv.n
                kb.W("t", dv, tmp_dv)
                bankw("t", 7)
                kb.E("t", "matmul", PSW(7, 0, 1, TT), ones_c_bf[0:HARM, 0:1],
                     tmpa2[sl][0:HARM, :], start=True, stop=True, inc=mm)
                w_mm = mm.n
                kb.W("a", dv, tmp_dv)
                kb.W("a", ac, ac.n)
                kb.E("a", "activation", sq5, tmpa2[sl][0:HARM, :], AF.Square,
                     bias=0.0, scale=1.0, inc=ac)
                kb.W("a", mm, w_mm)
                kb.E("a", "activation", row(0), PSW(7, 0, 1, TT), AF.Copy,
                     bias=0.0, scale=1.0, inc=ac)
                w_ac = ac.n
                kb.W("t", ac, w_ac)
                kb.E("t", "matmul", PSW(7, 0, 1, TT), ones_c_f[0:HARM, 0:1],
                     sq5, start=True, stop=True, inc=mm)
                t2_mm = mm.n
                kb.W("a", mm, t2_mm)
                kb.W("a", ac, ac.n)
                kb.E("a", "activation", row(1), PSW(7, 0, 1, TT), AF.Copy,
                     bias=0.0, scale=1.0, inc=ac)
                bankg(7, ac)
                rows_ac = ac.n
                kb.W("v", ac, rows_ac)
                kb.W("v", mm, prev5["rbmm5"])
                kb.W("v", dv, dv.n)
                kb.E("v", "tensor_scalar", row(3), row(0), bstat[:, 0:1], None,
                     ALU.mult, inc=dv)
                kb.W("v", dv, dv.n)
                kb.E("v", "tensor_scalar", row(2), row(0), bstat[:, 1:2], None,
                     ALU.mult, inc=dv)
                kb.W("v", dv, dv.n)
                kb.E("v", "tensor_max", row(3), row(3), row(2), inc=dv)
                kb.W("a", dv, dv.n)
                kb.E("a", "activation", row(2), row(0), AF.Square, bias=0.0,
                     scale=0.125, inc=ac)
                kb.W("v", ac, ac.n)
                kb.W("v", dv, dv.n)
                kb.E("v", "tensor_sub", row(2), row(1), row(2), inc=dv)
                kb.W("v", dv, dv.n)
                kb.E("v", "tensor_scalar_max", row(2), row(2), 0.0, inc=dv)
                kb.W("v", dv, dv.n)
                kb.E("v", "tensor_scalar", row(2), row(2), bstat[:, 2:3], None,
                     ALU.mult, inc=dv)
                kb.W("a", dv, dv.n)
                kb.E("a", "activation", row(2), row(2), AF.Sqrt, bias=0.0,
                     scale=1.0, inc=ac)
                kb.W("v", ac, ac.n)
                kb.W("v", dv, dv.n)
                kb.E("v", "tensor_add", row(3), row(3), row(2), inc=dv)
                kb.W("v", dv, dv.n)
                kb.E("v", "tensor_scalar", row(3), row(3), -1.0, -2.0,
                     ALU.mult, ALU.add, inc=dv)
                kb.W("v", dv, dv.n)
                kb.E("v", "tensor_copy", tmpa2[sl][HARM:HARM + 1, :], row(3),
                     inc=dv)
                tmpa_done[sl] = dv.n

            def emit_scores5(i, g, ng):
                sl = i % 2
                k0 = 2 * g
                kcs = [k0] if k0 == NKL - 1 else [k0, k0 + 1]
                pb_ = (g % 2) * 2
                kb.W("t", ac, exp_g5[pb_])
                if g == 0:
                    kb.W("t", dv, tmpa_done[sl])
                for j, kc in enumerate(kcs):
                    kb.E("t", "matmul", PS(pb_ + j, 128, TT),
                         rka_sb[:, kc * 128:(kc + 1) * 128],
                         tmpa2[sl], start=True, stop=True, inc=mm)
                kb.W("a", mm, mm.n)
                kb.W("a", mm, p5_pe)
                nk_ = len(kcs)
                src_ = psum[0:128, pb_ * 512:(pb_ + 2) * 512].rearrange(
                    "p (g r) -> p g r", g=2)[:, 0:nk_, 0:TT]
                dst_ = pb5[:, k0 * TT:(k0 + nk_) * TT].rearrange(
                    "p (g t) -> p g t", g=nk_)
                kb.E("a", "activation", dst_, src_, AF.Exp, bias=0.0,
                     scale=ISCALE, inc=ac)
                exp_g5[pb_] = ac.n
                return kcs

            ng5 = (NKL + 1) // 2
            pending5 = [None]
            prev5 = {"rb5": 0, "rbmm5": 0, "outh_pe": 0}
            emit_tmpbound(0)

            def tail5(i, h, tp, ctx_mm):
                kb.W("v", mm, ctx_mm)
                kb.W("v", dv, dv.n)
                kb.E("v", "tensor_copy", ctx64, PSW(4, 0, HARM, TT), inc=dv)
                kb.W("v", dv, dv.n)
                kb.E("v", "reciprocal", row(7), PSW(4, HARM, HARM + 1, TT),
                     inc=dv)
                ctx_dv = dv.n
                kb.W("t", dv, ctx_dv)
                kb.W("t", dv, prev5["rb5"])
                kb.E("t", "matmul", PS(4, 128, TT), ones_r_f[:, :], row(7),
                     start=True, stop=True, inc=mm)
                rbmm = mm.n
                prev5["rbmm5"] = rbmm
                bankw("t", 5)
                kb.E("t", "matmul", PS(5, 128, TT),
                     wTvo_sb[:, h * 128:(h + 1) * 128],
                     ctx64, start=True, stop=True, inc=mm)
                oh_mm = mm.n
                kb.W("v", mm, rbmm)
                kb.W("v", dv, dv.n)
                kb.E("v", "tensor_copy", rb5, PS(4, 128, TT), inc=dv)
                bankg(4, dv)
                prev5["rb5"] = dv.n
                kb.W("v", mm, oh_mm)
                kb.W("v", mm, prev5.get("outh_pe", 0))
                kb.W("v", dv, dv.n)
                kb.E("v", "tensor_mul", outh, PS(5, 128, TT), rb5, inc=dv)
                bankg(5, dv)
                outh_dv = dv.n
                kb.W("t", dv, outh_dv)
                if h == 0:
                    bankw("t", 6)
                kb.E("t", "matmul", PSW(6, 0, HARM, TT),
                     b4v[:, h, 1, :], outh,
                     start=(h == 0), stop=(h == NHL1 - 1), inc=mm)
                prev5["outh_pe"] = mm.n

            for i, (h, tp) in enumerate(idxs):
                emit_scores5(i, 0, ng5)
                if pending5[0] is not None:
                    pending5[0]()
                    pending5[0] = None
                for g in range(ng5):
                    if g + 1 < ng5:
                        emit_scores5(i, g + 1, ng5)
                    if g == 2 and i + 1 < len(idxs):
                        emit_tmpbound(i + 1)
                    kcs = [2 * g] if 2 * g == NKL - 1 else [2 * g, 2 * g + 1]
                    kb.W("t", ac, exp_g5[(g % 2) * 2])
                    if g == 0:
                        bankw("t", 4)
                    for kc in kcs:
                        kb.E("t", "matmul", PSW(4, 0, HARM + 1, TT),
                             rva_sb[:, kc * RVW:kc * RVW + HARM + 1],
                             pb5[:, kc * TT:(kc + 1) * TT],
                             start=(kc == 0), stop=(kc == NKL - 1),
                             inc=(mm if kc == NKL - 1 else None))
                if ng5 <= 2 and i + 1 < len(idxs):
                    emit_tmpbound(i + 1)
                ctx_mm = mm.n
                p5_pe = ctx_mm
                tmpa_pe[i % 2] = ctx_mm
                pending5[0] = (lambda i_=i, h_=h, tp_=tp, c_=ctx_mm:
                               tail5(i_, h_, tp_, c_))
            if pending5[0] is not None:
                pending5[0]()
                pending5[0] = None
            ores_mm = mm.n

            for tp in range(NTP):
                kb.W("v", mm, ores_mm)
                kb.W("v", dv, dv.n)
                kb.E("v", "tensor_copy", ores[tp][0:HARM, 0:TT],
                     PSW(6 + tp, 0, HARM, TT), inc=dv)
                bankg(6 + tp, dv)
            ores_dv = dv.n

            def ld_x1(tt, fo):
                def em(sem):
                    dma("y", sem, x1t5[S_sL.i % 2],
                        x1_scr[fo * 128:(fo + 1) * 128,
                               q0 + tt * TT:q0 + (tt + 1) * TT])
                return S_sL.load(em)

            x1_i0 = ld_x1(0, 0)
            for tp in range(NTP):
                for oc in range(NF):
                    ii = tp * NF + oc
                    S_sL.need("v", x1_i0 + ii)
                    if ii + 1 < NTP * NF:
                        ld_x1(*divmod(ii + 1, NF))
                    b_ = oc % 2
                    kb.W("t", dv, ores_dv)
                    bankw("t", b_)
                    kb.E("t", "matmul", PS(b_, 128, TT),
                         wTvo_sb[:, D + oc * 128:D + (oc + 1) * 128],
                         ores[tp][0:HARM, 0:TT], start=True, stop=True, inc=mm)
                    kb.W("v", mm, mm.n)
                    kb.W("v", dv, dv.n)
                    kb.E("v", "tensor_add",
                         res2[:, oc * TOKP + tp * TT:oc * TOKP + (tp + 1) * TT],
                         PS(b_, 128, TT), x1t5[(x1_i0 + ii) % 2], inc=dv)
                    bankg(b_, dv)
                    S_sL.done(x1_i0 + ii, dv, dv.n)

            def ld_xn1(sem):
                dma("y", sem, xn1_t.rearrange("p (f t) -> p f t", f=NF),
                    xn1_scr[:, q0:q0 + TOKP].rearrange("(f p) t -> p f t", p=128))

            xn1_i = S_kb.load(ld_xn1)
            S_kb.need("t", xn1_i)

            state = {"silu": {0: 0, 1: 0}, "fuse": {2: 0, 3: 0, "s": 0},
                     "fuse_pe": p5_pe, "cbb_dv": 0, "cbb_cons": 0}

            def mk_ldg(w_dram, groups, bufs, stream):
                def ld(ig):
                    og0, gw = groups[ig]

                    def em(sem):
                        dma(stream.qn, sem, bufs[stream.i % 2][:, 0:NF * gw],
                            w_dram[:, NF * og0:NF * og0 + NF * gw])
                    return stream.load(em)
                return ld

            def mk_ldd(w_dram, nic_, stream):
                def ld(fo):
                    def em(sem):
                        dma(stream.qn, sem, wds[stream.i % 2][:, 0:nic_ * 128],
                            w_dram[:, fo * nic_ * 128:(fo + 1) * nic_ * 128])
                    return stream.load(em)
                return ld

            def ffn_block(n_out_chunks, groups, ldg, ldu, ldd, use_comb,
                          e_idx, stA=None, stB=None, stC=None):
                for tp in range(NTP):
                    xo5 = lambda f: xn1_t[:, f * TOKP + tp * TT:
                                          f * TOKP + (tp + 1) * TT]
                    if use_comb:
                        def ld_c(sem):
                            dma("y", sem, crow2[S_sL.i % 2],
                                comb_scr[e_idx:e_idx + 1,
                                         q0 + tp * TT:q0 + (tp + 1) * TT])
                        ci_ = S_sL.load(ld_c)
                        S_sL.need("t", ci_)
                        bankw("t", 7)
                        kb.E("t", "matmul", PS(7, 128, TT), ones_r_f[:, :],
                             crow2[ci_ % 2], start=True, stop=True, inc=mm)
                        S_sL.done(ci_, mm, mm.n)
                        kb.W("v", mm, mm.n)
                        kb.W("v", dv, state["cbb_cons"])
                        kb.W("v", dv, dv.n)
                        kb.E("v", "tensor_copy", cbb, PS(7, 128, TT), inc=dv)
                        bankg(7, dv)
                        state["cbb_dv"] = dv.n
                    gis = []
                    for ig in range(len(groups)):
                        gis.append((ldg(ig), ldu(ig)))
                    for ig, (og0, gw) in enumerate(groups):
                        gi_, ui_ = gis[ig]
                        stA.need("t", gi_)
                        stB.need("t", ui_)
                        for oc in range(gw // 128):
                            gc = og0 // 128 + oc
                            gb = oc % 2
                            ub = 2 + oc % 2
                            ssl = gc % 2
                            kb.W("t", ac, state["silu"][gb])
                            kb.W("t", dv, state["fuse"][ub])
                            bankw("t", gb)
                            bankw("t", ub)
                            mml(PS(gb, 128, TT),
                                [(wgs[gi_ % 2][:, f * gw + oc * 128:
                                               f * gw + (oc + 1) * 128], xo5(f))
                                 for f in range(NF)], inc=mm)
                            g_mm = mm.n
                            mml(PS(ub, 128, TT),
                                [(wus[ui_ % 2][:, f * gw + oc * 128:
                                               f * gw + (oc + 1) * 128], xo5(f))
                                 for f in range(NF)], inc=mm)
                            u_mm = mm.n
                            kb.W("a", mm, g_mm)
                            kb.W("a", dv, state["fuse"].get(("s", ssl), 0))
                            kb.E("a", "activation", silu5[ssl], PS(gb, 128, TT),
                                 AF.Silu, bias=0.0, scale=1.0, inc=ac)
                            state["silu"][gb] = ac.n
                            bankg(gb, ac)
                            kb.W("v", ac, ac.n)
                            if use_comb:
                                kb.W("v", dv, state["cbb_dv"])
                                kb.W("v", dv, dv.n)
                                kb.E("v", "tensor_mul", silu5b, silu5[ssl], cbb,
                                     inc=dv)
                                state["cbb_cons"] = dv.n
                                sl_use = silu5b
                            else:
                                sl_use = silu5[ssl]
                            kb.W("v", mm, u_mm)
                            kb.W("v", mm, state["fuse_pe"])
                            kb.W("v", dv, dv.n)
                            kb.E("v", "tensor_mul",
                                 fuse5[:, gc * TT:(gc + 1) * TT],
                                 PS(ub, 128, TT), sl_use, inc=dv)
                            state["fuse"][ub] = dv.n
                            state[("fuse", "s", ssl)] = dv.n
                            state["fuse"][("s", ssl)] = dv.n
                            bankg(ub, dv)
                            state["fuse"]["s"] = dv.n
                        stA.done(gi_, mm, mm.n)
                        stB.done(ui_, mm, mm.n)
                    fuse_dv_ = dv.n
                    dls = [ldd(0)]
                    for fo in range(NF):
                        di_ = dls[fo]
                        if fo + 1 < NF:
                            dls.append(ldd(fo + 1))
                        stC.need("t", di_)
                        db = 4 + fo % 2
                        kb.W("t", dv, fuse_dv_)
                        bankw("t", db)
                        mml(PS(db, 128, TT),
                            [(wds[di_ % 2][:, ic * 128:(ic + 1) * 128],
                              fuse5[:, ic * TT:(ic + 1) * TT])
                             for ic in range(n_out_chunks)], inc=mm)
                        stC.done(di_, mm, mm.n)
                        kb.W("v", mm, mm.n)
                        kb.W("v", dv, dv.n)
                        kb.E("v", "tensor_add",
                             res2[:, fo * TOKP + tp * TT:
                                  fo * TOKP + (tp + 1) * TT],
                             PS(db, 128, TT),
                             res2[:, fo * TOKP + tp * TT:
                                  fo * TOKP + (tp + 1) * TT], inc=dv)
                        bankg(db, dv)
                    state["fuse_pe"] = mm.n

            ffn_block(NDS, ds_groups,
                      mk_ldg(wsg_b, ds_groups, wgs, S_yA),
                      mk_ldg(wsu_b, ds_groups, wus, S_yB),
                      mk_ldd(wsd_b, NDS, S_yC), False, 0,
                      stA=S_yA, stB=S_yB, stC=S_yC)

            for e in range(E):
                wege = weg_b[e * 128:(e + 1) * 128, :]
                weue = weu_b[e * 128:(e + 1) * 128, :]
                wede = wed_b[e * 128:(e + 1) * 128, :]
                ffn_block(NDE, de_groups,
                          mk_ldg(wege, de_groups, wgs, S_yA),
                          mk_ldg(weue, de_groups, wus, S_yB),
                          mk_ldd(wede, NDE, S_yC), True, e,
                          stA=S_yA, stB=S_yB, stC=S_yC)
            S_kb.done(xn1_i, mm, mm.n)

            out_dv = dv.n
            kb.W("y", dv, out_dv)
            for fo in range(NF):
                dma("y", ys, yout[fo * 128:(fo + 1) * 128, q0:q0 + TOKP],
                    res2[:, fo * TOKP:(fo + 1) * TOKP])
            arena.pop()
            kb.barrier()

        kb.W("y", ys, ys.n)

        @block.tensor
        def _(eng):
            for f in kb.q["t"]:
                f(eng)

        @block.vector
        def _(eng):
            for f in kb.q["v"]:
                f(eng)

        @block.scalar
        def _(eng):
            for f in kb.q["a"]:
                f(eng)

        @block.gpsimd
        def _(eng):
            for f in kb.q["g"]:
                f(eng)

        @block.sync
        def _(eng):
            for f in kb.q["y"]:
                f(eng)

    return nc



def make_masks(cfg):
    TT, NM = cfg["TT"], cfg["NM"]
    k = np.arange(NM * 128)[:, None]
    qq = np.arange(TT)[None, :]
    kk = k % 128 + (k // 128) * 128
    return np.where(qq >= kk, 0.0, NEGM).astype(np.float32)


def prep_core_inputs(cfg, I, core):
    S, D, TT = cfg["S"], cfg["D"], cfg["TT"]
    NH, HD, TOK = cfg["NH_CORE"], cfg["HD"], cfg["TOK"]
    NIC, NF = cfg["NIC"], cfg["NF"]
    DE, DS = cfg["DE"], cfg["DS"]
    b, r = core // 2, core % 2
    C = np.ascontiguousarray

    def tfm(w, gw):
        K_, O_ = w.shape
        return C(w.reshape(K_ // 128, 128, O_ // gw, gw)
                 .transpose(2, 1, 0, 3).reshape((O_ // gw) * 128,
                                                (K_ // 128) * gw))

    def tgrp(w, groups):
        Kc = w.shape[0] // 128
        parts = [w.reshape(Kc, 128, w.shape[1])[:, :, og0:og0 + gw]
                 .transpose(1, 0, 2).reshape(128, Kc * gw)
                 for og0, gw in groups]
        return np.concatenate(parts, axis=1)

    GW0 = min(256, DE, DS)
    ds_groups = []
    _o = 0
    while _o < DS:
        _w = min(GW0, DS - _o)
        ds_groups.append((_o, _w))
        _o += _w
    de_groups = []
    _o = 0
    while _o < DE:
        _w = min(GW0, DE - _o)
        de_groups.append((_o, _w))
        _o += _w
    fo_groups = [(fo * 128, 128) for fo in range(NF)]

    d = {}
    x = np.asarray(I["L0_attn_x"][b])
    d["xT"] = C(x.T)
    d["xoT"] = C(x.T[:, r * TOK:(r + 1) * TOK])
    hsl = slice(r * HD, (r + 1) * HD)
    d["wq_t"] = tfm(C(np.asarray(I["L0_attn_Wq"]).T[:, hsl]), 128)
    d["wk_t"] = tfm(C(np.asarray(I["L0_attn_Wk"]).T[:, hsl]), HD)
    d["wv_t"] = tfm(C(np.asarray(I["L0_attn_Wv"]).T[:, hsl]), HD)
    d["wo_t"] = tfm(C(np.asarray(I["L0_attn_Wo"]).T), 128)
    d["fg8t"] = tfm(np.asarray(I["L0_ffn_w_gate"]), 128)
    d["fu8t"] = tfm(np.asarray(I["L0_ffn_w_up"]), 128)
    d["fd8t"] = tfm(np.asarray(I["L0_ffn_w_down"]), 128)
    d["sgT"] = C(np.asarray(I["L0_ffn_s_gate"]).reshape(NIC, 128).T)
    d["suT"] = C(np.asarray(I["L0_ffn_s_up"]).reshape(NIC, 128).T)
    d["sdT"] = C(np.asarray(I["L0_ffn_s_down"]).reshape(NF, 128).T)
    d["bqT"] = C(np.asarray(I["L1_attn_basis_q"]).T)
    d["bkT"] = C(np.asarray(I["L1_attn_basis_k"]).T)
    d["bvT"] = C(np.asarray(I["L1_attn_basis_v"]).T)
    d["boT"] = C(np.asarray(I["L1_attn_basis_o"]).T)
    d["phT"] = C(np.concatenate(
        [np.asarray(I[f"L1_attn_phase_{x_}"]).T for x_ in "qkvo"], 0))
    d["amT"] = C(np.concatenate(
        [np.asarray(I[f"L1_attn_amp_{x_}"]).T for x_ in "qkvo"], 0))
    d["wr"] = C(np.asarray(I["L1_ffn_W_router"]))
    weg = np.asarray(I["L1_ffn_We_gate"])
    weu = np.asarray(I["L1_ffn_We_up"])
    wed = np.asarray(I["L1_ffn_We_down"])
    d["weg_t"] = C(np.stack([tgrp(weg[e], de_groups) for e in range(8)])
                   .reshape(8 * 128, -1))
    d["weu_t"] = C(np.stack([tgrp(weu[e], de_groups) for e in range(8)])
                   .reshape(8 * 128, -1))
    d["wed_t"] = C(np.stack([tgrp(wed[e], fo_groups) for e in range(8)])
                   .reshape(8 * 128, -1))
    d["wsg_t"] = C(tgrp(np.asarray(I["L1_ffn_Ws_gate"]), ds_groups))
    d["wsu_t"] = C(tgrp(np.asarray(I["L1_ffn_Ws_up"]), ds_groups))
    d["wsd_t"] = C(tgrp(np.asarray(I["L1_ffn_Ws_down"]), fo_groups))
    d["idn"] = np.eye(128, dtype=np.float32)
    d["cmask"] = make_masks(cfg)
    d["rolep"] = np.array([[r * TOK, 0]], np.int32)
    return d


_NC_CACHE = {}


def kernel(**inputs):
    cfg = make_cfg(mini=False)
    if "full" not in _NC_CACHE:
        _NC_CACHE["full"] = build_nc(cfg)
    nc = _NC_CACHE["full"]
    in_maps = [prep_core_inputs(cfg, inputs, c) for c in range(8)]
    res = run_bass_kernel_spmd(nc, in_maps, core_ids=list(range(8)))
    S, D, TOK = cfg["S"], cfg["D"], cfg["TOK"]
    out = np.empty((4, S, D), np.float32)
    for c in range(8):
        b, r = c // 2, c % 2
        out[b, r * TOK:(r + 1) * TOK, :] = res.results[c]["yout"].T
    return out
